# version 05
# baseline (speedup 1.0000x reference)
import numpy as np
import ml_dtypes
import concourse.bass as bass
import concourse.mybir as mybir
from concourse.bass_utils import run_bass_kernel_spmd

F32 = mybir.dt.float32
BF16 = mybir.dt.bfloat16
AF = mybir.ActivationFunctionType
ALU = mybir.AluOpType
AX = mybir.AxisListType.X

D = 1024
DT = 8
DECAY = 1.0 - 1.0 / 3000.0
EPSF = 0.08 * DECAY

ENGS = ("tensor", "vector", "scalar", "gpsimd", "sync")


class Buf:
    __slots__ = ("name", "writer", "readers")

    def __init__(self, name):
        self.name = name
        self.writer = None
        self.readers = []


class Op:
    __slots__ = ("eng", "emit", "is_dma", "waits", "tag", "src")


class Prog:
    def __init__(self):
        self.ops = []
        self.cnt = {e: 0 for e in ENGS}
        self.dma_cnt = {e: 0 for e in ENGS}

    NQ = 16

    def op(self, eng, emit, reads=(), writes=(), dma=False):
        import sys
        o = Op()
        f = sys._getframe(1)
        while f and f.f_code.co_name in ("dma", "dmaJ", "mm", "mmg", "tr", "act",
                                         "cp", "tt", "ts", "tss", "stt", "red",
                                         "rcp", "mset", "sel", "vop", "op"):
            f = f.f_back
        o.src = "%s:%d" % (f.f_code.co_name, f.f_lineno) if f else "?"
        o.eng, o.emit, o.is_dma, o.waits = eng, emit, dma, {}
        if dma:
            n = self.dma_cnt[eng]
            self.dma_cnt[eng] += 1
            s, r = n % Prog.NQ, n // Prog.NQ
            o.tag = (("q", eng, s), 16 * (r + 1))
        else:
            self.cnt[eng] += 1
            o.tag = (("e", eng), self.cnt[eng])
        deps = []
        for b in reads:
            if b.writer is not None:
                deps.append(b.writer)
        for b in writes:
            if b.writer is not None:
                deps.append(b.writer)
            deps.extend(b.readers)
        for d in deps:
            if d is o:
                continue
            k, v = d.tag
            if o.waits.get(k, -1) < v:
                o.waits[k] = v
        for b in reads:
            b.readers.append(o)
        for b in writes:
            b.writer = o
            b.readers = []
        self.ops.append(o)
        return o

    def emit_all(self, sems, qsems, block, gps_prolog):
        per = {e: [] for e in ENGS}
        for o in self.ops:
            per[o.eng].append(o)
        dma_tot = dict(self.dma_cnt)

        def runner(eng_name):
            def body(eng):
                import contextlib
                with contextlib.ExitStack() as rs:
                    if eng_name == "gpsimd":
                        gps_prolog(eng, rs)
                    hi = {}
                    ndma = 0
                    for o in per[eng_name]:
                        for k in sorted(o.waits):
                            v = o.waits[k]
                            if hi.get(k, 0) >= v:
                                continue
                            sem = sems[k] if k[0] == "e" else qsems[k[1]][k[2]]
                            eng.wait_ge(sem, v)
                            hi[k] = v
                        if o.is_dma:
                            s, r = ndma % Prog.NQ, ndma // Prog.NQ
                            ndma += 1
                            gk = ("q", eng_name, s)
                            if r > 0 and hi.get(gk, 0) < 16 * r:
                                eng.wait_ge(qsems[eng_name][s], 16 * r)
                                hi[gk] = 16 * r
                        try:
                            last = o.emit(eng)
                        except Exception:
                            import sys
                            dfl = o.emit.__defaults__ or ()
                            print("EMIT FAIL", eng_name, "op#",
                                  per[eng_name].index(o), "tag", o.tag,
                                  "defaults", [getattr(d, "__code__", None)
                                               and d.__code__.co_firstlineno
                                               for d in dfl],
                                  file=sys.stderr)
                            raise
                        last.annotate(o.src)
                        if o.is_dma:
                            last.then_inc(qsems[eng_name][o.tag[0][2]], 16)
                        else:
                            last.then_inc(sems[("e", eng_name)], 1)
                    for s in range(Prog.NQ):
                        tot = (dma_tot[eng_name] + Prog.NQ - 1 - s) // Prog.NQ
                        if tot and hi.get(("q", eng_name, s), 0) < 16 * tot:
                            eng.wait_ge(qsems[eng_name][s], 16 * tot)
            return body

        block.tensor(runner("tensor"))
        block.vector(runner("vector"))
        block.scalar(runner("scalar"))
        block.gpsimd(runner("gpsimd"))
        block.sync(runner("sync"))


def sub_ap(ap2d, off_elems, dims):
    return bass.AP(ap2d.tensor, ap2d.offset + off_elems,
                   [list(ap2d.ap[0])] + [[s, c] for (s, c) in dims])


class Region:
    def __init__(self, arena, name, off, sz, buf):
        self.arena, self.name, self.off, self.sz, self.buf = arena, name, off, sz, buf

    def f(self, c0, cols, p0=0, p1=128):
        assert 0 <= c0 and c0 + cols <= self.sz, (self.name, c0, cols, self.sz)
        return self.arena.h.ap()[p0:p1, self.off + c0:self.off + c0 + cols]

    def b(self, c0, cols, p0=0, p1=128):
        assert 0 <= c0 and c0 + cols <= 2 * self.sz, (self.name, c0, cols)
        full = self.arena.h.ap()[p0:p1, self.off:self.off + self.sz]
        return full.bitcast(BF16)[:, c0:c0 + cols]

    def fap(self, dims, c0=0, p0=0, p1=128):
        return sub_ap(self.arena.h.ap()[p0:p1, self.off:self.off + self.sz],
                      c0, dims)

    def bap(self, dims, c0=0, p0=0, p1=128):
        base = self.arena.h.ap()[p0:p1, self.off:self.off + self.sz].bitcast(BF16)
        return sub_ap(base, c0, dims)


class Arena:
    def __init__(self, handle, n_f32):
        self.h, self.n = handle, n_f32
        self.free_list = [(0, n_f32)]
        self.live, self.retired = {}, []

    def alloc(self, name, f32_cols, high=False):
        sz = int(f32_cols)
        idxs = range(len(self.free_list) - 1, -1, -1) if high else \
            range(len(self.free_list))
        for i in idxs:
            off, s = self.free_list[i]
            if s >= sz:
                if s == sz:
                    self.free_list.pop(i)
                elif high:
                    off = off + s - sz
                    self.free_list[i] = (self.free_list[i][0], s - sz)
                else:
                    self.free_list[i] = (off + sz, s - sz)
                buf = Buf(name)
                keep = []
                for (ro, rs, rb) in self.retired:
                    if ro < off + sz and off < ro + rs:
                        if rb.writer is not None and rb.writer not in buf.readers:
                            buf.readers.append(rb.writer)
                        for r in rb.readers:
                            if r not in buf.readers:
                                buf.readers.append(r)
                        if ro < off:
                            keep.append((ro, off - ro, rb))
                        if ro + rs > off + sz:
                            keep.append((off + sz, ro + rs - off - sz, rb))
                    else:
                        keep.append((ro, rs, rb))
                self.retired = keep
                reg = Region(self, name, off, sz, buf)
                self.live[name] = reg
                return reg
        raise MemoryError(
            f"arena OOM: {name} needs {sz}, free={self.free_list}, "
            f"live={[(r.name, r.off, r.sz) for r in self.live.values()]}")

    def free(self, reg):
        del self.live[reg.name]
        self.retired.append((reg.off, reg.sz, reg.buf))
        self.free_list.append((reg.off, reg.sz))
        self.free_list.sort()
        out = []
        for o, s in self.free_list:
            if out and out[-1][0] + out[-1][1] == o:
                out[-1][1] += s
            else:
                out.append([o, s])
        self.free_list = [tuple(x) for x in out]


def build(S=4096, ncores=8, stages=("l0f1", "l0a", "l0f2", "l1f1", "l1a", "l1f2"),
          dbg=()):
    SH = S // 2
    SK = S // 128
    NBLK = S // 128
    NW = S // 64
    KSEL = int(S * 0.75)
    QC = min(512, S)
    dbg = set(dbg)
    stages = tuple(stages)

    nc = bass.Bass("TRN2", target_bir_lowering=False)
    P = Prog()
    REGS = {}

    def din(name, shape, dtype=F32):
        return nc.dram_tensor(name, shape, dtype, kind="ExternalInput")

    x_in = din("x_in", [128, DT * SH])
    pe_in = din("pe_in", [128, DT * SH])
    wq0 = din("wq0", [D, 512]); wk0 = din("wk0", [D, 512]); wv0 = din("wv0", [D, 512])
    wo0 = din("wo0", [D, D])
    bqkv0 = din("bqkv0", [128, 12])
    bo0 = din("bo0", [128, DT]); gam0 = din("gam0", [128, DT]); bet0 = din("bet0", [128, DT])
    wrt0 = din("wrt0", [128, DT]); wbl0 = din("wbl0", [D, D])
    w1t_h = din("w1t_h", [D, 2048]); w2t_h = din("w2t_h", [4096, 512])
    g1v = din("g1v", [128, DT]); b1lv = din("b1lv", [128, DT])
    g2v = din("g2v", [128, 32]); b2lv = din("b2lv", [128, 32])
    bias1 = din("bias1", [128, 32]); bias2 = din("bias2", [128, DT])
    wq1 = din("wq1", [D, 512]); wk1 = din("wk1", [D, 512]); wv1 = din("wv1", [D, 512])
    wb1 = din("wb1", [D, 2])
    cwq = din("cwq", [128, 16]); cwk = din("cwk", [128, 16]); cwv = din("cwv", [128, 16])
    fpad_l = din("fpad_l", [512, 191]); fpad_s = din("fpad_s", [512, 191])
    w1at = din("w1at", [D, D]); w1bt = din("w1bt", [16, D])
    gb1 = din("gb1", [128, DT]); w2gt = din("w2gt", [D, 4])
    gmisc = din("gmisc", [4, 5])
    onw2 = din("onw2", [1, 512])
    wot1 = din("wot1", [D, D])
    fconst = din("fconst", [128, 4])
    c_ones_bf = din("c_ones_bf", [128, 128], BF16)
    c_id_bf = din("c_id_bf", [128, 128], BF16)
    c_rev_bf = din("c_rev_bf", [128, 128], BF16)
    c_id_f = din("c_id_f", [128, 128])
    c_ones_f = din("c_ones_f", [128, 128])
    c_trils = din("c_trils", [128, 128])
    c_trili = din("c_trili", [128, 128])
    c_trius = din("c_trius", [128, 128])
    c_triui = din("c_triui", [128, 128])
    out_t = nc.dram_tensor("out", [128, DT * SH], F32, kind="ExternalOutput")

    def dsh(name, shape, dtype):
        return nc.dram_tensor(name, shape, dtype, kind="Internal",
                              addr_space="Shared")

    sh_xpe = dsh("sh_xpe", [128, DT * S], BF16)
    sh_av = dsh("sh_av", [128, DT * S], BF16)
    sh_xln1 = dsh("sh_xln1", [128, DT * S], BF16)
    sh_gsh = dsh("sh_gsh", [128, DT * S], BF16)
    sh_w1q = dsh("sh_w1q", [D, 4096], BF16)
    sh_w2q = dsh("sh_w2q", [4096, D], BF16)
    sh_qs = dsh("sh_qs", [2, 4], F32)
    sh_cg = dsh("sh_cg", [2, D], F32)
    sh_sc0 = dsh("sh_sc0", [2, SH], F32)
    sh_sc1 = dsh("sh_sc1", [2, SH], F32)
    bar_in = nc.dram_tensor("bar_in", [1, 4], F32, kind="Internal")
    bar_out = nc.dram_tensor("bar_out", [1, 4], F32, kind="Internal")
    av_aug = nc.dram_tensor("av_aug", [128, 8 * (S // 128) * 65], BF16,
                            kind="Internal")
    tp_l = nc.dram_tensor("tp_l", [512, 191], BF16, kind="Internal")
    tp_s = nc.dram_tensor("tp_s", [512, 191], BF16, kind="Internal")
    B_sh = {t: Buf("sh_" + t) for t in
            ("xpe", "av", "xln1", "gsh", "w1q", "w2q", "qs", "cg", "sc0", "sc1",
             "tpl", "tps", "bar")}
    ALL_SH = list(B_sh.values())

    B_avaug = Buf("av_aug")
    ARENA = 52224
    arena_h = nc.alloc_sbuf_tensor("arena", [128, ARENA], F32)
    A = Arena(arena_h, ARENA)
    R_xres = A.alloc("xres", DT * SH)
    R_wst = A.alloc("wst", 2048)
    R_scal = A.alloc("scal", 64)
    STAT = {"r": None, "n": 0}
    R_const = A.alloc("const", 960)
    R_vecs = A.alloc("vecs", 960)
    
    psAB = nc.alloc_psum_tensor("psAB", [128, 2048], F32)
    psA = psAB.ap()[:, 0:1024]
    psB = psAB.ap()[:, 1024:2048]
    psC = nc.alloc_psum_tensor("psC", [128, 512], F32)
    psD = nc.alloc_psum_tensor("psD", [128, 512], F32)
    psE = nc.alloc_psum_tensor("psE", [128, 512], F32)
    psF = nc.alloc_psum_tensor("psF", [128, 512], F32)
    bA0, bA1, bB0, bB1 = Buf("A0"), Buf("A1"), Buf("B0"), Buf("B1")
    bC, bD, bE, bF = Buf("C"), Buf("D"), Buf("E"), Buf("F")

    C_TRILS, C_TRILI, C_TRIUS, C_TRIUI, C_IDF, C_ONESF = 0, 128, 256, 384, 512, 640
    C_ONESBF, C_IDBF, C_REVBF = 768, 832, 0
    TRILS = R_const.f(C_TRILS, 128); TRILI = R_const.f(C_TRILI, 128)
    TRIUS = R_const.f(C_TRIUS, 128); TRIUI = R_const.f(C_TRIUI, 128)
    IDF = R_const.f(C_IDF, 128); ONESF = R_const.f(C_ONESF, 128)
    ONESBF = R_const.b(2 * C_ONESBF, 128); IDBF = R_const.b(2 * C_IDBF, 128)
    REVBF = R_const.b(2 * 896, 128)

    V_BQKV, V_BO, V_G0, V_B0, V_WRT = 0, 12, 20, 28, 36
    V_G1, V_B1L, V_BIAS2 = 44, 52, 60
    V_G2, V_B2L, V_BIAS1 = 68, 100, 132
    V_GB1, V_GMISC = 164, 172
    V_CWQ, V_CWK, V_CWV = 180, 196, 212
    V_ONW = 228
    V_SCAL = 740
    V_C, V_G, V_HCB = 770, 778, 786
    V_Z = 794
    V_FC = 800

    def VS(i, n=1, p0=0, p1=128):
        return R_scal.f(i, n, p0, p1)

    def dma(eng, out_ap, in_ap, reads=(), writes=(), allow_nc=False):
        return P.op(eng, lambda e, o=out_ap, i=in_ap, a=allow_nc:
                    e.dma_start(out=o, in_=i, allow_slow_non_contiguous=a),
                    reads=list(reads), writes=list(writes), dma=True)

    def dmaJ(out_fn, in_fn, reads=(), writes=()):
        return P.op("gpsimd",
                    lambda e, of=out_fn, inf=in_fn: e.dma_start(out=of(), in_=inf()),
                    reads=list(reads), writes=list(writes), dma=True)

    def mm(out_ap, lhsT, rhs, start, stop, reads, wbuf):
        return P.op("tensor", lambda e, o=out_ap, l=lhsT, r=rhs, s1=start, s2=stop:
                    e.matmul(out=o, lhsT=l, rhs=r, start=s1, stop=s2),
                    reads=list(reads), writes=[wbuf])

    def mmg(group, reads, wbuf):
        def em(e, g=tuple(group)):
            last = None
            for (o, l, r, s1, s2) in g:
                last = e.matmul(out=o, lhsT=l, rhs=r, start=s1, stop=s2)
            return last
        return P.op("tensor", em, reads=list(reads), writes=[wbuf])

    def tr(out_ap, in_ap, ident, reads, wbuf):
        return P.op("tensor", lambda e, o=out_ap, i=in_ap, d=ident:
                    e.transpose(out=o, in_=i, identity=d),
                    reads=list(reads), writes=[wbuf])

    def act(out_ap, in_ap, func, reads, writes, bias=0.0, scale=1.0, accum=None):
        def em(e, o=out_ap, i=in_ap, f=func, b=bias, s=scale, a=accum):
            return e.activation(out=o, in_=i, func=f, bias=b, scale=s, accum_out=a)
        r = list(reads)
        if (isinstance(bias, float) and bias != 0.0) or \
           (isinstance(scale, float) and scale != 1.0):
            r.append(R_vecs.buf)
        return P.op("scalar", em, reads=r, writes=list(writes))

    def cp(out_ap, in_ap, reads, writes, eng="vector"):
        return P.op(eng, lambda e, o=out_ap, i=in_ap: e.tensor_copy(out=o, in_=i),
                    reads=list(reads), writes=list(writes))

    def tt(out_ap, a, b, op, reads, writes, eng="vector"):
        return P.op(eng, lambda e, o=out_ap, x=a, y=b, p=op:
                    e.tensor_tensor(out=o, in0=x, in1=y, op=p),
                    reads=list(reads), writes=list(writes))

    def ts(out_ap, a, s1, s2, op0, op1, reads, writes, eng="vector"):
        if s2 is None:
            def em(e, o=out_ap, x=a, u=s1, p0=op0):
                return e.tensor_single_scalar(out=o, in_=x, scalar=u, op=p0)
        else:
            def em(e, o=out_ap, x=a, u=s1, v=s2, p0=op0, p1=op1):
                return e.tensor_scalar(out=o, in0=x, scalar1=u, scalar2=v,
                                       op0=p0, op1=p1)
        return P.op(eng, em, reads=list(reads), writes=list(writes))

    def tss(out_ap, a, s1, op, reads, writes, eng="vector"):
        def em(e, o=out_ap, x=a, u=s1, p=op):
            return e.tensor_single_scalar(out=o, in_=x, scalar=u, op=p)
        return P.op(eng, em, reads=list(reads), writes=list(writes))

    def stt(out_ap, in0, scalar, in1, op0, op1, reads, writes, eng="vector"):
        def em(e, o=out_ap, x=in0, s=scalar, y=in1, p0=op0, p1=op1):
            return e.scalar_tensor_tensor(out=o, in0=x, scalar=s, in1=y,
                                          op0=p0, op1=p1)
        return P.op(eng, em, reads=list(reads), writes=list(writes))

    def red(out_ap, in_ap, op, reads, writes, absval=False):
        def em(e, o=out_ap, i=in_ap, p=op, a=absval):
            return e.tensor_reduce(out=o, in_=i, axis=AX, op=p,
                                   apply_absolute_value=a)
        return P.op("vector", em, reads=list(reads), writes=list(writes))

    def rcp(out_ap, in_ap, reads, writes):
        return P.op("vector", lambda e, o=out_ap, i=in_ap:
                    e.reciprocal(out=o, in_=i),
                    reads=list(reads), writes=list(writes))

    def mset(ap, val, reads, writes, eng="vector"):
        return P.op(eng, lambda e, a=ap, v=val: e.memset(a, v),
                    reads=list(reads), writes=list(writes))

    def sel(out_ap, mask, t_ap, f_ap, reads, writes):
        return P.op("vector", lambda e, o=out_ap, m=mask, a=t_ap, b=f_ap:
                    e.select(out=o, mask=m, on_true=a, on_false=b,
                             add_drain=True),
                    reads=list(reads), writes=list(writes))

    def barrier():
        def em(e):
            return e.collective_compute(
                "AllReduce", ALU.add,
                replica_groups=[[2 * k, 2 * k + 1] for k in range(ncores // 2)],
                ins=[bar_in.ap()], outs=[bar_out.ap()])
        return P.op("gpsimd", em, reads=[], writes=ALL_SH)

    dbg_bufs = {}

    def dump(name, ap, rbufs):
        if name not in dbg:
            return
        import functools, operator
        fs = functools.reduce(operator.mul, ap.shape[1:], 1)
        t = nc.dram_tensor("dbg_" + name, [ap.shape[0], fs], ap.dtype,
                           kind="ExternalOutput")
        o = P.op("gpsimd", lambda e, t=t, a=ap: e.dma_start(out=t.ap(), in_=a),
                 reads=list(rbufs), writes=[], dma=True)
        o.src = "dump_" + name

    def gps_prolog(g, rs):
        r0 = rs.enter_context(g.register("rj"))
        pid = g.partition_id()
        g.reg_mod(r0, pid, 2)
        specs = [("j4", 4, 0), ("jSH", SH, 0), ("j2048", 2048, 0),
                 ("j512", 512, 0), ("j4S", 4 * S, 0), ("j4So2", 4 * S, 2 * S)]
        for qc in range(1, S // 1024):
            specs.append(("j4Sq%d" % qc, 4 * S, qc * 1024))
        for c0 in range(512, SH, 512):
            specs.append(("jSHp%d" % c0, SH, c0))
        for key, mult, add in specs:
            r = rs.enter_context(g.register("rg_" + key))
            g.reg_mul(r, r0, mult)
            if add:
                g.reg_add(r, r, add)
            REGS[key] = g.snap(r, min_val=0, max_val=mult + add)
        REGS["jSHp0"] = REGS["jSH"]
        REGS["j4Sq0"] = REGS["j4S"]

    dma("sync", R_const.f(C_TRILS, 128), c_trils.ap(), writes=[R_const.buf])
    dma("sync", R_const.f(C_TRILI, 128), c_trili.ap(), writes=[R_const.buf])
    dma("sync", R_const.f(C_TRIUS, 128), c_trius.ap(), writes=[R_const.buf])
    dma("sync", R_const.f(C_TRIUI, 128), c_triui.ap(), writes=[R_const.buf])
    dma("sync", R_const.f(C_IDF, 128), c_id_f.ap(), writes=[R_const.buf])
    dma("sync", R_const.f(C_ONESF, 128), c_ones_f.ap(), writes=[R_const.buf])
    dma("sync", R_const.b(2 * C_ONESBF, 128), c_ones_bf.ap(), writes=[R_const.buf])
    dma("sync", R_const.b(2 * C_IDBF, 128), c_id_bf.ap(), writes=[R_const.buf])
    dma("sync", R_const.b(2 * 896, 128), c_rev_bf.ap(), writes=[R_const.buf])
    for tname, col, w in ((bqkv0, V_BQKV, 12), (bo0, V_BO, 8), (gam0, V_G0, 8),
                          (bet0, V_B0, 8), (wrt0, V_WRT, 8), (g1v, V_G1, 8),
                          (b1lv, V_B1L, 8), (bias2, V_BIAS2, 8), (g2v, V_G2, 32),
                          (b2lv, V_B2L, 32), (bias1, V_BIAS1, 32), (gb1, V_GB1, 8),
                          (cwq, V_CWQ, 16), (cwk, V_CWK, 16), (cwv, V_CWV, 16)):
        dma("sync", R_vecs.f(col, w), tname.ap(), writes=[R_vecs.buf])
    dma("sync", R_vecs.f(V_GMISC, 5, 0, 4), gmisc.ap(), writes=[R_vecs.buf])
    dma("sync", R_vecs.f(V_ONW, 512), bass.AP(onw2, 0, [[0, 128], [1, 512]]),
        writes=[R_vecs.buf])
    dma("sync", R_vecs.f(V_FC, 4), fconst.ap(), writes=[R_vecs.buf])
    for fi, fv in enumerate((1e-5, 1e-6, 0.125, -1.0)):
        nc.const_aps.register(F32, fv, R_vecs.f(V_FC + fi, 1))
    dma("sync", R_xres.f(0, DT * SH), x_in.ap(), writes=[R_xres.buf])
    mset(R_vecs.f(V_Z, 4, 0, 1), 0.0, [], [R_vecs.buf])
    dma("gpsimd", bar_in.ap(), R_vecs.f(V_Z, 4, 0, 1), reads=[R_vecs.buf],
        writes=[B_sh["bar"]])

    def quantize():
        RQ = A.alloc("qstage", 2048 * 2)
        RT = A.alloc("qtmp", 4416)
        RO = A.alloc("qout", 8192)
        RS = A.alloc("qsum", 64)
        RCG = A.alloc("rcg", 512)
        for t in range(8):
            st_ap = RQ.f((t % 2) * 2048, 2048)
            dma("sync", st_ap, w1t_h.ap()[t * 128:(t + 1) * 128, :], writes=[RQ.buf])
            red(RS.f(t, 1), st_ap, ALU.add, [RQ.buf], [RS.buf], absval=True)
        for t in range(32):
            st_ap = RQ.f((t % 2) * 2048, 512)
            dma("sync", st_ap, w2t_h.ap()[t * 128:(t + 1) * 128, :], writes=[RQ.buf])
            red(RS.f(8 + t, 1), st_ap, ALU.add, [RQ.buf], [RS.buf], absval=True)
        red(RS.f(40, 1), RS.f(0, 8), ALU.add, [RS.buf], [RS.buf])
        red(RS.f(41, 1), RS.f(8, 32), ALU.add, [RS.buf], [RS.buf])
        mm(psC[0:1, 0:2], ONESF[:, 0:1], RS.f(40, 2), True, True,
           [R_const.buf, RS.buf], bC)
        cp(RS.f(42, 2, 0, 1), psC[0:1, 0:2], [bC], [RS.buf])
        dmaJ(lambda: bass.AP(sh_qs, REGS["j4"], [[4, 1], [1, 2]]),
             lambda: RS.f(42, 2, 0, 1), reads=[RS.buf], writes=[B_sh["qs"]])
        barrier()
        dma("sync", RS.f(44, 2, 0, 1), sh_qs.ap()[0:1, 0:2], reads=[B_sh["qs"]],
            writes=[RS.buf])
        dma("sync", RS.f(46, 2, 0, 1), sh_qs.ap()[1:2, 0:2], reads=[B_sh["qs"]],
            writes=[RS.buf])
        P.op("vector", lambda e: e.tensor_add(out=RS.f(48, 2, 0, 1),
                                              in0=RS.f(44, 2, 0, 1),
                                              in1=RS.f(46, 2, 0, 1)),
             reads=[RS.buf], writes=[RS.buf])
        ts(RS.f(48, 2, 0, 1), RS.f(48, 2, 0, 1), 1.0 / (1024 * 4096), 1e-5,
           ALU.mult, ALU.max, [RS.buf], [RS.buf])
        mm(psC[0:128, 0:2], ONESF[0:1, :], RS.f(48, 2, 0, 1), True, True,
           [R_const.buf, RS.buf], bC)
        cp(VS(0, 2), psC[:, 0:2], [bC], [R_scal.buf])
        rcp(VS(2, 2), VS(0, 2), [R_scal.buf], [R_scal.buf])
        ts(VS(14, 1), VS(1, 1), -1.0, None, ALU.mult, ALU.mult, [R_scal.buf],
           [R_scal.buf])
        for t in range(8):
            sl = t % 2
            st_ap = RQ.f(sl * 2048, 2048)
            dma("sync", st_ap, w1t_h.ap()[t * 128:(t + 1) * 128, :], writes=[RQ.buf])
            pa, na = RT.f(0, 2048), RT.f(2048, 2048)
            ts(pa, st_ap, VS(2, 1), 0.5, ALU.mult, ALU.is_ge,
               [RQ.buf, R_scal.buf], [RT.buf])
            ts(na, st_ap, VS(2, 1), -0.5, ALU.mult, ALU.is_le,
               [RQ.buf, R_scal.buf], [RT.buf])
            tt(RO.b(t * 2048, 2048), pa, na, ALU.subtract, [RT.buf], [RO.buf])
        dmaJ(lambda: bass.AP(sh_w1q, REGS["j2048"],
                             [[4096, 128], [128 * 4096, 8], [1, 2048]]),
             lambda: RO.b(0, 16384), reads=[RO.buf], writes=[B_sh["w1q"]])
        for t in range(32):
            sl = t % 2
            st_ap = RQ.f(sl * 2048, 512)
            dma("sync", st_ap, w2t_h.ap()[t * 128:(t + 1) * 128, :], writes=[RQ.buf])
            pa, na = RT.f(0, 512), RT.f(2048, 512)
            ts(pa, st_ap, VS(3, 1), 0.5, ALU.mult, ALU.is_ge,
               [RQ.buf, R_scal.buf], [RT.buf])
            ts(na, st_ap, VS(3, 1), -0.5, ALU.mult, ALU.is_le,
               [RQ.buf, R_scal.buf], [RT.buf])
            raw = RT.b(2 * 4096 + 64, 512)
            tt(raw, pa, na, ALU.subtract, [RT.buf], [RT.buf])
            cp(RT.b(2 * 4096, 1), R_vecs.f(V_B2L + t, 1), [R_vecs.buf], [RT.buf])
            cp(RT.b(2 * 4096 + 1, 1), R_vecs.f(V_G2 + t, 1), [R_vecs.buf],
               [RT.buf])
            mm(psD[0:1, 0:512], RT.b(2 * 4096, 1), raw, t == 0, t == 31,
               [RT.buf], bD)
            mm(psE[0:1, 0:512], RT.b(2 * 4096 + 1, 1), raw, t == 0, t == 31,
               [RT.buf], bE)
            ts(RO.b(t * 512, 512), raw, R_vecs.f(V_G2 + t, 1), None, ALU.mult,
               ALU.mult, [RT.buf, R_vecs.buf], [RO.buf])
        dmaJ(lambda: bass.AP(sh_w2q, REGS["j512"],
                             [[1024, 128], [128 * 1024, 32], [1, 512]]),
             lambda: RO.b(0, 16384), reads=[RO.buf], writes=[B_sh["w2q"]])
        cp(RCG.f(0, 512, 0, 1), psD[0:1, 0:512], [bD], [RCG.buf])
        cp(RCG.f(0, 512, 32, 33), psE[0:1, 0:512], [bE], [RCG.buf])
        dmaJ(lambda: bass.AP(sh_cg, REGS["j512"], [[1024, 2], [1, 512]]),
             lambda: bass.AP(arena_h, RCG.off, [[32 * ARENA, 2], [1, 512]]),
             reads=[RCG.buf], writes=[B_sh["cg"]])
        for (fsrc, fdst, bname) in ((fpad_l, tp_l, "tpl"), (fpad_s, tp_s, "tps")):
            for t in range(4):
                st_ap = RQ.f((t % 2) * 2048, 191)
                dma("sync", st_ap, fsrc.ap()[t * 128:(t + 1) * 128, :],
                    writes=[RQ.buf])
                ob = RT.b((t % 2) * 4096, 191)
                cp(ob, st_ap, [RQ.buf], [RT.buf])
                dma("gpsimd", fdst.ap()[t * 128:(t + 1) * 128, :], ob,
                    reads=[RT.buf], writes=[B_sh[bname]])
        barrier()
        dma("sync", R_vecs.f(V_C, 8),
            bass.AP(sh_cg, 0, [[1, 128], [128, 8]]),
            reads=[B_sh["cg"]], writes=[R_vecs.buf], allow_nc=True)
        dma("sync", R_vecs.f(V_G, 8),
            bass.AP(sh_cg, D, [[1, 128], [128, 8]]),
            reads=[B_sh["cg"]], writes=[R_vecs.buf], allow_nc=True)
        ts(R_vecs.f(V_HCB, 8), R_vecs.f(V_C, 8), VS(1, 1), None, ALU.mult,
           ALU.mult, [R_scal.buf, R_vecs.buf], [R_scal.buf, R_vecs.buf])
        tt(R_vecs.f(V_HCB, 8), R_vecs.f(V_HCB, 8), R_vecs.f(V_BIAS2, 8), ALU.add,
           [R_vecs.buf], [R_vecs.buf])
        ts(R_vecs.f(V_HCB, 8), R_vecs.f(V_HCB, 8), 0.5, None, ALU.mult, ALU.mult,
           [R_vecs.buf], [R_vecs.buf])
        A.free(RQ); A.free(RT); A.free(RO); A.free(RS); A.free(RCG)

    def free_stat():
        if STAT["r"] is not None:
            A.free(STAT["r"])
            STAT["r"] = None

    def ln_stats(eps=1e-5):
        assert STAT["r"] is None
        STAT["n"] += 1
        STAT["r"] = A.alloc("stat%d" % STAT["n"], 2 * SH)
        RB = A.alloc("lnb", 1024)
        for c0 in range(0, SH, 512):
            cw = min(512, SH - c0)
            for t in range(DT):
                src = R_xres.f(t * SH + c0, cw)
                xb = RB.b((t % 2) * 512, cw)
                cp(xb, src, [R_xres.buf], [RB.buf])
                mm(psA[:, 0:cw], ONESBF, xb, t == 0, t == DT - 1,
                   [R_const.buf, RB.buf], bA0)
                sq = RB.b(1024 + (t % 2) * 512, cw)
                act(sq, src, AF.Square, [R_xres.buf], [RB.buf])
                mm(psA[:, 512:512 + cw], ONESBF, sq, t == 0, t == DT - 1,
                   [R_const.buf, RB.buf], bA1)
            m = STAT["r"].f(c0, cw); iv = STAT["r"].f(SH + c0, cw)
            ts(m, psA[:, 0:cw], 1.0 / D, None, ALU.mult, ALU.mult, [bA0],
               [STAT["r"].buf])
            tt(iv, m, m, ALU.mult, [STAT["r"].buf], [STAT["r"].buf])
            stt(iv, psA[:, 512:512 + cw], 1.0 / D, iv, ALU.mult, ALU.subtract,
                [bA1, STAT["r"].buf], [STAT["r"].buf])
            act(iv, iv, AF.Sqrt, [STAT["r"].buf], [STAT["r"].buf], bias=eps)
            rcp(iv, iv, [STAT["r"].buf], [STAT["r"].buf])
        A.free(RB)

    def load_w_bf16(wsrc, ncolt, reg, kt=DT):
        for m in range(ncolt):
            st_ap = R_wst.f((m % 2) * 1024, kt * 128)
            dma("sync", st_ap, bass.AP(wsrc, m * 128,
                                       [[ncolt * 128, 128],
                                        [128 * ncolt * 128, kt], [1, 128]]),
                writes=[R_wst.buf])
            cp(reg.b(m * kt * 128, kt * 128), st_ap, [R_wst.buf], [reg.buf])

    def l0_ffn(inv):
        sh_sc = sh_sc0 if inv == 0 else sh_sc1
        b_sc = B_sh["sc0" if inv == 0 else "sc1"]
        ln_stats()
        XLC = A.alloc("xlc", DT * 512)
        XB = A.alloc("xb", DT * SH // 2)
        WBL = A.alloc("wbl", 4096)
        YB0 = A.alloc("y0", DT * 512)
        YB1 = A.alloc("y1", DT * 512)
        SELB = A.alloc("selb", 512)
        SCA = A.alloc("sca", 512)
        SCR = A.alloc("scr", SH)
        for c0 in range(0, SH, 512):
            cw = min(512, SH - c0)
            for t in range(DT):
                d = XLC.f(t * 512, cw)
                tt(d, R_xres.f(t * SH + c0, cw), STAT["r"].f(c0, cw), ALU.subtract,
                   [R_xres.buf, STAT["r"].buf], [XLC.buf])
                tt(d, d, STAT["r"].f(SH + c0, cw), ALU.mult, [XLC.buf, STAT["r"].buf],
                   [XLC.buf])
                cp(XB.b(t * SH + c0, cw), d, [XLC.buf], [XB.buf])
                if t == 0:
                    ts(SCA.f(0, cw), d, R_vecs.f(V_WRT + t, 1), None, ALU.mult,
                       ALU.mult, [XLC.buf, R_vecs.buf], [SCA.buf])
                else:
                    stt(SCA.f(0, cw), d, R_vecs.f(V_WRT + t, 1), SCA.f(0, cw),
                        ALU.mult, ALU.add, [XLC.buf, R_vecs.buf, SCA.buf],
                        [SCA.buf])
            mm(psC[0:1, 0:cw], ONESF[:, 0:1], SCA.f(0, cw), True, True,
               [R_const.buf, SCA.buf], bC)
            cp(SCR.f(c0, cw, 0, 1), psC[0:1, 0:cw], [bC], [SCR.buf])
        dmaJ(lambda: bass.AP(sh_sc, REGS["jSH"], [[SH, 1], [1, SH]]),
             lambda: SCR.f(0, SH, 0, 1), reads=[SCR.buf], writes=[b_sc])
        barrier()
        W = S // 128
        SRCH = R_wst
        dma("sync", R_wst.f(0, W), bass.AP(sh_sc, 0, [[W, 128], [1, W]]),
            reads=[b_sc], writes=[R_wst.buf])
        tt(R_wst.f(W, 8), R_vecs.f(V_WRT, 8), R_vecs.f(V_WRT, 8), ALU.mult,
           [R_vecs.buf], [R_wst.buf])
        red(R_wst.f(W + 8, 1), R_wst.f(W, 8), ALU.add, [R_wst.buf], [R_wst.buf])
        mm(psD[0:1, 0:1], ONESF[:, 0:1], R_wst.f(W + 8, 1), True, True,
           [R_const.buf, R_wst.buf], bD)
        act(VS(7, 1, 0, 1), psD[0:1, 0:1], AF.Sqrt, [bD], [R_scal.buf])
        ts(VS(7, 1, 0, 1), VS(7, 1, 0, 1), 33.0, None, ALU.mult, ALU.mult,
           [R_scal.buf], [R_scal.buf])
        ts(VS(8, 1, 0, 1), VS(7, 1, 0, 1), -1.0, None, ALU.mult, ALU.mult,
           [R_scal.buf], [R_scal.buf])
        cp(VS(9, 1, 0, 1), VS(7, 1, 0, 1), [R_scal.buf], [R_scal.buf])
        CMPB = None
        for it in range(34):
            tt(VS(10, 1, 0, 1), VS(8, 1, 0, 1), VS(9, 1, 0, 1), ALU.add,
               [R_scal.buf], [R_scal.buf])
            ts(VS(10, 1, 0, 1), VS(10, 1, 0, 1), 0.5, None, ALU.mult, ALU.mult,
               [R_scal.buf], [R_scal.buf])
            mm(psD[0:128, 0:1], ONESF[0:1, :], VS(10, 1, 0, 1), True, True,
               [R_const.buf, R_scal.buf], bD)
            cp(VS(13, 1), psD[:, 0:1], [bD], [R_scal.buf])
            tss(R_wst.f(128, W), R_wst.f(0, W), VS(13, 1), ALU.is_gt,
                [R_wst.buf, R_scal.buf], [R_wst.buf])
            red(R_wst.f(128 + W, 1), R_wst.f(128, W), ALU.add, [R_wst.buf], [R_wst.buf])
            mm(psD[0:1, 4:5], ONESF[:, 0:1], R_wst.f(128 + W, 1), True, True,
               [R_const.buf, R_wst.buf], bD)
            tss(VS(12, 1, 0, 1), psD[0:1, 4:5], float(KSEL), ALU.is_ge,
                [bD], [R_scal.buf])
            tt(VS(13, 1, 0, 1), VS(10, 1, 0, 1), VS(8, 1, 0, 1), ALU.subtract,
               [R_scal.buf], [R_scal.buf])
            stt(VS(8, 1, 0, 1), VS(13, 1, 0, 1), VS(12, 1, 0, 1), VS(8, 1, 0, 1),
                ALU.mult, ALU.add, [R_scal.buf], [R_scal.buf])
            tt(VS(13, 1, 0, 1), VS(9, 1, 0, 1), VS(10, 1, 0, 1), ALU.subtract,
               [R_scal.buf], [R_scal.buf])
            stt(VS(9, 1, 0, 1), VS(13, 1, 0, 1), VS(12, 1, 0, 1), VS(10, 1, 0, 1),
                ALU.mult, ALU.add, [R_scal.buf], [R_scal.buf])
        tss(SCR.b(0, SH, 32, 33), SCR.f(0, SH, 0, 1), VS(8, 1, 0, 1), ALU.is_gt,
            [SCR.buf, R_scal.buf], [SCR.buf])
        A.free(SCA)
        load_w_bf16(wbl0, 8, WBL)
        slots = ((psA, bA0, 0), (psA, bA1, 512), (psB, bB0, 0), (psB, bB1, 512))
        for ci, c0 in enumerate(range(0, SH, 512)):
            cw = min(512, SH - c0)
            Y = (YB0, YB1)[ci % 2]
            for m in range(DT):
                ps, pb, po = slots[m % 4]
                g = [(ps[:, po:po + cw], WBL.b((m * 8 + k) * 128, 128),
                      XB.b(k * SH + c0, cw), k == 0, k == DT - 1)
                     for k in range(DT)]
                mmg(g, [WBL.buf, XB.buf], pb)
                act(Y.f(m * 512, cw), ps[:, po:po + cw], AF.Gelu_apprx_tanh,
                    [pb], [Y.buf])
            mm(psE[:, 0:cw], ONESBF[32:33, :], SCR.b(c0, cw, 32, 33), True,
               True, [R_const.buf, SCR.buf], bE)
            for t in range(DT):
                d = XLC.f(t * 512, cw)
                tt(d, R_xres.f(t * SH + c0, cw), STAT["r"].f(c0, cw), ALU.subtract,
                   [R_xres.buf, STAT["r"].buf], [XLC.buf])
                tt(d, d, STAT["r"].f(SH + c0, cw), ALU.mult, [XLC.buf, STAT["r"].buf],
                   [XLC.buf])
                tt(SELB.f(0, cw), Y.f(t * 512, cw), d, ALU.subtract,
                   [Y.buf, XLC.buf], [SELB.buf])
                tt(SELB.f(0, cw), SELB.f(0, cw), psE[:, 0:cw], ALU.mult,
                   [SELB.buf, bE], [SELB.buf])
                tt(SELB.f(0, cw), SELB.f(0, cw), d, ALU.add,
                   [SELB.buf, XLC.buf], [SELB.buf])
                stt(R_xres.f(t * SH + c0, cw), SELB.f(0, cw), 0.5,
                    R_xres.f(t * SH + c0, cw), ALU.mult, ALU.add,
                    [SELB.buf, R_xres.buf], [R_xres.buf])
        A.free(XLC); A.free(XB); A.free(SCR)
        A.free(WBL); A.free(YB0); A.free(YB1); A.free(SELB)
        free_stat()

    def l0_attn():
        ln_stats()
        XPB = A.alloc("xpb", DT * SH // 2)
        XPE = A.alloc("xpe", DT * SH)
        PEST = A.alloc("pest", 1024 * 2)
        for t in range(DT):
            d = XPE.f(t * SH, SH)
            tt(d, R_xres.f(t * SH, SH), STAT["r"].f(0, SH), ALU.subtract,
               [R_xres.buf, STAT["r"].buf], [XPE.buf])
            tt(d, d, STAT["r"].f(SH, SH), ALU.mult, [XPE.buf, STAT["r"].buf], [XPE.buf])
            for ci, c0 in enumerate(range(0, SH, 1024)):
                cw = min(1024, SH - c0)
                pa = PEST.f((ci % 2) * 1024, cw)
                dma("sync", pa, pe_in.ap()[:, t * SH + c0:t * SH + c0 + cw],
                    writes=[PEST.buf])
                tt(XPE.f(t * SH + c0, cw), XPE.f(t * SH + c0, cw), pa, ALU.add,
                   [XPE.buf, PEST.buf], [XPE.buf])
            cp(XPB.b(t * SH, SH), d, [XPE.buf], [XPB.buf])
        dmaJ(lambda: bass.AP(sh_xpe, REGS["jSH"],
                             [[DT * S, 128], [S, DT], [1, SH]]),
             lambda: XPB.b(0, DT * SH), reads=[XPB.buf], writes=[B_sh["xpe"]])
        A.free(XPE); A.free(PEST); A.free(XPB)
        barrier()
        WQ = A.alloc("wqa", 2048); WK = A.alloc("wka", 2048); WV = A.alloc("wva", 2048)
        load_w_bf16(wq0, 4, WQ); load_w_bf16(wk0, 4, WK); load_w_bf16(wv0, 4, WV)
        QT = A.alloc("qt", 2 * S)
        KT = A.alloc("kt", 2 * S)
        VSTG = A.alloc("vstg", 512)
        PIECE = A.alloc("piece", 264 * 2)
        XST = A.alloc("xst", 2048)
        nkc = 512 // 128
        slots = ((psA, bA0, 0), (psA, bA1, 512), (psB, bB0, 0), (psB, bB1, 512))
        si = 0
        for ci, c0 in enumerate(range(0, S, 512)):
            st_ap = XST.b(0, 4096)
            dma("sync", st_ap,
                bass.AP(sh_xpe, c0, [[DT * S, 128], [S, 8], [1, 512]]),
                reads=[B_sh["xpe"]], writes=[XST.buf])
            for wi, (wreg, dreg, bc) in enumerate(((WQ, QT, 0), (WK, KT, 4),
                                                   (WV, None, 8))):
                for m in range(4):
                    ps, pb, po = slots[si % 4]; si += 1
                    g = [(ps[:, po:po + 512], wreg.b((m * 8 + k) * 128, 128),
                          XST.b(k * 512, 512), k == 0, k == DT - 1)
                         for k in range(DT)]
                    mmg(g, [wreg.buf, XST.buf], pb)
                    if dreg is not None:
                        act(dreg.b(m * S + c0, 512), ps[:, po:po + 512],
                            AF.Identity, [pb, R_vecs.buf], [dreg.buf],
                            bias=R_vecs.f(V_BQKV + bc + m, 1))
                    else:
                        act(VSTG.b(0, 512), ps[:, po:po + 512], AF.Identity,
                            [pb, R_vecs.buf], [VSTG.buf],
                            bias=R_vecs.f(V_BQKV + bc + m, 1))
                        pc = PIECE.b((ci % 2) * 528, 520)
                        mset(pc, 1.0, [], [PIECE.buf])
                        for hh in range(2):
                            for kkk in range(nkc):
                                pp, ppb = (psE, bE) if kkk % 2 == 0 else (psF, bF)
                                tr(pp[:, 0:32].bitcast(BF16),
                                   VSTG.b(kkk * 128, 128, hh * 64, hh * 64 + 64),
                                   IDBF[hh * 64:hh * 64 + 64, hh * 64:hh * 64 + 64],
                                   [VSTG.buf, R_const.buf], ppb)
                                cp(PIECE.b((ci % 2) * 528 + hh * 260 + kkk * 65,
                                           64), pp[:, 0:32].bitcast(BF16),
                                   [ppb], [PIECE.buf])
                        dma("sync",
                            bass.AP(av_aug, (2 * m) * (SK * 65) + ci * nkc * 65,
                                    [[8 * SK * 65, 128], [SK * 65, 2],
                                     [1, nkc * 65]]),
                            PIECE.b((ci % 2) * 528, 520), reads=[PIECE.buf],
                            writes=[B_avaug])
        dump("qt", QT.b(0, 4 * S), [QT.buf])
        dump("kt", KT.b(0, 4 * S), [KT.buf])
        A.free(WQ); A.free(WK); A.free(WV); A.free(XST); A.free(VSTG)
        A.free(PIECE)
        SQB = A.alloc("sqb", 1024)
        mset(VS(5, 2), 0.0, [], [R_scal.buf])
        for reg, vi in ((QT, 5), (KT, 6)):
            for c0 in range(0, S, 512):
                for pt in range(4):
                    act(SQB.b(pt * 512, 512), reg.b(pt * S + c0, 512), AF.Square,
                        [reg.buf], [SQB.buf])
                    mm(psE[:, 0:512], ONESBF, SQB.b(pt * 512, 512), pt == 0,
                       pt == 3, [R_const.buf, SQB.buf], bE)
                red(VS(13, 1), psE[:, 0:512], ALU.max, [bE], [R_scal.buf])
                tt(VS(vi, 1), VS(vi, 1), VS(13, 1), ALU.max, [R_scal.buf],
                   [R_scal.buf])
        tt(VS(4, 1), VS(5, 1), VS(6, 1), ALU.mult, [R_scal.buf], [R_scal.buf])
        act(VS(4, 1), VS(4, 1), AF.Sqrt, [R_scal.buf], [R_scal.buf])
        ts(VS(4, 1), VS(4, 1), -0.125, None, ALU.mult, ALU.mult, [R_scal.buf],
           [R_scal.buf])
        A.free(SQB)
        EB = A.alloc("eb", 2 * 1024, high=True)
        DEN = A.alloc("den", 2 * QC + 8, high=True)
        VA = A.alloc("va", 2 * SK * 65 + 8, high=True)
        AVN = A.alloc("avn", 2 * 1024, high=True)
        va_head = SK * 65
        GK = 4
        for qc in range(S // QC):
            for hg in range(2):
                dma("sync", VA.b(0, 4 * va_head),
                    av_aug.ap()[:, hg * 4 * va_head:(hg + 1) * 4 * va_head],
                    reads=[B_avaug], writes=[VA.buf])
                for h4 in range(4):
                    h = hg * 4 + h4
                    pt, r0 = h // 2, (h % 2) * 64
                    for g in range(SK // GK):
                        for gi in range(GK):
                            kk = g * GK + gi
                            mm(psAB.ap()[:, gi * 512:(gi + 1) * 512],
                               KT.b(pt * S + kk * 128, 128, r0, r0 + 64),
                               QT.b(pt * S + qc * QC, QC, r0, r0 + 64),
                               True, True, [KT.buf, QT.buf],
                               (bA0, bA1, bB0, bB1)[gi])
                        e_ap = EB.b((g % 2) * 2048, 2048)
                        act(e_ap, psAB.ap()[:, 0:2048], AF.Exp,
                            [bA0, bA1, bB0, bB1, R_scal.buf], [EB.buf],
                            bias=VS(4, 1), scale=0.125)
                        for gi in range(GK):
                            kk = g * GK + gi
                            mm(psC[0:65, 0:512],
                               VA.b(h4 * va_head + kk * 65, 65),
                               EB.b((g % 2) * 2048 + gi * 512, 512),
                               kk == 0, kk == SK - 1, [VA.buf, EB.buf], bC)
                    cp(DEN.f(0, 512, 0, 1), psC[64:65, 0:512], [bC], [DEN.buf])
                    rcp(DEN.f(0, QC, 0, 1), DEN.f(0, QC, 0, 1), [DEN.buf],
                        [DEN.buf])
                    mm(psE[0:64, 0:512], ONESF[0:1, 0:64], DEN.f(0, 512, 0, 1),
                       True, True, [R_const.buf, DEN.buf], bE)
                    cp(DEN.f(QC, 512, 0, 64), psE[0:64, 0:512], [bE], [DEN.buf])
                    tt(AVN.b(pt * 1024 + (qc % 2) * 512, 512, r0, r0 + 64),
                       psC[0:64, 0:512], DEN.f(QC, 512, 0, 64), ALU.mult,
                       [bC, DEN.buf], [AVN.buf])
            if qc % 2 == 1:
                dmaJ(lambda qc=qc: bass.AP(sh_av, REGS["j4Sq%d" % (qc // 2)],
                                           [[DT * S, 128], [S, 4], [1, 1024]]),
                     lambda: AVN.b(0, 4 * 1024), reads=[AVN.buf],
                     writes=[B_sh["av"]])
        A.free(EB); A.free(DEN); A.free(QT); A.free(KT); A.free(VA); A.free(AVN)
        barrier()
        WO = A.alloc("wo", 4096)
        load_w_bf16(wo0, 8, WO)
        AVST = A.alloc("avst", 2048 * 2)
        HATT = A.alloc("hatt", DT * 512)
        XPC = A.alloc("xpc", DT * 512)
        PEST = A.alloc("pest2", 512 * 2)
        SQB = A.alloc("sqb2", 512)
        slots = ((psA, bA0, 0), (psA, bA1, 512), (psB, bB0, 0), (psB, bB1, 512))
        si = 0
        for ci, c0 in enumerate(range(0, SH, 512)):
            st_ap = AVST.b((ci % 2) * 4096, 4096)
            dmaJ(lambda st_ap=st_ap: st_ap,
                 lambda c0=c0: bass.AP(sh_av, REGS["jSHp%d" % c0],
                                       [[DT * S, 128], [S, 8], [1, 512]]),
                 reads=[B_sh["av"]], writes=[AVST.buf])
            for t in range(DT):
                d = XPC.f(t * 512, 512)
                tt(d, R_xres.f(t * SH + c0, 512), STAT["r"].f(c0, 512),
                   ALU.subtract, [R_xres.buf, STAT["r"].buf], [XPC.buf])
                tt(d, d, STAT["r"].f(SH + c0, 512), ALU.mult,
                   [XPC.buf, STAT["r"].buf], [XPC.buf])
                pa = PEST.f((t % 2) * 512, 512)
                dma("sync", pa, pe_in.ap()[:, t * SH + c0:t * SH + c0 + 512],
                    writes=[PEST.buf])
                tt(d, d, pa, ALU.add, [XPC.buf, PEST.buf], [XPC.buf])
            for m in range(DT):
                ps, pb, po = slots[si % 4]; si += 1
                g = [(ps[:, po:po + 512], WO.b((m * 8 + k) * 128, 128),
                      AVST.b((ci % 2) * 4096 + k * 512, 512), k == 0,
                      k == DT - 1) for k in range(DT)]
                mmg(g, [WO.buf, AVST.buf], pb)
                act(HATT.f(m * 512, 512), ps[:, po:po + 512], AF.Identity,
                    [pb, R_vecs.buf], [HATT.buf], bias=R_vecs.f(V_BO + m, 1))
                tt(HATT.f(m * 512, 512), HATT.f(m * 512, 512),
                   XPC.f(m * 512, 512), ALU.add, [HATT.buf, XPC.buf],
                   [HATT.buf])
            if ci == 0:
                dump("avst0", AVST.b(0, 4096), [AVST.buf])
                dump("hatt0", HATT.f(0, DT * 512), [HATT.buf])
            for t in range(DT):
                xb = SQB.b(0, 512)
                cp(xb, HATT.f(t * 512, 512), [HATT.buf], [SQB.buf])
                mm(psC[:, 0:512], ONESBF, xb, t == 0, t == DT - 1,
                   [R_const.buf, SQB.buf], bC)
                sq = SQB.b(512, 512)
                act(sq, HATT.f(t * 512, 512), AF.Square, [HATT.buf], [SQB.buf])
                mm(psD[:, 0:512], ONESBF, sq, t == 0, t == DT - 1,
                   [R_const.buf, SQB.buf], bD)
            m2 = STAT["r"].f(c0, 512); i2 = STAT["r"].f(SH + c0, 512)
            ts(m2, psC[:, 0:512], 1.0 / D, None, ALU.mult, ALU.mult, [bC],
               [STAT["r"].buf])
            tt(i2, m2, m2, ALU.mult, [STAT["r"].buf], [STAT["r"].buf])
            stt(i2, psD[:, 0:512], 1.0 / D, i2, ALU.mult, ALU.subtract,
                [bD, STAT["r"].buf], [STAT["r"].buf])
            act(i2, i2, AF.Sqrt, [STAT["r"].buf], [STAT["r"].buf], bias=1e-5)
            rcp(i2, i2, [STAT["r"].buf], [STAT["r"].buf])
            for t in range(DT):
                tt(HATT.f(t * 512, 512), HATT.f(t * 512, 512), m2, ALU.subtract,
                   [HATT.buf, STAT["r"].buf], [HATT.buf])
                stt(HATT.f(t * 512, 512), HATT.f(t * 512, 512),
                    R_vecs.f(V_G0 + t, 1), i2, ALU.mult, ALU.mult,
                    [HATT.buf, R_vecs.buf, STAT["r"].buf], [HATT.buf])
                ts(HATT.f(t * 512, 512), HATT.f(t * 512, 512),
                   R_vecs.f(V_B0 + t, 1), None, ALU.add, ALU.add,
                   [HATT.buf, R_vecs.buf], [HATT.buf])
                tt(R_xres.f(t * SH + c0, 512), R_xres.f(t * SH + c0, 512),
                   HATT.f(t * 512, 512), ALU.add, [R_xres.buf, HATT.buf],
                   [R_xres.buf])
        A.free(WO); A.free(AVST); A.free(HATT); A.free(XPC); A.free(PEST)
        A.free(SQB)
        free_stat()

    def l1_ffn(inv):
        ln_stats()
        XLC = A.alloc("xlc1", DT * 512)
        XB = A.alloc("xb1", DT * SH // 2)
        for c0 in range(0, SH, 512):
            cw = min(512, SH - c0)
            for t in range(DT):
                d = XLC.f(t * 512, cw)
                tt(d, R_xres.f(t * SH + c0, cw), STAT["r"].f(c0, cw), ALU.subtract,
                   [R_xres.buf, STAT["r"].buf], [XLC.buf])
                tt(d, d, STAT["r"].f(SH + c0, cw), ALU.mult, [XLC.buf, STAT["r"].buf],
                   [XLC.buf])
                ts(XB.b(t * SH + c0, cw), d, R_vecs.f(V_G1 + t, 1),
                   R_vecs.f(V_B1L + t, 1), ALU.mult, ALU.add,
                   [XLC.buf, R_vecs.buf], [XB.buf])
        A.free(XLC)
        free_stat()
        W1S = A.alloc("w1s", 512 * 4, high=True)
        W2S = A.alloc("w2s", 2048 * 4, high=True)
        H2 = A.alloc("h2", 32 * 256)
        M2 = A.alloc("m2", 1024)
        TMPE = A.alloc("tmpe", 512)
        SQ2 = A.alloc("sq2", 256)
        slots = ((psA, bA0, 0), (psA, bA1, 512), (psB, bB0, 0), (psB, bB1, 512))
        for ci, c0 in enumerate(range(0, SH, 512)):
            cw = min(512, SH - c0)
            si = 0
            for m in range(32):
                st_ap = W1S.b((m % 4) * 1024, 1024)
                dma("sync", st_ap,
                    bass.AP(sh_w1q, m * 128,
                            [[4096, 128], [128 * 4096, 8], [1, 128]]),
                    reads=[B_sh["w1q"]], writes=[W1S.buf])
                ps, pb, po = slots[si % 4]; si += 1
                g = [(ps[:, po:po + cw], W1S.b((m % 4) * 1024 + k * 128, 128),
                      XB.b(k * SH + c0, cw), k == 0, k == DT - 1)
                     for k in range(DT)]
                mmg(g, [W1S.buf, XB.buf], pb)
                act(H2.b(m * 512, cw), ps[:, po:po + cw], AF.Gelu,
                    [pb, R_scal.buf, R_vecs.buf], [H2.buf], bias=R_vecs.f(V_BIAS1 + m, 1),
                    scale=VS(0, 1))
            for t in range(32):
                mm(psC[:, 0:cw], ONESBF, H2.b(t * 512, cw), t == 0, t == 31,
                   [R_const.buf, H2.buf], bC)
                sq = SQ2.b(0, cw)
                act(sq, H2.b(t * 512, cw), AF.Square, [H2.buf], [SQ2.buf])
                mm(psD[:, 0:cw], ONESBF, sq, t == 0, t == 31,
                   [R_const.buf, SQ2.buf], bD)
            m2 = M2.f(0, cw); i2 = M2.f(512, cw)
            ts(m2, psC[:, 0:cw], 1.0 / 4096, None, ALU.mult, ALU.mult, [bC],
               [M2.buf])
            tt(i2, m2, m2, ALU.mult, [M2.buf], [M2.buf])
            stt(i2, psD[:, 0:cw], 1.0 / 4096, i2, ALU.mult, ALU.subtract,
                [bD, M2.buf], [M2.buf])
            act(i2, i2, AF.Sqrt, [M2.buf], [M2.buf], bias=1e-5)
            rcp(i2, i2, [M2.buf], [M2.buf])
            si = 0
            for m in range(DT):
                st_ap = W2S.b((m % 4) * 4096, 4096)
                dma("sync", st_ap,
                    bass.AP(sh_w2q, m * 128,
                            [[1024, 128], [128 * 1024, 32], [1, 128]]),
                    reads=[B_sh["w2q"]], writes=[W2S.buf])
                ps, pb, po = slots[si % 4]; si += 1
                g = [(ps[:, po:po + cw], W2S.b((m % 4) * 4096 + k * 128, 128),
                      H2.b(k * 512, cw), k == 0, k == 31)
                     for k in range(32)]
                mmg(g, [W2S.buf, H2.buf], pb)
                stt(TMPE.f(0, cw), m2, R_vecs.f(V_G + m, 1), ps[:, po:po + cw],
                    ALU.mult, ALU.subtract, [M2.buf, R_vecs.buf, pb], [TMPE.buf])
                stt(TMPE.f(0, cw), TMPE.f(0, cw), VS(14, 1), i2, ALU.mult,
                    ALU.mult, [TMPE.buf, R_scal.buf, M2.buf], [TMPE.buf])
                stt(R_xres.f(m * SH + c0, cw), TMPE.f(0, cw), 0.5,
                    R_xres.f(m * SH + c0, cw), ALU.mult, ALU.add,
                    [TMPE.buf, R_xres.buf], [R_xres.buf])
        for t in range(DT):
            ts(R_xres.f(t * SH, SH), R_xres.f(t * SH, SH),
               R_vecs.f(V_HCB + t, 1), None, ALU.add, ALU.add,
               [R_xres.buf, R_vecs.buf], [R_xres.buf])
        A.free(XB); A.free(W1S); A.free(W2S); A.free(H2); A.free(M2)
        A.free(TMPE); A.free(SQ2)

    def l1_attn():
        ln_stats()
        XLC = A.alloc("xlc2", DT * 512)
        XB = A.alloc("xb2", DT * SH // 2)
        for c0 in range(0, SH, 512):
            cw = min(512, SH - c0)
            for t in range(DT):
                d = XLC.f(t * 512, cw)
                tt(d, R_xres.f(t * SH + c0, cw), STAT["r"].f(c0, cw), ALU.subtract,
                   [R_xres.buf, STAT["r"].buf], [XLC.buf])
                tt(d, d, STAT["r"].f(SH + c0, cw), ALU.mult, [XLC.buf, STAT["r"].buf],
                   [XLC.buf])
                cp(XB.b(t * SH + c0, cw), d, [XLC.buf], [XB.buf])
        dmaJ(lambda: bass.AP(sh_xln1, REGS["jSH"],
                             [[DT * S, 128], [S, DT], [1, SH]]),
             lambda: XB.b(0, DT * SH), reads=[XB.buf], writes=[B_sh["xln1"]])
        A.free(XLC)
        free_stat()
        WGA = A.alloc("wga", 4096)
        load_w_bf16(w1at, 8, WGA)
        GOB = A.alloc("gob", DT * SH // 2)
        slots = ((psA, bA0, 0), (psA, bA1, 512), (psB, bB0, 0), (psB, bB1, 512))
        si = 0
        for ci, c0 in enumerate(range(0, SH, 512)):
            cw = min(512, SH - c0)
            for m in range(DT):
                ps, pb, po = slots[si % 4]; si += 1
                g = [(ps[:, po:po + cw], WGA.b((m * 8 + k) * 128, 128),
                      XB.b(k * SH + c0, cw), k == 0, k == DT - 1)
                     for k in range(DT)]
                mmg(g, [WGA.buf, XB.buf], pb)
                cp(GOB.b(m * SH + c0, cw), ps[:, po:po + cw], [pb], [GOB.buf])
        dmaJ(lambda: bass.AP(sh_gsh, REGS["jSH"],
                             [[DT * S, 128], [S, DT], [1, SH]]),
             lambda: GOB.b(0, DT * SH), reads=[GOB.buf], writes=[B_sh["gsh"]])
        A.free(WGA); A.free(GOB); A.free(XB)
        barrier()
        for hl in range(2):
            _l1_head(hl)
        barrier()
        WOT = A.alloc("wot", 4096)
        load_w_bf16(wot1, 8, WOT)
        OST = A.alloc("ost", 2048 * 2)
        si = 0
        for ci, c0 in enumerate(range(0, SH, 512)):
            st_ap = OST.b((ci % 2) * 4096, 4096)
            dmaJ(lambda st_ap=st_ap: st_ap,
                 lambda c0=c0: bass.AP(sh_av, REGS["jSHp%d" % c0],
                                       [[DT * S, 128], [S, 8], [1, 512]]),
                 reads=[B_sh["av"]], writes=[OST.buf])
            for m in range(DT):
                ps, pb, po = slots[si % 4]; si += 1
                g = [(ps[:, po:po + 512], WOT.b((m * 8 + k) * 128, 128),
                      OST.b((ci % 2) * 4096 + k * 512, 512), k == 0, k == DT - 1)
                     for k in range(DT)]
                mmg(g, [WOT.buf, OST.buf], pb)
                tt(R_xres.f(m * SH + c0, 512), R_xres.f(m * SH + c0, 512),
                   ps[:, po:po + 512], ALU.add, [R_xres.buf, pb], [R_xres.buf])
        A.free(WOT); A.free(OST)

    def _l1_head(hl):
        HG = None
        WST = A.alloc("wsth", 2048)
        WQH = A.alloc("wqh", 1024, high=True); WKH = A.alloc("wkh", 1024, high=True)
        WVH = A.alloc("wvh", 1024, high=True); WBH = A.alloc("wbh", 8, high=True)
        for wsrc, wreg in ((wq1, WQH), (wk1, WKH), (wv1, WVH)):
            st_ap = WST.f(0, 2048)
            dma("sync", st_ap, bass.AP(wsrc, hl * 256,
                                       [[512, 128], [128 * 512, 8], [1, 256]]),
                writes=[WST.buf])
            cp(wreg.b(0, 2048), st_ap, [WST.buf], [wreg.buf])
        st_ap = WST.f(0, 8)
        dma("sync", st_ap, bass.AP(wb1, hl, [[2, 128], [256, 8], [1, 1]]),
            writes=[WST.buf], allow_nc=True)
        cp(WBH.b(0, 8), st_ap, [WST.buf], [WBH.buf])
        A.free(WST)
        PAD = 4
        QP = A.alloc("qp", S + 2 * PAD)
        KP = A.alloc("kp", S + 2 * PAD)
        VP = A.alloc("vp", S + 2 * PAD)
        BTOK = A.alloc("btok", NBLK + 8, high=True)
        BROW = A.alloc("brow", 520)
        XST = A.alloc("xsth", 2048 * 2, high=True)
        for reg in (QP, KP, VP):
            mset(reg.b(0, 2 * PAD), 0.0, [], [reg.buf])
            mset(reg.b(S + PAD + PAD, 2 * PAD), 0.0, [], [reg.buf])
        for ci, c0 in enumerate(range(0, S, 512)):
            st_ap = XST.b((ci % 2) * 4096, 4096)
            dma("sync", st_ap,
                bass.AP(sh_xln1, c0, [[DT * S, 128], [S, 8], [1, 512]]),
                reads=[B_sh["xln1"]], writes=[XST.buf])
            for wi, (wreg, dreg) in enumerate(((WQH, QP), (WKH, KP), (WVH, VP))):
                for m in range(2):
                    ps, pb = (psC, bC) if (wi * 2 + m) % 2 == 0 else (psD, bD)
                    g = [(ps[:, 0:512], wreg.b((k * 2 + m) * 128, 128),
                          XST.b((ci % 2) * 4096 + k * 512, 512), k == 0,
                          k == DT - 1) for k in range(DT)]
                    mmg(g, [wreg.buf, XST.buf], pb)
                    cp(dreg.b(m * (S + 2 * PAD) + PAD + c0, 512), ps[:, 0:512],
                       [pb], [dreg.buf])
            g = [(psE[0:1, 0:512], WBH.b(k, 1),
                  XST.b((ci % 2) * 4096 + k * 512, 512), k == 0, k == DT - 1)
                 for k in range(DT)]
            mmg(g, [WBH.buf, XST.buf], bE)
            act(BROW.f(0, 512, 0, 1), psE[0:1, 0:512], AF.Sigmoid, [bE],
                [BROW.buf])
            for bi in range(4):
                tr(psF[0:128, bi:bi + 1], BROW.f(bi * 128, 128, 0, 1),
                   IDF[0:1, 0:1], [BROW.buf, R_const.buf], bF)
            cp(BTOK.f(ci * 4, 4), psF[0:128, 0:4], [bF], [BTOK.buf])
        A.free(XST); A.free(BROW); A.free(WBH)
        A.free(WQH); A.free(WKH); A.free(WVH)
        DIAG = A.alloc("diag", 8 * 64)
        mset(DIAG.f(0, 8 * 64), 0.0, [], [DIAG.buf])
        cvs = []
        for (pre, pnm, wcol) in ((QP, "qcv", V_CWQ), (KP, "kcv", V_CWK),
                                 (VP, "vcv", V_CWV)):
            post = A.alloc(pnm, S, high=True)
            cvs.append(post)
            for ct in range(2):
                ctg = hl * 2 + ct
                for tap in range(4):
                    tss(DIAG.b((ct * 4 + tap) * 128, 128), IDF,
                        R_vecs.f(wcol + ctg * 4 + tap, 1), ALU.mult,
                        [R_const.buf, R_vecs.buf], [DIAG.buf])
            for ci, c0 in enumerate(range(0, S, 512)):
                for ct in range(2):
                    ps, pb = (psC, bC) if ct == 0 else (psD, bD)
                    g = [(ps[:, 0:512], DIAG.b((ct * 4 + tap) * 128, 128),
                          pre.b(ct * (S + 2 * PAD) + PAD + c0 - (3 - tap), 512),
                          tap == 0, tap == 3) for tap in range(4)]
                    mmg(g, [DIAG.buf, pre.buf], pb)
                    act(post.b(ct * S + c0, 512), ps[:, 0:512], AF.Silu,
                        [pb], [post.buf])
            A.free(pre)
        QCV, KCV, VCV = cvs
        A.free(DIAG)
        RN = A.alloc("rn", 512)
        SQB = A.alloc("sqbh", 512)
        for reg in (QCV, KCV):
            for c0 in range(0, S, 512):
                for ct in range(2):
                    act(SQB.b(ct * 512, 512), reg.b(ct * S + c0, 512), AF.Square,
                        [reg.buf], [SQB.buf])
                    mm(psE[:, 0:512], ONESBF, SQB.b(ct * 512, 512), ct == 0,
                       ct == 1, [R_const.buf, SQB.buf], bE)
                rnc = RN.f(0, 512)
                act(rnc, psE[:, 0:512], AF.Sqrt, [bE], [RN.buf], bias=1e-6)
                rcp(rnc, rnc, [RN.buf], [RN.buf])
                for ct in range(2):
                    tt(reg.b(ct * S + c0, 512), reg.b(ct * S + c0, 512), rnc,
                       ALU.mult, [reg.buf, RN.buf], [reg.buf])
        A.free(RN); A.free(SQB)
        SRC4 = A.alloc("src4", NBLK * 512)
        mset(SRC4.f(0, NBLK * 512), -777.0, [], [SRC4.buf])
        for blk in range(NBLK):
            for ct in range(2):
                ps, pb = (psE, bE) if ct == 0 else (psF, bF)
                tr(ps[:, 0:64].bitcast(BF16), VCV.b(ct * S + blk * 128, 128),
                   IDBF, [VCV.buf, R_const.buf], pb)
                cp(SRC4.b(blk * 1024 + 256 + ct * 128, 128),
                   ps[:, 0:64].bitcast(BF16), [pb], [SRC4.buf])
        dump("qcv%d" % hl, QCV.b(0, 2 * S), [QCV.buf])
        dump("kcv%d" % hl, KCV.b(0, 2 * S), [KCV.buf])
        dump("vcv%d" % hl, VCV.b(0, 2 * S), [VCV.buf])
        dump("btok%d" % hl, BTOK.f(0, NBLK), [BTOK.buf])
        A.free(VCV)
        _delta(QCV, KCV, BTOK, SRC4)
        A.free(QCV); A.free(KCV)
        _fir(hl, SRC4)
        for sl, snm in ((0, "od"), (256, "vd"), (512, "fs"), (768, "fl")):
            dump("%s%d" % (snm, hl), SRC4.bap([(1024, NBLK), (1, 256)], sl),
                 [SRC4.buf])
        STATSF = A.alloc("statsf", S // 2 + 8)
        _stats(SRC4, STATSF)
        dump("stf%d" % hl, STATSF.b(0, S, 0, 16), [STATSF.buf])
        _gate(hl, SRC4, STATSF)
        A.free(STATSF)
        dump("mix%d" % hl, SRC4.b(0, NBLK * 1024), [SRC4.buf])
        OT = A.alloc("ot", S)
        for blk in range(NBLK):
            for ct in range(2):
                ps, pb = (psE, bE) if ct == 0 else (psF, bF)
                tr(ps[:, 0:64].bitcast(BF16), SRC4.b(blk * 1024 + ct * 128, 128),
                   IDBF, [SRC4.buf, R_const.buf], pb)
                cp(OT.b(ct * S + blk * 128, 128), ps[:, 0:64].bitcast(BF16),
                   [pb], [OT.buf])
        dmaJ(lambda hl=hl: bass.AP(sh_av,
                                   REGS["j4S" if hl == 0 else "j4So2"],
                                   [[DT * S, 128], [S, 2], [1, S]]),
             lambda: OT.b(0, 2 * S), reads=[OT.buf], writes=[B_sh["av"]])
        A.free(OT); A.free(SRC4); A.free(BTOK)

    def _delta(QN, KN, BTOK, SRC4):
        NCH = S // 128
        POW = A.alloc("pow", 7 * 64)
        APW = A.alloc("apow", 7 * 64)
        TA = A.alloc("ta", 256)
        TB = A.alloc("tb", 128)
        KTOK = A.alloc("ktok", 128)
        WT = A.alloc("wtd", 128)
        SSB = A.alloc("ssb", 256)
        SSF = A.alloc("ssf", 512)
        UB = A.alloc("ub", 128)
        ATT = A.alloc("attd", 64)
        BR = A.alloc("brd", 132)
        for i in range(NCH):
            first, last = (i == 0), (i == NCH - 1)
            tr(psF[0:1, 0:128], BTOK.f(i, 1), IDF[0:128, 0:128], [BTOK.buf,
                                                                  R_const.buf], bF)
            cp(BR.f(0, 128, 0, 1), psF[0:1, 0:128], [bF], [BR.buf])
            mm(psF[:, 128:256], ONESF[0:1, :], BR.f(0, 128, 0, 1), True, True,
               [R_const.buf, BR.buf], bF)
            g = [(psC[:, 0:128], KN.b(ct * S + i * 128, 128),
                  KN.b(ct * S + i * 128, 128), ct == 0, ct == 1)
                 for ct in range(2)]
            mmg(g, [KN.buf], bC)
            a0 = APW.b(0, 128)
            stt(a0, psC[:, 0:128], BTOK.f(i, 1), TRILS, ALU.mult, ALU.mult,
                [bC, BTOK.buf, R_const.buf], [APW.buf])
            ts(a0, a0, -1.0, None, ALU.mult, ALU.mult, [APW.buf], [APW.buf])
            BM = TB.f(0, 128)
            tt(BM, TRIUS, psF[:, 128:256], ALU.mult, [R_const.buf, bF], [TB.buf])
            n0 = POW.b(0, 128)
            stt(n0, psC[:, 0:128], -1.0, BM, ALU.mult, ALU.mult, [bC, TB.buf],
                [POW.buf])
            for jj in range(1, 7):
                mm(psD[:, 0:128], APW.b((jj - 1) * 128, 128),
                   POW.b((jj - 1) * 128, 128), True, True, [APW.buf, POW.buf], bD)
                cp(POW.b(jj * 128, 128), psD[:, 0:128], [bD], [POW.buf])
                mm(psD[:, 128:256], POW.b((jj - 1) * 128, 128),
                   APW.b((jj - 1) * 128, 128), True, True, [APW.buf, POW.buf], bD)
                cp(APW.b(jj * 128, 128), psD[:, 128:256], [bD], [APW.buf])
            for ct in range(2):
                ps, pb = (psE, bE) if ct == 0 else (psF, bF)
                tr(ps[:, 0:64].bitcast(BF16), KN.b(ct * S + i * 128, 128),
                   IDBF, [KN.buf, R_const.buf], pb)
                cp(KTOK.b(ct * 128, 128), ps[:, 0:64].bitcast(BF16), [pb],
                   [KTOK.buf])
            for ct in range(2):
                ts(TA.b(256 + ct * 128, 128), KTOK.b(ct * 128, 128),
                   BTOK.f(i, 1), None, ALU.mult, ALU.mult,
                   [KTOK.buf, BTOK.buf], [TA.buf])
                ts(TA.b(ct * 128, 128), SRC4.b(i * 1024 + 256 + ct * 128, 128),
                   BTOK.f(i, 1), None, ALU.mult, ALU.mult,
                   [SRC4.buf, BTOK.buf], [TA.buf])
            for jj in range(6, -1, -1):
                mm(psB[:, 0:512], POW.b(jj * 128, 128), TA.b(0, 512), True, True,
                   [POW.buf, TA.buf], bB0)
                tt(TA.b(0, 512), TA.b(0, 512), psB[:, 0:512], ALU.add,
                   [TA.buf, bB0], [TA.buf])
            for ct in range(2):
                ps, pb = (psE, bE) if ct == 0 else (psF, bF)
                tr(ps[:, 0:64].bitcast(BF16), TA.b(256 + ct * 128, 128),
                   IDBF, [TA.buf, R_const.buf], pb)
                cp(WT.b(ct * 128, 128), ps[:, 0:64].bitcast(BF16), [pb],
                   [WT.buf])
            if first:
                cp(UB.b(0, 256), TA.b(0, 256), [TA.buf], [UB.buf])
            else:
                g = [(psB[:, 512:768], WT.b(ct * 128, 128), SSB.b(ct * 256, 256),
                      ct == 0, ct == 1) for ct in range(2)]
                mmg(g, [WT.buf, SSB.buf], bB1)
                tt(UB.b(0, 256), TA.b(0, 256), psB[:, 512:768], ALU.subtract,
                   [TA.buf, bB1], [UB.buf])
            g = [(psC[:, 128:256], KN.b(ct * S + i * 128, 128),
                  QN.b(ct * S + i * 128, 128), ct == 0, ct == 1)
                 for ct in range(2)]
            mmg(g, [KN.buf, QN.buf], bC)
            tt(ATT.b(0, 128), TRIUI, psC[:, 128:256], ALU.mult,
               [R_const.buf, bC], [ATT.buf])
            if not first:
                g = [(psD[:, 0:256], QN.b(ct * S + i * 128, 128),
                      SSB.b(ct * 256, 256), ct == 0, False) for ct in range(2)]
                g.append((psD[:, 0:256], ATT.b(0, 128), UB.b(0, 256), False, True))
                mmg(g, [QN.buf, SSB.buf, ATT.buf, UB.buf], bD)
            else:
                mmg([(psD[:, 0:256], ATT.b(0, 128), UB.b(0, 256), True, True)],
                    [ATT.buf, UB.buf], bD)
            cp(SRC4.b(i * 1024, 256), psD[:, 0:256], [bD], [SRC4.buf])
            if not last:
                mm(psA[:, 0:256], KTOK.b(0, 128), UB.b(0, 256), True, True,
                   [KTOK.buf, UB.buf], bA0)
                mm(psA[:, 512:768], KTOK.b(128, 128), UB.b(0, 256), True, True,
                   [KTOK.buf, UB.buf], bA1)
                if first:
                    cp(SSF.f(0, 256), psA[:, 0:256], [bA0], [SSF.buf])
                    cp(SSF.f(256, 256), psA[:, 512:768], [bA1], [SSF.buf])
                else:
                    tt(SSF.f(0, 256), SSF.f(0, 256), psA[:, 0:256], ALU.add,
                       [SSF.buf, bA0], [SSF.buf])
                    tt(SSF.f(256, 256), SSF.f(256, 256), psA[:, 512:768],
                       ALU.add, [SSF.buf, bA1], [SSF.buf])
                cp(SSB.b(0, 256), SSF.f(0, 256), [SSF.buf], [SSB.buf])
                cp(SSB.b(256, 256), SSF.f(256, 256), [SSF.buf], [SSB.buf])
        for r in (POW, APW, TA, TB, KTOK, WT, SSB, SSF, UB, ATT, BR):
            A.free(r)

    def _fir(hl, SRC4):
        TOEP = A.alloc("toep", 64 * 64)
        RHW = A.alloc("rhw", 32 * NW + 8)
        RHW2 = A.alloc("rhw2", 32 * NW + 8)
        for g4 in range(4):
            ch0 = hl * 256 + g4 * 64
            dma("sync", TOEP.bap([(128, 64), (1, 64)], 0),
                bass.AP(tp_l, ch0 * 191, [[1, 128], [191, 64], [1, 64]]),
                reads=[B_sh["tpl"]], writes=[TOEP.buf])
            dma("sync", TOEP.bap([(128, 64), (1, 64)], 64),
                bass.AP(tp_s, ch0 * 191, [[1, 128], [191, 64], [1, 64]]),
                reads=[B_sh["tps"]], writes=[TOEP.buf])
            voff = 256 + g4 * 64
            P.op("vector", lambda e, voff=voff: e.tensor_copy(
                out=RHW.bap([(NW, 64), (2, NW // 2)], 1),
                in_=SRC4.bap([(1, 64), (1024, NW // 2)], voff)),
                reads=[SRC4.buf], writes=[RHW.buf])
            P.op("vector", lambda e, voff=voff: e.tensor_copy(
                out=RHW.bap([(NW, 64), (2, NW // 2)], 0, 64, 128),
                in_=SRC4.bap([(1, 64), (1024, NW // 2)], voff, 0, 64)),
                reads=[SRC4.buf], writes=[RHW.buf])
            P.op("vector", lambda e, voff=voff: e.tensor_copy(
                out=RHW.bap([(NW, 64), (2, NW // 2 - 1)], 2, 0, 64),
                in_=SRC4.bap([(1, 64), (1024, NW // 2 - 1)], voff, 64, 128)),
                reads=[SRC4.buf], writes=[RHW.buf])
            mset(RHW.bap([(NW, 64)], 0, 0, 64), 0.0, [], [RHW.buf])
            for b8 in range(64 * NW // 512):
                pr, pbr = (psE, bE) if b8 % 2 == 0 else (psF, bF)
                mm(pr[:, 0:512], REVBF, RHW.b(b8 * 512, 512), True, True,
                   [R_const.buf, RHW.buf], pbr)
                cp(RHW2.b(b8 * 512, 512), pr[:, 0:512], [pbr], [RHW2.buf])
            if g4 == 0:
                dump("rhw%d" % hl, RHW.b(0, 64 * NW), [RHW.buf])
                dump("toep%d" % hl, TOEP.b(0, 64 * 128), [TOEP.buf])
            for cg in range(8):
                ps, pb = (psC, bC) if cg % 2 == 0 else (psD, bD)
                grp = []
                for c8 in range(8):
                    c = cg * 8 + c8
                    grp.append((ps[:, c8 * NW:c8 * NW + NW],
                                TOEP.b(c * 128, 128),
                                RHW2.b(c * NW, NW), True, True))
                mmg(grp, [TOEP.buf, RHW2.buf], pb)
                for (rows0, slot) in ((0, 768), (64, 512)):
                    for par, po in ((0, 0), (64, 1)):
                        eng = "vector" if par == 0 else "vector"
                        P.op(eng, lambda e, ps=ps, rows0=rows0, slot=slot,
                             par=par, po=po, cg=cg, g4=g4: e.tensor_copy(
                                 out=SRC4.bap([(1, 8), (1024, NW // 2)],
                                              slot + g4 * 64 + cg * 8, par,
                                              par + 64),
                                 in_=sub_ap(ps[rows0:rows0 + 64, 0:512], po,
                                            [(NW, 8), (2, NW // 2)])),
                             reads=[pb], writes=[SRC4.buf])
        A.free(TOEP); A.free(RHW); A.free(RHW2)

    def _stats(SRC4, STATSF):
        RAW = A.alloc("raw", 3 * 4 * NBLK + 16)
        SQS = A.alloc("sqs", 2048)
        for kind in range(2):
            red(RAW.f(kind * 4 * NBLK, 4 * NBLK),
                SRC4.bap([(256, 4 * NBLK), (1, 256)], 0), ALU.add,
                [SRC4.buf], [RAW.buf], absval=(kind == 1))
        for g8 in range(NBLK // 4):
            act(SQS.b(0, 4096), SRC4.b(g8 * 4096, 4096), AF.Square, [SRC4.buf],
                [SQS.buf])
            red(RAW.f(8 * NBLK + g8 * 16, 16), SQS.bap([(256, 16), (1, 256)], 0),
                ALU.add, [SQS.buf], [RAW.buf])
        A.free(SQS)
        DS = A.alloc("ds", NBLK * 16)
        perm = {0: 8, 1: 12, 2: 0, 3: 4}
        for sidx in range(4):
            rb = perm[sidx]
            sm = RAW.fap([(4, NBLK)], sidx)
            ab = RAW.fap([(4, NBLK)], 4 * NBLK + sidx)
            qq = RAW.fap([(4, NBLK)], 8 * NBLK + sidx)
            mean = DS.fap([(16, NBLK)], rb + 0)
            var = DS.fap([(16, NBLK)], rb + 1)
            am = DS.fap([(16, NBLK)], rb + 2)
            l2 = DS.fap([(16, NBLK)], rb + 3)
            ts(mean, sm, 1.0 / 256, None, ALU.mult, ALU.mult, [RAW.buf], [DS.buf])
            ts(am, ab, 1.0 / 256, None, ALU.mult, ALU.mult, [RAW.buf], [DS.buf])
            P.op("scalar", lambda e, o=l2, i=qq: e.activation(
                out=o, in_=i, func=AF.Sqrt), reads=[RAW.buf], writes=[DS.buf])
            tt(var, mean, mean, ALU.mult, [DS.buf], [DS.buf])
            stt(var, qq, 1.0 / 256, var, ALU.mult, ALU.subtract,
                [RAW.buf, DS.buf], [DS.buf])
        for blk in range(NBLK):
            ps, pb = (psE, bE) if blk % 2 == 0 else (psF, bF)
            tr(ps[0:16, 0:128], DS.f(blk * 16, 16), IDF, [DS.buf, R_const.buf],
               pb)
            cp(STATSF.b(blk * 128, 128, 0, 16), ps[0:16, 0:128], [pb],
               [STATSF.buf])
        A.free(RAW); A.free(DS)

    def _gate(hl, SRC4, STATSF):
        W1B = A.alloc("w1b", 520)
        W2G = A.alloc("w2g", 20)
        WST = A.alloc("wstg2", 1024)
        dma("sync", WST.f(0, 1024, 0, 16),
            bass.AP(w1bt, 0, [[1024, 16], [1, 1024]]), writes=[WST.buf])
        cp(W1B.b(0, 1024, 0, 16), WST.f(0, 1024, 0, 16), [WST.buf], [W1B.buf])
        dma("sync", WST.f(0, 32), bass.AP(w2gt, 0, [[4, 128], [512, 8], [1, 4]]),
            writes=[WST.buf])
        cp(W2G.b(0, 32), WST.f(0, 32), [WST.buf], [W2G.buf])
        A.free(WST)
        act(VS(16, 1, 0, 4), R_vecs.f(V_GMISC + 3 + hl, 1, 0, 4), AF.Exp,
            [R_scal.buf, R_vecs.buf], [R_scal.buf, R_vecs.buf], scale=-1.0)
        ts(VS(17, 1, 0, 4), R_vecs.f(V_GMISC + 1 + hl, 1, 0, 4), DECAY, None,
           ALU.mult, ALU.mult, [R_scal.buf, R_vecs.buf], [R_scal.buf, R_vecs.buf])
        tt(VS(17, 1, 0, 4), VS(17, 1, 0, 4), R_vecs.f(V_GMISC, 1, 0, 4),
           ALU.add, [R_scal.buf, R_vecs.buf], [R_scal.buf, R_vecs.buf])
        tt(VS(18, 1, 0, 4), VS(17, 1, 0, 4), VS(16, 1, 0, 4), ALU.mult,
           [R_scal.buf], [R_scal.buf])
        GST = A.alloc("gst", 2048 * 2)
        H1 = A.alloc("h1", 2048)
        LG = A.alloc("lg", 1040)
        WGT = A.alloc("wgt", NBLK * 4 + 8)
        slots = ((psA, bA0, 0), (psA, bA1, 512), (psB, bB0, 0), (psB, bB1, 512))
        si = 0
        for ci, c0 in enumerate(range(0, S, 512)):
            st_ap = GST.b((ci % 2) * 4096, 4096)
            dma("sync", st_ap,
                bass.AP(sh_gsh, c0, [[DT * S, 128], [S, 8], [1, 512]]),
                reads=[B_sh["gsh"]], writes=[GST.buf])
            for m in range(DT):
                ps, pb, po = slots[si % 4]; si += 1
                g = [(ps[:, po:po + 512], W1B.b(m * 128, 128, 0, 16),
                      STATSF.b(c0, 512, 0, 16), True, False),
                     (ps[:, po:po + 512], IDBF,
                      GST.b((ci % 2) * 4096 + m * 512, 512), False, True)]
                mmg(g, [W1B.buf, STATSF.buf, GST.buf, R_const.buf], pb)
                act(H1.b(m * 512, 512), ps[:, po:po + 512], AF.Gelu_apprx_tanh,
                    [pb, R_vecs.buf], [H1.buf], bias=R_vecs.f(V_GB1 + m, 1))
            g = [(psE[0:4, 0:512], W2G.b(k * 4, 4), H1.b(k * 512, 512),
                  k == 0, k == DT - 1) for k in range(DT)]
            mmg(g, [W2G.buf, H1.buf], bE)
            act(LG.f(0, 512, 0, 4), psE[0:4, 0:512], AF.Identity,
                [bE, R_scal.buf], [LG.buf], bias=VS(18, 1, 0, 4),
                scale=VS(16, 1, 0, 4))
            dump("lgz%d_%d" % (hl, ci), LG.f(0, 512, 0, 4), [LG.buf])
            dump("h1_%d_%d" % (hl, ci), H1.b(0, 4096), [H1.buf])
            red(LG.f(512, 1, 0, 4), LG.f(0, 512, 0, 4), ALU.max, [LG.buf],
                [LG.buf])
            tr(psF[0:1, 0:4], LG.f(512, 1, 0, 4), IDF[0:4, 0:4],
               [LG.buf, R_const.buf], bF)
            cp(LG.f(513, 4, 0, 1), psF[0:1, 0:4], [bF], [LG.buf])
            red(LG.f(517, 1, 0, 1), LG.f(513, 4, 0, 1), ALU.max, [LG.buf],
                [LG.buf])
            mm(psF[0:4, 4:5], ONESF[0:1, 0:4], LG.f(517, 1, 0, 1), True, True,
               [R_const.buf, LG.buf], bF)
            ts(LG.f(518, 1, 0, 4), psF[0:4, 4:5], -1.0, None, ALU.mult, ALU.mult,
               [bF], [LG.buf])
            act(LG.f(0, 512, 0, 4), LG.f(0, 512, 0, 4), AF.Exp, [LG.buf],
                [LG.buf], bias=LG.f(518, 1, 0, 4))
            mm(psF[:, 0:512], ONESF[0:4, :], LG.f(0, 512, 0, 4), True, True,
               [R_const.buf, LG.buf], bF)
            rcp(LG.f(520, 512, 0, 4), psF[0:4, 0:512], [bF], [LG.buf])
            tt(LG.f(0, 512, 0, 4), LG.f(0, 512, 0, 4), LG.f(520, 512, 0, 4),
               ALU.mult, [LG.buf], [LG.buf])
            ts(LG.f(0, 512, 0, 4), LG.f(0, 512, 0, 4), 1.0 - 4.0 * EPSF, EPSF,
               ALU.mult, ALU.add, [LG.buf], [LG.buf])
            for bi in range(4):
                tr(psE[0:128, 0:4], LG.f(bi * 128, 128, 0, 4), IDF[0:4, 0:4],
                   [LG.buf, R_const.buf], bE)
                cp(WGT.f((ci * 4 + bi) * 4, 4), psE[0:128, 0:4], [bE], [WGT.buf])
        A.free(GST); A.free(H1); A.free(LG); A.free(W1B); A.free(W2G)
        RS4 = A.alloc("rs4", 264)
        for blk in range(NBLK):
            w0 = WGT.f(blk * 4 + 0, 1); w1 = WGT.f(blk * 4 + 1, 1)
            w2 = WGT.f(blk * 4 + 2, 1); w3 = WGT.f(blk * 4 + 3, 1)
            od = SRC4.b(blk * 1024, 256); vd = SRC4.b(blk * 1024 + 256, 256)
            fs = SRC4.b(blk * 1024 + 512, 256); fl = SRC4.b(blk * 1024 + 768, 256)
            acc = RS4.b(0, 256)
            ts(acc, fs, w0, None, ALU.mult, ALU.mult, [SRC4.buf, WGT.buf],
               [RS4.buf])
            stt(acc, fl, w1, acc, ALU.mult, ALU.add, [SRC4.buf, WGT.buf,
                                                      RS4.buf], [RS4.buf])
            stt(acc, od, w2, acc, ALU.mult, ALU.add, [SRC4.buf, WGT.buf,
                                                      RS4.buf], [RS4.buf])
            stt(od, vd, w3, acc, ALU.mult, ALU.add, [SRC4.buf, WGT.buf,
                                                     RS4.buf], [SRC4.buf])
            act(RS4.b(256, 256), od, AF.Square, [SRC4.buf], [RS4.buf],
                accum=RS4.f(256, 1))
            ts(RS4.f(257, 1), RS4.f(256, 1), 1.0 / 256, None, ALU.mult, ALU.mult,
               [RS4.buf], [RS4.buf])
            act(RS4.f(257, 1), RS4.f(257, 1), AF.Sqrt, [RS4.buf], [RS4.buf],
                bias=1e-5)
            rcp(RS4.f(257, 1), RS4.f(257, 1), [RS4.buf], [RS4.buf])
            stt(od, od, RS4.f(257, 1), R_vecs.f(V_ONW, 256), ALU.mult, ALU.mult,
                [SRC4.buf, RS4.buf, R_vecs.buf], [SRC4.buf])
        A.free(RS4); A.free(WGT)

    quantize()
    if "l0f1" in stages:
        l0_ffn(0)
    if "l0a" in stages:
        l0_attn()
    if "l0f2" in stages:
        l0_ffn(1)
    if "l1f1" in stages:
        l1_ffn(0)
    if "l1a" in stages:
        l1_attn()
    if "l1f2" in stages:
        l1_ffn(1)
    dma("gpsimd", out_t.ap(), R_xres.f(0, DT * SH), reads=[R_xres.buf],
        writes=[])

    import contextlib
    with contextlib.ExitStack() as st:
        sems, qsems = {}, {}
        for e in ENGS:
            sems[("e", e)] = st.enter_context(nc.semaphore("sem_" + e))
            qsems[e] = [st.enter_context(nc.semaphore("q_%s_%d" % (e, s)))
                        for s in range(Prog.NQ)] if P.dma_cnt[e] else []
        block = st.enter_context(nc.Block())
        P.emit_all(sems, qsems, block, gps_prolog)
    return nc


def _fm(a):
    n = a.shape[0]
    return np.ascontiguousarray(
        a.T.reshape(DT, 128, n).transpose(1, 0, 2).reshape(128, DT * n))


def _pp(v, nt):
    return np.ascontiguousarray(v.reshape(nt, 128).T)


def make_inputs(inputs, S=4096, ncores=8):
    f32 = np.float32
    SH = S // 2
    g = {k: np.asarray(v) for k, v in inputs.items()}
    cid = np.eye(128)
    consts = {
        "fconst": np.tile(np.array([[1e-5, 1e-6, 0.125, -1.0]], f32), (128, 1)),
        "c_ones_bf": np.ones((128, 128), ml_dtypes.bfloat16),
        "c_id_bf": cid.astype(ml_dtypes.bfloat16),
        "c_rev_bf": cid[::-1].astype(ml_dtypes.bfloat16),
        "c_id_f": cid.astype(f32),
        "c_ones_f": np.ones((128, 128), f32),
        "c_trils": np.tril(np.ones((128, 128), f32), -1),
        "c_trili": np.tril(np.ones((128, 128), f32)),
        "c_trius": np.triu(np.ones((128, 128), f32), 1),
        "c_triui": np.triu(np.ones((128, 128), f32)),
    }
    fl = np.asarray(g["L1_attn_fir_long_filt"], f32).reshape(1024, 64)
    fs = np.asarray(g["L1_attn_fir_short_filt"], f32).reshape(1024, 5)
    w1T = np.ascontiguousarray(np.asarray(g["L1_ffn_w1"], f32).T)
    w2T = np.ascontiguousarray(np.asarray(g["L1_ffn_w2"], f32).T)
    wq1T = np.ascontiguousarray(np.asarray(g["L1_attn_Wq"], f32).T)
    wk1T = np.ascontiguousarray(np.asarray(g["L1_attn_Wk"], f32).T)
    wv1T = np.ascontiguousarray(np.asarray(g["L1_attn_Wv"], f32).T)
    wb1T = np.ascontiguousarray(np.asarray(g["L1_attn_Wb"], f32).T)
    w1aT = np.ascontiguousarray(np.asarray(g["L1_attn_gate_W1"], f32)[:, :1024].T)
    w1bT = np.ascontiguousarray(np.asarray(g["L1_attn_gate_W1"], f32)[:, 1024:].T)
    w2gT = np.ascontiguousarray(np.asarray(g["L1_attn_gate_W2"], f32).T)
    wo1T = np.ascontiguousarray(np.asarray(g["L1_attn_Wo"], f32).T)
    in_maps = []
    for c in range(ncores):
        b, j = c // 2, c % 2
        sl = slice(j * SH, (j + 1) * SH)
        hc0 = j * 512
        m = dict(consts)
        m["x_in"] = _fm(np.asarray(g["L0_attn_x"], f32)[b, sl])
        m["pe_in"] = _fm(np.asarray(g["L0_attn_pe"], f32)[0, sl])
        m["wq0"] = np.ascontiguousarray(np.asarray(g["L0_attn_wq"], f32)[:, hc0:hc0 + 512])
        m["wk0"] = np.ascontiguousarray(np.asarray(g["L0_attn_wk"], f32)[:, hc0:hc0 + 512])
        m["wv0"] = np.ascontiguousarray(np.asarray(g["L0_attn_wv"], f32)[:, hc0:hc0 + 512])
        m["wo0"] = np.asarray(g["L0_attn_wo"], f32)
        m["bqkv0"] = np.concatenate(
            [_pp(np.asarray(g["L0_attn_b" + t], f32)[hc0:hc0 + 512], 4)
             for t in ("q", "k", "v")], axis=1)
        m["bo0"] = _pp(np.asarray(g["L0_attn_bo"], f32), 8)
        m["gam0"] = _pp(np.asarray(g["L0_attn_gamma"], f32), 8)
        m["bet0"] = _pp(np.asarray(g["L0_attn_beta"], f32), 8)
        m["wrt0"] = _pp(np.asarray(g["L0_ffn_w_router"], f32), 8)
        m["wbl0"] = np.asarray(g["L0_ffn_w_block"], f32)
        m["w1t_h"] = np.ascontiguousarray(w1T[:, j * 2048:(j + 1) * 2048])
        m["w2t_h"] = np.ascontiguousarray(w2T[:, j * 512:(j + 1) * 512])
        m["g1v"] = _pp(np.asarray(g["L1_ffn_ln1_g"], f32), 8)
        m["b1lv"] = _pp(np.asarray(g["L1_ffn_ln1_b"], f32), 8)
        m["g2v"] = _pp(np.asarray(g["L1_ffn_ln2_g"], f32), 32)
        m["b2lv"] = _pp(np.asarray(g["L1_ffn_ln2_b"], f32), 32)
        m["bias1"] = _pp(np.asarray(g["L1_ffn_b1"], f32), 32)
        m["bias2"] = _pp(np.asarray(g["L1_ffn_b2"], f32), 8)
        m["wq1"] = np.ascontiguousarray(wq1T[:, j * 512:(j + 1) * 512])
        m["wk1"] = np.ascontiguousarray(wk1T[:, j * 512:(j + 1) * 512])
        m["wv1"] = np.ascontiguousarray(wv1T[:, j * 512:(j + 1) * 512])
        m["wb1"] = np.ascontiguousarray(wb1T[:, 2 * j:2 * j + 2])
        for nm, cw in (("cwq", "L1_attn_conv_q_w"), ("cwk", "L1_attn_conv_k_w"),
                       ("cwv", "L1_attn_conv_v_w")):
            w = np.asarray(g[cw], f32)[j * 512:(j + 1) * 512]
            m[nm] = np.ascontiguousarray(
                w.reshape(4, 128, 4).transpose(1, 0, 2).reshape(128, 16))
        for nm, filt, K in (("fpad_l", fl, 64), ("fpad_s", fs, 5)):
            p = np.zeros((512, 191), f32)
            p[:, 63:63 + K] = filt[j * 512:(j + 1) * 512, ::-1]
            m[nm] = p
        m["w1at"] = w1aT
        m["w1bt"] = w1bT
        m["gb1"] = _pp(np.asarray(g["L1_attn_gate_b1"], f32), 8)
        m["w2gt"] = w2gT
        gm = np.zeros((4, 5), f32)
        gm[:, 0] = np.asarray(g["L1_attn_gate_b2"], f32)
        gm[3, 1] = np.asarray(g["L1_attn_gate_copy_bias"], f32)[2 * j]
        gm[3, 2] = np.asarray(g["L1_attn_gate_copy_bias"], f32)[2 * j + 1]
        gm[:, 3] = np.asarray(g["L1_attn_gate_log_temp"], f32)[2 * j]
        gm[:, 4] = np.asarray(g["L1_attn_gate_log_temp"], f32)[2 * j + 1]
        m["gmisc"] = gm
        m["onw2"] = np.tile(np.asarray(g["L1_attn_o_norm_w"], f32), 2)[None, :]
        m["wot1"] = wo1T
        in_maps.append(m)
    return in_maps


def kernel(**inputs):
    S = 4096
    SH = S // 2
    nc = build(S=S)
    in_maps = make_inputs(inputs, S=S)
    res = run_bass_kernel_spmd(nc, in_maps, core_ids=list(range(8)))
    out = np.zeros((4, S, D), np.float32)
    for c in range(8):
        b, j = c // 2, c % 2
        o = np.asarray(res.results[c]["out"])
        o = o.reshape(128, DT, SH).transpose(1, 0, 2).reshape(D, SH)
        out[b, j * SH:(j + 1) * SH, :] = o.T
    return out


# version 06
# speedup vs baseline: 1.0702x; 1.0702x over previous
import numpy as np
import ml_dtypes
import concourse.bass as bass
import concourse.mybir as mybir
from concourse.bass_utils import run_bass_kernel_spmd

F32 = mybir.dt.float32
BF16 = mybir.dt.bfloat16
AF = mybir.ActivationFunctionType
ALU = mybir.AluOpType
AX = mybir.AxisListType.X

D = 1024
DT = 8
DECAY = 1.0 - 1.0 / 3000.0
EPSF = 0.08 * DECAY

ENGS = ("tensor", "vector", "scalar", "gpsimd", "sync")


class Buf:
    __slots__ = ("name", "writer", "readers")

    def __init__(self, name):
        self.name = name
        self.writer = None
        self.readers = []


class Op:
    __slots__ = ("eng", "emit", "is_dma", "waits", "tag", "src")


class Prog:
    def __init__(self):
        self.ops = []
        self.cnt = {e: 0 for e in ENGS}
        self.dma_cnt = {e: 0 for e in ENGS}

    NQ = 16

    def op(self, eng, emit, reads=(), writes=(), dma=False):
        import sys
        o = Op()
        f = sys._getframe(1)
        while f and f.f_code.co_name in ("dma", "dmaJ", "mm", "mmg", "tr", "act",
                                         "cp", "tt", "ts", "tss", "stt", "red",
                                         "rcp", "mset", "sel", "vop", "op"):
            f = f.f_back
        o.src = "%s:%d" % (f.f_code.co_name, f.f_lineno) if f else "?"
        o.eng, o.emit, o.is_dma, o.waits = eng, emit, dma, {}
        if dma:
            n = self.dma_cnt[eng]
            self.dma_cnt[eng] += 1
            s, r = n % Prog.NQ, n // Prog.NQ
            o.tag = (("q", eng, s), 16 * (r + 1))
        else:
            self.cnt[eng] += 1
            o.tag = (("e", eng), self.cnt[eng])
        deps = []
        for b in reads:
            if b.writer is not None:
                deps.append(b.writer)
        for b in writes:
            if b.writer is not None:
                deps.append(b.writer)
            deps.extend(b.readers)
        for d in deps:
            if d is o:
                continue
            k, v = d.tag
            if o.waits.get(k, -1) < v:
                o.waits[k] = v
        for b in reads:
            b.readers.append(o)
        for b in writes:
            b.writer = o
            b.readers = []
        self.ops.append(o)
        return o

    def emit_all(self, sems, qsems, block, gps_prolog):
        per = {e: [] for e in ENGS}
        for o in self.ops:
            per[o.eng].append(o)
        dma_tot = dict(self.dma_cnt)

        def runner(eng_name):
            def body(eng):
                import contextlib
                with contextlib.ExitStack() as rs:
                    if eng_name == "gpsimd":
                        gps_prolog(eng, rs)
                    hi = {}
                    ndma = 0
                    for o in per[eng_name]:
                        for k in sorted(o.waits):
                            v = o.waits[k]
                            if hi.get(k, 0) >= v:
                                continue
                            sem = sems[k] if k[0] == "e" else qsems[k[1]][k[2]]
                            eng.wait_ge(sem, v)
                            hi[k] = v
                        if o.is_dma:
                            s, r = ndma % Prog.NQ, ndma // Prog.NQ
                            ndma += 1
                            gk = ("q", eng_name, s)
                            if r > 0 and hi.get(gk, 0) < 16 * r:
                                eng.wait_ge(qsems[eng_name][s], 16 * r)
                                hi[gk] = 16 * r
                        try:
                            last = o.emit(eng)
                        except Exception:
                            import sys
                            dfl = o.emit.__defaults__ or ()
                            print("EMIT FAIL", eng_name, "op#",
                                  per[eng_name].index(o), "tag", o.tag,
                                  "defaults", [getattr(d, "__code__", None)
                                               and d.__code__.co_firstlineno
                                               for d in dfl],
                                  file=sys.stderr)
                            raise
                        last.annotate(o.src)
                        if o.is_dma:
                            last.then_inc(qsems[eng_name][o.tag[0][2]], 16)
                        else:
                            last.then_inc(sems[("e", eng_name)], 1)
                    for s in range(Prog.NQ):
                        tot = (dma_tot[eng_name] + Prog.NQ - 1 - s) // Prog.NQ
                        if tot and hi.get(("q", eng_name, s), 0) < 16 * tot:
                            eng.wait_ge(qsems[eng_name][s], 16 * tot)
            return body

        block.tensor(runner("tensor"))
        block.vector(runner("vector"))
        block.scalar(runner("scalar"))
        block.gpsimd(runner("gpsimd"))
        block.sync(runner("sync"))


def sub_ap(ap2d, off_elems, dims):
    return bass.AP(ap2d.tensor, ap2d.offset + off_elems,
                   [list(ap2d.ap[0])] + [[s, c] for (s, c) in dims])


class Region:
    def __init__(self, arena, name, off, sz, buf):
        self.arena, self.name, self.off, self.sz, self.buf = arena, name, off, sz, buf

    def f(self, c0, cols, p0=0, p1=128):
        assert 0 <= c0 and c0 + cols <= self.sz, (self.name, c0, cols, self.sz)
        return self.arena.h.ap()[p0:p1, self.off + c0:self.off + c0 + cols]

    def b(self, c0, cols, p0=0, p1=128):
        assert 0 <= c0 and c0 + cols <= 2 * self.sz, (self.name, c0, cols)
        full = self.arena.h.ap()[p0:p1, self.off:self.off + self.sz]
        return full.bitcast(BF16)[:, c0:c0 + cols]

    def fap(self, dims, c0=0, p0=0, p1=128):
        return sub_ap(self.arena.h.ap()[p0:p1, self.off:self.off + self.sz],
                      c0, dims)

    def bap(self, dims, c0=0, p0=0, p1=128):
        base = self.arena.h.ap()[p0:p1, self.off:self.off + self.sz].bitcast(BF16)
        return sub_ap(base, c0, dims)


class Arena:
    def __init__(self, handle, n_f32):
        self.h, self.n = handle, n_f32
        self.free_list = [(0, n_f32)]
        self.live, self.retired = {}, []

    def alloc(self, name, f32_cols, high=False):
        sz = int(f32_cols)
        idxs = range(len(self.free_list) - 1, -1, -1) if high else \
            range(len(self.free_list))
        for i in idxs:
            off, s = self.free_list[i]
            if s >= sz:
                if s == sz:
                    self.free_list.pop(i)
                elif high:
                    off = off + s - sz
                    self.free_list[i] = (self.free_list[i][0], s - sz)
                else:
                    self.free_list[i] = (off + sz, s - sz)
                buf = Buf(name)
                keep = []
                for (ro, rs, rb) in self.retired:
                    if ro < off + sz and off < ro + rs:
                        if rb.writer is not None and rb.writer not in buf.readers:
                            buf.readers.append(rb.writer)
                        for r in rb.readers:
                            if r not in buf.readers:
                                buf.readers.append(r)
                        if ro < off:
                            keep.append((ro, off - ro, rb))
                        if ro + rs > off + sz:
                            keep.append((off + sz, ro + rs - off - sz, rb))
                    else:
                        keep.append((ro, rs, rb))
                self.retired = keep
                reg = Region(self, name, off, sz, buf)
                self.live[name] = reg
                return reg
        raise MemoryError(
            f"arena OOM: {name} needs {sz}, free={self.free_list}, "
            f"live={[(r.name, r.off, r.sz) for r in self.live.values()]}")

    def free(self, reg):
        del self.live[reg.name]
        self.retired.append((reg.off, reg.sz, reg.buf))
        self.free_list.append((reg.off, reg.sz))
        self.free_list.sort()
        out = []
        for o, s in self.free_list:
            if out and out[-1][0] + out[-1][1] == o:
                out[-1][1] += s
            else:
                out.append([o, s])
        self.free_list = [tuple(x) for x in out]


def build(S=4096, ncores=8, stages=("l0f1", "l0a", "l0f2", "l1f1", "l1a", "l1f2"),
          dbg=()):
    SH = S // 2
    SK = S // 128
    NBLK = S // 128
    NW = S // 64
    KSEL = int(S * 0.75)
    QC = min(1024, S)
    dbg = set(dbg)
    stages = tuple(stages)

    nc = bass.Bass("TRN2", target_bir_lowering=False)
    P = Prog()
    REGS = {}

    def din(name, shape, dtype=F32):
        return nc.dram_tensor(name, shape, dtype, kind="ExternalInput")

    x_in = din("x_in", [128, DT * SH])
    pe_in = din("pe_in", [128, DT * SH])
    wq0 = din("wq0", [D, 512]); wk0 = din("wk0", [D, 512]); wv0 = din("wv0", [D, 512])
    wo0 = din("wo0", [D, D])
    bqkv0 = din("bqkv0", [128, 12])
    bo0 = din("bo0", [128, DT]); gam0 = din("gam0", [128, DT]); bet0 = din("bet0", [128, DT])
    wrt0 = din("wrt0", [128, DT]); wbl0 = din("wbl0", [D, D])
    w1t_h = din("w1t_h", [D, 2048]); w2t_h = din("w2t_h", [4096, 512])
    g1v = din("g1v", [128, DT]); b1lv = din("b1lv", [128, DT])
    g2v = din("g2v", [128, 32]); b2lv = din("b2lv", [128, 32])
    bias1 = din("bias1", [128, 32]); bias2 = din("bias2", [128, DT])
    wq1 = din("wq1", [D, 512]); wk1 = din("wk1", [D, 512]); wv1 = din("wv1", [D, 512])
    wb1 = din("wb1", [D, 2])
    cwq = din("cwq", [128, 16]); cwk = din("cwk", [128, 16]); cwv = din("cwv", [128, 16])
    fpad_l = din("fpad_l", [512, 191]); fpad_s = din("fpad_s", [512, 191])
    w1at = din("w1at", [D, D]); w1bt = din("w1bt", [16, D])
    gb1 = din("gb1", [128, DT]); w2gt = din("w2gt", [D, 4])
    gmisc = din("gmisc", [4, 5])
    onw2 = din("onw2", [1, 512])
    wot1 = din("wot1", [D, D])
    fconst = din("fconst", [128, 4])
    c_ones_bf = din("c_ones_bf", [128, 128], BF16)
    c_id_bf = din("c_id_bf", [128, 128], BF16)
    c_rev_bf = din("c_rev_bf", [128, 128], BF16)
    c_id_f = din("c_id_f", [128, 128])
    c_ones_f = din("c_ones_f", [128, 128])
    c_trils = din("c_trils", [128, 128])
    c_trili = din("c_trili", [128, 128])
    c_trius = din("c_trius", [128, 128])
    c_triui = din("c_triui", [128, 128])
    out_t = nc.dram_tensor("out", [128, DT * SH], F32, kind="ExternalOutput")

    def dsh(name, shape, dtype):
        return nc.dram_tensor(name, shape, dtype, kind="Internal",
                              addr_space="Shared")

    sh_xpe = dsh("sh_xpe", [128, DT * S], BF16)
    sh_av = dsh("sh_av", [128, DT * S], BF16)
    sh_xln1 = dsh("sh_xln1", [128, DT * S], BF16)
    sh_gsh = dsh("sh_gsh", [128, DT * S], BF16)
    sh_w1q = dsh("sh_w1q", [D, 4096], BF16)
    sh_w2q = dsh("sh_w2q", [4096, D], BF16)
    sh_qs = dsh("sh_qs", [2, 4], F32)
    sh_cg = dsh("sh_cg", [2, D], F32)
    sh_sc0 = dsh("sh_sc0", [2, SH], F32)
    sh_sc1 = dsh("sh_sc1", [2, SH], F32)
    bar_in = nc.dram_tensor("bar_in", [1, 4], F32, kind="Internal")
    bar_out = nc.dram_tensor("bar_out", [1, 4], F32, kind="Internal")
    av_aug = nc.dram_tensor("av_aug", [128, 8 * (S // 128) * 65], BF16,
                            kind="Internal")
    tp_l = nc.dram_tensor("tp_l", [512, 191], BF16, kind="Internal")
    tp_s = nc.dram_tensor("tp_s", [512, 191], BF16, kind="Internal")
    B_sh = {t: Buf("sh_" + t) for t in
            ("xpe", "av", "xln1", "gsh", "w1q", "w2q", "qs", "cg", "sc0", "sc1",
             "tpl", "tps", "bar")}
    ALL_SH = list(B_sh.values())

    B_avaug = Buf("av_aug")
    ARENA = 52224
    arena_h = nc.alloc_sbuf_tensor("arena", [128, ARENA], F32)
    A = Arena(arena_h, ARENA)
    R_xres = A.alloc("xres", DT * SH)
    R_wst = A.alloc("wst", 2048)
    R_scal = A.alloc("scal", 64)
    STAT = {"r": None, "n": 0}
    R_const = A.alloc("const", 960)
    R_vecs = A.alloc("vecs", 960)
    
    psA = nc.alloc_psum_tensor("psA", [128, 1024], F32)
    psB = nc.alloc_psum_tensor("psB", [128, 1024], F32)
    psC = nc.alloc_psum_tensor("psC", [128, 512], F32)
    psD = nc.alloc_psum_tensor("psD", [128, 512], F32)
    psE = nc.alloc_psum_tensor("psE", [128, 512], F32)
    psF = nc.alloc_psum_tensor("psF", [128, 512], F32)
    bA0, bA1, bB0, bB1 = Buf("A0"), Buf("A1"), Buf("B0"), Buf("B1")
    bC, bD, bE, bF = Buf("C"), Buf("D"), Buf("E"), Buf("F")

    C_TRILS, C_TRILI, C_TRIUS, C_TRIUI, C_IDF, C_ONESF = 0, 128, 256, 384, 512, 640
    C_ONESBF, C_IDBF, C_REVBF = 768, 832, 0
    TRILS = R_const.f(C_TRILS, 128); TRILI = R_const.f(C_TRILI, 128)
    TRIUS = R_const.f(C_TRIUS, 128); TRIUI = R_const.f(C_TRIUI, 128)
    IDF = R_const.f(C_IDF, 128); ONESF = R_const.f(C_ONESF, 128)
    ONESBF = R_const.b(2 * C_ONESBF, 128); IDBF = R_const.b(2 * C_IDBF, 128)
    REVBF = R_const.b(2 * 896, 128)

    V_BQKV, V_BO, V_G0, V_B0, V_WRT = 0, 12, 20, 28, 36
    V_G1, V_B1L, V_BIAS2 = 44, 52, 60
    V_G2, V_B2L, V_BIAS1 = 68, 100, 132
    V_GB1, V_GMISC = 164, 172
    V_CWQ, V_CWK, V_CWV = 180, 196, 212
    V_ONW = 228
    V_SCAL = 740
    V_C, V_G, V_HCB = 770, 778, 786
    V_Z = 794
    V_FC = 800

    def VS(i, n=1, p0=0, p1=128):
        return R_scal.f(i, n, p0, p1)

    def dma(eng, out_ap, in_ap, reads=(), writes=(), allow_nc=False):
        return P.op(eng, lambda e, o=out_ap, i=in_ap, a=allow_nc:
                    e.dma_start(out=o, in_=i, allow_slow_non_contiguous=a),
                    reads=list(reads), writes=list(writes), dma=True)

    def dmaJ(out_fn, in_fn, reads=(), writes=()):
        return P.op("gpsimd",
                    lambda e, of=out_fn, inf=in_fn: e.dma_start(out=of(), in_=inf()),
                    reads=list(reads), writes=list(writes), dma=True)

    def mm(out_ap, lhsT, rhs, start, stop, reads, wbuf):
        return P.op("tensor", lambda e, o=out_ap, l=lhsT, r=rhs, s1=start, s2=stop:
                    e.matmul(out=o, lhsT=l, rhs=r, start=s1, stop=s2),
                    reads=list(reads), writes=[wbuf])

    def mmg(group, reads, wbuf):
        def em(e, g=tuple(group)):
            last = None
            for (o, l, r, s1, s2) in g:
                last = e.matmul(out=o, lhsT=l, rhs=r, start=s1, stop=s2)
            return last
        return P.op("tensor", em, reads=list(reads), writes=[wbuf])

    def tr(out_ap, in_ap, ident, reads, wbuf):
        return P.op("tensor", lambda e, o=out_ap, i=in_ap, d=ident:
                    e.transpose(out=o, in_=i, identity=d),
                    reads=list(reads), writes=[wbuf])

    def act(out_ap, in_ap, func, reads, writes, bias=0.0, scale=1.0, accum=None):
        def em(e, o=out_ap, i=in_ap, f=func, b=bias, s=scale, a=accum):
            return e.activation(out=o, in_=i, func=f, bias=b, scale=s, accum_out=a)
        r = list(reads)
        if (isinstance(bias, float) and bias != 0.0) or \
           (isinstance(scale, float) and scale != 1.0):
            r.append(R_vecs.buf)
        return P.op("scalar", em, reads=r, writes=list(writes))

    def cp(out_ap, in_ap, reads, writes, eng="vector"):
        return P.op(eng, lambda e, o=out_ap, i=in_ap: e.tensor_copy(out=o, in_=i),
                    reads=list(reads), writes=list(writes))

    def tt(out_ap, a, b, op, reads, writes, eng="vector"):
        return P.op(eng, lambda e, o=out_ap, x=a, y=b, p=op:
                    e.tensor_tensor(out=o, in0=x, in1=y, op=p),
                    reads=list(reads), writes=list(writes))

    def ts(out_ap, a, s1, s2, op0, op1, reads, writes, eng="vector"):
        if s2 is None:
            def em(e, o=out_ap, x=a, u=s1, p0=op0):
                return e.tensor_single_scalar(out=o, in_=x, scalar=u, op=p0)
        else:
            def em(e, o=out_ap, x=a, u=s1, v=s2, p0=op0, p1=op1):
                return e.tensor_scalar(out=o, in0=x, scalar1=u, scalar2=v,
                                       op0=p0, op1=p1)
        return P.op(eng, em, reads=list(reads), writes=list(writes))

    def tss(out_ap, a, s1, op, reads, writes, eng="vector"):
        def em(e, o=out_ap, x=a, u=s1, p=op):
            return e.tensor_single_scalar(out=o, in_=x, scalar=u, op=p)
        return P.op(eng, em, reads=list(reads), writes=list(writes))

    def stt(out_ap, in0, scalar, in1, op0, op1, reads, writes, eng="vector"):
        def em(e, o=out_ap, x=in0, s=scalar, y=in1, p0=op0, p1=op1):
            return e.scalar_tensor_tensor(out=o, in0=x, scalar=s, in1=y,
                                          op0=p0, op1=p1)
        return P.op(eng, em, reads=list(reads), writes=list(writes))

    def red(out_ap, in_ap, op, reads, writes, absval=False):
        def em(e, o=out_ap, i=in_ap, p=op, a=absval):
            return e.tensor_reduce(out=o, in_=i, axis=AX, op=p,
                                   apply_absolute_value=a)
        return P.op("vector", em, reads=list(reads), writes=list(writes))

    def rcp(out_ap, in_ap, reads, writes):
        return P.op("vector", lambda e, o=out_ap, i=in_ap:
                    e.reciprocal(out=o, in_=i),
                    reads=list(reads), writes=list(writes))

    def mset(ap, val, reads, writes, eng="vector"):
        return P.op(eng, lambda e, a=ap, v=val: e.memset(a, v),
                    reads=list(reads), writes=list(writes))

    def sel(out_ap, mask, t_ap, f_ap, reads, writes):
        return P.op("vector", lambda e, o=out_ap, m=mask, a=t_ap, b=f_ap:
                    e.select(out=o, mask=m, on_true=a, on_false=b,
                             add_drain=True),
                    reads=list(reads), writes=list(writes))

    def barrier():
        def em(e):
            return e.collective_compute(
                "AllReduce", ALU.add,
                replica_groups=[[2 * k, 2 * k + 1] for k in range(ncores // 2)],
                ins=[bar_in.ap()], outs=[bar_out.ap()])
        return P.op("gpsimd", em, reads=[], writes=ALL_SH)

    dbg_bufs = {}

    def dump(name, ap, rbufs):
        if name not in dbg:
            return
        import functools, operator
        fs = functools.reduce(operator.mul, ap.shape[1:], 1)
        t = nc.dram_tensor("dbg_" + name, [ap.shape[0], fs], ap.dtype,
                           kind="ExternalOutput")
        o = P.op("gpsimd", lambda e, t=t, a=ap: e.dma_start(out=t.ap(), in_=a),
                 reads=list(rbufs), writes=[], dma=True)
        o.src = "dump_" + name

    def gps_prolog(g, rs):
        r0 = rs.enter_context(g.register("rj"))
        pid = g.partition_id()
        g.reg_mod(r0, pid, 2)
        specs = [("j4", 4, 0), ("jSH", SH, 0), ("j2048", 2048, 0),
                 ("j512", 512, 0), ("j4S", 4 * S, 0), ("j4So2", 4 * S, 2 * S)]
        for qc in range(1, S // QC):
            specs.append(("j4Sq%d" % qc, 4 * S, qc * QC))
        for c0 in range(512, SH, 512):
            specs.append(("jSHp%d" % c0, SH, c0))
        for key, mult, add in specs:
            r = rs.enter_context(g.register("rg_" + key))
            g.reg_mul(r, r0, mult)
            if add:
                g.reg_add(r, r, add)
            REGS[key] = g.snap(r, min_val=0, max_val=mult + add)
        REGS["jSHp0"] = REGS["jSH"]
        REGS["j4Sq0"] = REGS["j4S"]

    dma("sync", R_const.f(C_TRILS, 128), c_trils.ap(), writes=[R_const.buf])
    dma("sync", R_const.f(C_TRILI, 128), c_trili.ap(), writes=[R_const.buf])
    dma("sync", R_const.f(C_TRIUS, 128), c_trius.ap(), writes=[R_const.buf])
    dma("sync", R_const.f(C_TRIUI, 128), c_triui.ap(), writes=[R_const.buf])
    dma("sync", R_const.f(C_IDF, 128), c_id_f.ap(), writes=[R_const.buf])
    dma("sync", R_const.f(C_ONESF, 128), c_ones_f.ap(), writes=[R_const.buf])
    dma("sync", R_const.b(2 * C_ONESBF, 128), c_ones_bf.ap(), writes=[R_const.buf])
    dma("sync", R_const.b(2 * C_IDBF, 128), c_id_bf.ap(), writes=[R_const.buf])
    dma("sync", R_const.b(2 * 896, 128), c_rev_bf.ap(), writes=[R_const.buf])
    for tname, col, w in ((bqkv0, V_BQKV, 12), (bo0, V_BO, 8), (gam0, V_G0, 8),
                          (bet0, V_B0, 8), (wrt0, V_WRT, 8), (g1v, V_G1, 8),
                          (b1lv, V_B1L, 8), (bias2, V_BIAS2, 8), (g2v, V_G2, 32),
                          (b2lv, V_B2L, 32), (bias1, V_BIAS1, 32), (gb1, V_GB1, 8),
                          (cwq, V_CWQ, 16), (cwk, V_CWK, 16), (cwv, V_CWV, 16)):
        dma("sync", R_vecs.f(col, w), tname.ap(), writes=[R_vecs.buf])
    dma("sync", R_vecs.f(V_GMISC, 5, 0, 4), gmisc.ap(), writes=[R_vecs.buf])
    dma("sync", R_vecs.f(V_ONW, 512), bass.AP(onw2, 0, [[0, 128], [1, 512]]),
        writes=[R_vecs.buf])
    dma("sync", R_vecs.f(V_FC, 4), fconst.ap(), writes=[R_vecs.buf])
    for fi, fv in enumerate((1e-5, 1e-6, 0.125, -1.0)):
        nc.const_aps.register(F32, fv, R_vecs.f(V_FC + fi, 1))
    dma("sync", R_xres.f(0, DT * SH), x_in.ap(), writes=[R_xres.buf])
    mset(R_vecs.f(V_Z, 4, 0, 1), 0.0, [], [R_vecs.buf])
    dma("gpsimd", bar_in.ap(), R_vecs.f(V_Z, 4, 0, 1), reads=[R_vecs.buf],
        writes=[B_sh["bar"]])

    def quantize():
        RQ = A.alloc("qstage", 2048 * 2)
        RT = A.alloc("qtmp", 4416)
        RO = A.alloc("qout", 8192)
        RS = A.alloc("qsum", 64)
        RCG = A.alloc("rcg", 512)
        for t in range(8):
            st_ap = RQ.f((t % 2) * 2048, 2048)
            dma("sync", st_ap, w1t_h.ap()[t * 128:(t + 1) * 128, :], writes=[RQ.buf])
            red(RS.f(t, 1), st_ap, ALU.add, [RQ.buf], [RS.buf], absval=True)
        for t in range(32):
            st_ap = RQ.f((t % 2) * 2048, 512)
            dma("sync", st_ap, w2t_h.ap()[t * 128:(t + 1) * 128, :], writes=[RQ.buf])
            red(RS.f(8 + t, 1), st_ap, ALU.add, [RQ.buf], [RS.buf], absval=True)
        red(RS.f(40, 1), RS.f(0, 8), ALU.add, [RS.buf], [RS.buf])
        red(RS.f(41, 1), RS.f(8, 32), ALU.add, [RS.buf], [RS.buf])
        mm(psC[0:1, 0:2], ONESF[:, 0:1], RS.f(40, 2), True, True,
           [R_const.buf, RS.buf], bC)
        cp(RS.f(42, 2, 0, 1), psC[0:1, 0:2], [bC], [RS.buf])
        dmaJ(lambda: bass.AP(sh_qs, REGS["j4"], [[4, 1], [1, 2]]),
             lambda: RS.f(42, 2, 0, 1), reads=[RS.buf], writes=[B_sh["qs"]])
        barrier()
        dma("sync", RS.f(44, 2, 0, 1), sh_qs.ap()[0:1, 0:2], reads=[B_sh["qs"]],
            writes=[RS.buf])
        dma("sync", RS.f(46, 2, 0, 1), sh_qs.ap()[1:2, 0:2], reads=[B_sh["qs"]],
            writes=[RS.buf])
        P.op("vector", lambda e: e.tensor_add(out=RS.f(48, 2, 0, 1),
                                              in0=RS.f(44, 2, 0, 1),
                                              in1=RS.f(46, 2, 0, 1)),
             reads=[RS.buf], writes=[RS.buf])
        ts(RS.f(48, 2, 0, 1), RS.f(48, 2, 0, 1), 1.0 / (1024 * 4096), 1e-5,
           ALU.mult, ALU.max, [RS.buf], [RS.buf])
        mm(psC[0:128, 0:2], ONESF[0:1, :], RS.f(48, 2, 0, 1), True, True,
           [R_const.buf, RS.buf], bC)
        cp(VS(0, 2), psC[:, 0:2], [bC], [R_scal.buf])
        rcp(VS(2, 2), VS(0, 2), [R_scal.buf], [R_scal.buf])
        ts(VS(14, 1), VS(1, 1), -1.0, None, ALU.mult, ALU.mult, [R_scal.buf],
           [R_scal.buf])
        for t in range(8):
            sl = t % 2
            st_ap = RQ.f(sl * 2048, 2048)
            dma("sync", st_ap, w1t_h.ap()[t * 128:(t + 1) * 128, :], writes=[RQ.buf])
            pa, na = RT.f(0, 2048), RT.f(2048, 2048)
            ts(pa, st_ap, VS(2, 1), 0.5, ALU.mult, ALU.is_ge,
               [RQ.buf, R_scal.buf], [RT.buf])
            ts(na, st_ap, VS(2, 1), -0.5, ALU.mult, ALU.is_le,
               [RQ.buf, R_scal.buf], [RT.buf])
            tt(RO.b(t * 2048, 2048), pa, na, ALU.subtract, [RT.buf], [RO.buf])
        dmaJ(lambda: bass.AP(sh_w1q, REGS["j2048"],
                             [[4096, 128], [128 * 4096, 8], [1, 2048]]),
             lambda: RO.b(0, 16384), reads=[RO.buf], writes=[B_sh["w1q"]])
        for t in range(32):
            sl = t % 2
            st_ap = RQ.f(sl * 2048, 512)
            dma("sync", st_ap, w2t_h.ap()[t * 128:(t + 1) * 128, :], writes=[RQ.buf])
            pa, na = RT.f(0, 512), RT.f(2048, 512)
            ts(pa, st_ap, VS(3, 1), 0.5, ALU.mult, ALU.is_ge,
               [RQ.buf, R_scal.buf], [RT.buf])
            ts(na, st_ap, VS(3, 1), -0.5, ALU.mult, ALU.is_le,
               [RQ.buf, R_scal.buf], [RT.buf])
            raw = RT.b(2 * 4096 + 64, 512)
            tt(raw, pa, na, ALU.subtract, [RT.buf], [RT.buf])
            cp(RT.b(2 * 4096, 1), R_vecs.f(V_B2L + t, 1), [R_vecs.buf], [RT.buf])
            cp(RT.b(2 * 4096 + 1, 1), R_vecs.f(V_G2 + t, 1), [R_vecs.buf],
               [RT.buf])
            mm(psD[0:1, 0:512], RT.b(2 * 4096, 1), raw, t == 0, t == 31,
               [RT.buf], bD)
            mm(psE[0:1, 0:512], RT.b(2 * 4096 + 1, 1), raw, t == 0, t == 31,
               [RT.buf], bE)
            ts(RO.b(t * 512, 512), raw, R_vecs.f(V_G2 + t, 1), None, ALU.mult,
               ALU.mult, [RT.buf, R_vecs.buf], [RO.buf])
        dmaJ(lambda: bass.AP(sh_w2q, REGS["j512"],
                             [[1024, 128], [128 * 1024, 32], [1, 512]]),
             lambda: RO.b(0, 16384), reads=[RO.buf], writes=[B_sh["w2q"]])
        cp(RCG.f(0, 512, 0, 1), psD[0:1, 0:512], [bD], [RCG.buf])
        cp(RCG.f(0, 512, 32, 33), psE[0:1, 0:512], [bE], [RCG.buf])
        dmaJ(lambda: bass.AP(sh_cg, REGS["j512"], [[1024, 2], [1, 512]]),
             lambda: bass.AP(arena_h, RCG.off, [[32 * ARENA, 2], [1, 512]]),
             reads=[RCG.buf], writes=[B_sh["cg"]])
        for (fsrc, fdst, bname) in ((fpad_l, tp_l, "tpl"), (fpad_s, tp_s, "tps")):
            for t in range(4):
                st_ap = RQ.f((t % 2) * 2048, 191)
                dma("sync", st_ap, fsrc.ap()[t * 128:(t + 1) * 128, :],
                    writes=[RQ.buf])
                ob = RT.b((t % 2) * 4096, 191)
                cp(ob, st_ap, [RQ.buf], [RT.buf])
                dma("gpsimd", fdst.ap()[t * 128:(t + 1) * 128, :], ob,
                    reads=[RT.buf], writes=[B_sh[bname]])
        barrier()
        dma("sync", R_vecs.f(V_C, 8),
            bass.AP(sh_cg, 0, [[1, 128], [128, 8]]),
            reads=[B_sh["cg"]], writes=[R_vecs.buf], allow_nc=True)
        dma("sync", R_vecs.f(V_G, 8),
            bass.AP(sh_cg, D, [[1, 128], [128, 8]]),
            reads=[B_sh["cg"]], writes=[R_vecs.buf], allow_nc=True)
        ts(R_vecs.f(V_HCB, 8), R_vecs.f(V_C, 8), VS(1, 1), None, ALU.mult,
           ALU.mult, [R_scal.buf, R_vecs.buf], [R_scal.buf, R_vecs.buf])
        tt(R_vecs.f(V_HCB, 8), R_vecs.f(V_HCB, 8), R_vecs.f(V_BIAS2, 8), ALU.add,
           [R_vecs.buf], [R_vecs.buf])
        ts(R_vecs.f(V_HCB, 8), R_vecs.f(V_HCB, 8), 0.5, None, ALU.mult, ALU.mult,
           [R_vecs.buf], [R_vecs.buf])
        A.free(RQ); A.free(RT); A.free(RO); A.free(RS); A.free(RCG)

    def free_stat():
        if STAT["r"] is not None:
            A.free(STAT["r"])
            STAT["r"] = None

    def ln_stats(eps=1e-5):
        assert STAT["r"] is None
        STAT["n"] += 1
        STAT["r"] = A.alloc("stat%d" % STAT["n"], 2 * SH)
        RB = A.alloc("lnb", 1024)
        for c0 in range(0, SH, 512):
            cw = min(512, SH - c0)
            for t in range(DT):
                src = R_xres.f(t * SH + c0, cw)
                xb = RB.b((t % 2) * 512, cw)
                cp(xb, src, [R_xres.buf], [RB.buf])
                mm(psA[:, 0:cw], ONESBF, xb, t == 0, t == DT - 1,
                   [R_const.buf, RB.buf], bA0)
                sq = RB.b(1024 + (t % 2) * 512, cw)
                act(sq, src, AF.Square, [R_xres.buf], [RB.buf])
                mm(psA[:, 512:512 + cw], ONESBF, sq, t == 0, t == DT - 1,
                   [R_const.buf, RB.buf], bA1)
            m = STAT["r"].f(c0, cw); iv = STAT["r"].f(SH + c0, cw)
            ts(m, psA[:, 0:cw], 1.0 / D, None, ALU.mult, ALU.mult, [bA0],
               [STAT["r"].buf])
            tt(iv, m, m, ALU.mult, [STAT["r"].buf], [STAT["r"].buf])
            stt(iv, psA[:, 512:512 + cw], 1.0 / D, iv, ALU.mult, ALU.subtract,
                [bA1, STAT["r"].buf], [STAT["r"].buf])
            act(iv, iv, AF.Sqrt, [STAT["r"].buf], [STAT["r"].buf], bias=eps)
            rcp(iv, iv, [STAT["r"].buf], [STAT["r"].buf])
        A.free(RB)

    def load_w_bf16(wsrc, ncolt, reg, kt=DT):
        for m in range(ncolt):
            st_ap = R_wst.f((m % 2) * 1024, kt * 128)
            dma("sync", st_ap, bass.AP(wsrc, m * 128,
                                       [[ncolt * 128, 128],
                                        [128 * ncolt * 128, kt], [1, 128]]),
                writes=[R_wst.buf])
            cp(reg.b(m * kt * 128, kt * 128), st_ap, [R_wst.buf], [reg.buf])

    def l0_ffn(inv):
        sh_sc = sh_sc0 if inv == 0 else sh_sc1
        b_sc = B_sh["sc0" if inv == 0 else "sc1"]
        ln_stats()
        XLC = A.alloc("xlc", DT * 512)
        XB = A.alloc("xb", DT * SH // 2)
        WBL = A.alloc("wbl", 4096)
        YB0 = A.alloc("y0", DT * 512)
        YB1 = A.alloc("y1", DT * 512)
        SELB = A.alloc("selb", 512)
        SCA = A.alloc("sca", 512)
        SCR = A.alloc("scr", SH)
        for c0 in range(0, SH, 512):
            cw = min(512, SH - c0)
            for t in range(DT):
                d = XLC.f(t * 512, cw)
                tt(d, R_xres.f(t * SH + c0, cw), STAT["r"].f(c0, cw), ALU.subtract,
                   [R_xres.buf, STAT["r"].buf], [XLC.buf])
                tt(d, d, STAT["r"].f(SH + c0, cw), ALU.mult, [XLC.buf, STAT["r"].buf],
                   [XLC.buf])
                cp(XB.b(t * SH + c0, cw), d, [XLC.buf], [XB.buf])
                if t == 0:
                    ts(SCA.f(0, cw), d, R_vecs.f(V_WRT + t, 1), None, ALU.mult,
                       ALU.mult, [XLC.buf, R_vecs.buf], [SCA.buf])
                else:
                    stt(SCA.f(0, cw), d, R_vecs.f(V_WRT + t, 1), SCA.f(0, cw),
                        ALU.mult, ALU.add, [XLC.buf, R_vecs.buf, SCA.buf],
                        [SCA.buf])
            mm(psC[0:1, 0:cw], ONESF[:, 0:1], SCA.f(0, cw), True, True,
               [R_const.buf, SCA.buf], bC)
            cp(SCR.f(c0, cw, 0, 1), psC[0:1, 0:cw], [bC], [SCR.buf])
        dmaJ(lambda: bass.AP(sh_sc, REGS["jSH"], [[SH, 1], [1, SH]]),
             lambda: SCR.f(0, SH, 0, 1), reads=[SCR.buf], writes=[b_sc])
        barrier()
        W = S // 128
        SRCH = R_wst
        dma("sync", R_wst.f(0, W), bass.AP(sh_sc, 0, [[W, 128], [1, W]]),
            reads=[b_sc], writes=[R_wst.buf])
        tt(R_wst.f(W, 8), R_vecs.f(V_WRT, 8), R_vecs.f(V_WRT, 8), ALU.mult,
           [R_vecs.buf], [R_wst.buf])
        red(R_wst.f(W + 8, 1), R_wst.f(W, 8), ALU.add, [R_wst.buf], [R_wst.buf])
        mm(psD[0:1, 0:1], ONESF[:, 0:1], R_wst.f(W + 8, 1), True, True,
           [R_const.buf, R_wst.buf], bD)
        act(VS(7, 1, 0, 1), psD[0:1, 0:1], AF.Sqrt, [bD], [R_scal.buf])
        ts(VS(7, 1, 0, 1), VS(7, 1, 0, 1), 33.0, None, ALU.mult, ALU.mult,
           [R_scal.buf], [R_scal.buf])
        ts(VS(8, 1, 0, 1), VS(7, 1, 0, 1), -1.0, None, ALU.mult, ALU.mult,
           [R_scal.buf], [R_scal.buf])
        cp(VS(9, 1, 0, 1), VS(7, 1, 0, 1), [R_scal.buf], [R_scal.buf])
        CMPB = None
        for it in range(34):
            tt(VS(10, 1, 0, 1), VS(8, 1, 0, 1), VS(9, 1, 0, 1), ALU.add,
               [R_scal.buf], [R_scal.buf])
            ts(VS(10, 1, 0, 1), VS(10, 1, 0, 1), 0.5, None, ALU.mult, ALU.mult,
               [R_scal.buf], [R_scal.buf])
            mm(psD[0:128, 0:1], ONESF[0:1, :], VS(10, 1, 0, 1), True, True,
               [R_const.buf, R_scal.buf], bD)
            cp(VS(13, 1), psD[:, 0:1], [bD], [R_scal.buf])
            tss(R_wst.f(128, W), R_wst.f(0, W), VS(13, 1), ALU.is_gt,
                [R_wst.buf, R_scal.buf], [R_wst.buf])
            red(R_wst.f(128 + W, 1), R_wst.f(128, W), ALU.add, [R_wst.buf], [R_wst.buf])
            mm(psD[0:1, 4:5], ONESF[:, 0:1], R_wst.f(128 + W, 1), True, True,
               [R_const.buf, R_wst.buf], bD)
            tss(VS(12, 1, 0, 1), psD[0:1, 4:5], float(KSEL), ALU.is_ge,
                [bD], [R_scal.buf])
            tt(VS(13, 1, 0, 1), VS(10, 1, 0, 1), VS(8, 1, 0, 1), ALU.subtract,
               [R_scal.buf], [R_scal.buf])
            stt(VS(8, 1, 0, 1), VS(13, 1, 0, 1), VS(12, 1, 0, 1), VS(8, 1, 0, 1),
                ALU.mult, ALU.add, [R_scal.buf], [R_scal.buf])
            tt(VS(13, 1, 0, 1), VS(9, 1, 0, 1), VS(10, 1, 0, 1), ALU.subtract,
               [R_scal.buf], [R_scal.buf])
            stt(VS(9, 1, 0, 1), VS(13, 1, 0, 1), VS(12, 1, 0, 1), VS(10, 1, 0, 1),
                ALU.mult, ALU.add, [R_scal.buf], [R_scal.buf])
        tss(SCR.b(0, SH, 32, 33), SCR.f(0, SH, 0, 1), VS(8, 1, 0, 1), ALU.is_gt,
            [SCR.buf, R_scal.buf], [SCR.buf])
        A.free(SCA)
        load_w_bf16(wbl0, 8, WBL)
        slots = ((psA, bA0, 0), (psA, bA1, 512), (psB, bB0, 0), (psB, bB1, 512))
        for ci, c0 in enumerate(range(0, SH, 512)):
            cw = min(512, SH - c0)
            Y = (YB0, YB1)[ci % 2]
            for m in range(DT):
                ps, pb, po = slots[m % 4]
                g = [(ps[:, po:po + cw], WBL.b((m * 8 + k) * 128, 128),
                      XB.b(k * SH + c0, cw), k == 0, k == DT - 1)
                     for k in range(DT)]
                mmg(g, [WBL.buf, XB.buf], pb)
                act(Y.f(m * 512, cw), ps[:, po:po + cw], AF.Gelu_apprx_tanh,
                    [pb], [Y.buf])
            mm(psE[:, 0:cw], ONESBF[32:33, :], SCR.b(c0, cw, 32, 33), True,
               True, [R_const.buf, SCR.buf], bE)
            for t in range(DT):
                d = XLC.f(t * 512, cw)
                tt(d, R_xres.f(t * SH + c0, cw), STAT["r"].f(c0, cw), ALU.subtract,
                   [R_xres.buf, STAT["r"].buf], [XLC.buf])
                tt(d, d, STAT["r"].f(SH + c0, cw), ALU.mult, [XLC.buf, STAT["r"].buf],
                   [XLC.buf])
                tt(SELB.f(0, cw), Y.f(t * 512, cw), d, ALU.subtract,
                   [Y.buf, XLC.buf], [SELB.buf])
                tt(SELB.f(0, cw), SELB.f(0, cw), psE[:, 0:cw], ALU.mult,
                   [SELB.buf, bE], [SELB.buf])
                tt(SELB.f(0, cw), SELB.f(0, cw), d, ALU.add,
                   [SELB.buf, XLC.buf], [SELB.buf])
                stt(R_xres.f(t * SH + c0, cw), SELB.f(0, cw), 0.5,
                    R_xres.f(t * SH + c0, cw), ALU.mult, ALU.add,
                    [SELB.buf, R_xres.buf], [R_xres.buf])
        A.free(XLC); A.free(XB); A.free(SCR)
        A.free(WBL); A.free(YB0); A.free(YB1); A.free(SELB)
        free_stat()

    def l0_attn():
        ln_stats()
        XPB = A.alloc("xpb", DT * SH // 2)
        XPE = A.alloc("xpe", DT * SH)
        PEST = A.alloc("pest", 1024 * 2)
        for t in range(DT):
            d = XPE.f(t * SH, SH)
            tt(d, R_xres.f(t * SH, SH), STAT["r"].f(0, SH), ALU.subtract,
               [R_xres.buf, STAT["r"].buf], [XPE.buf])
            tt(d, d, STAT["r"].f(SH, SH), ALU.mult, [XPE.buf, STAT["r"].buf], [XPE.buf])
            for ci, c0 in enumerate(range(0, SH, 1024)):
                cw = min(1024, SH - c0)
                pa = PEST.f((ci % 2) * 1024, cw)
                dma("sync", pa, pe_in.ap()[:, t * SH + c0:t * SH + c0 + cw],
                    writes=[PEST.buf])
                tt(XPE.f(t * SH + c0, cw), XPE.f(t * SH + c0, cw), pa, ALU.add,
                   [XPE.buf, PEST.buf], [XPE.buf])
            cp(XPB.b(t * SH, SH), d, [XPE.buf], [XPB.buf])
        dmaJ(lambda: bass.AP(sh_xpe, REGS["jSH"],
                             [[DT * S, 128], [S, DT], [1, SH]]),
             lambda: XPB.b(0, DT * SH), reads=[XPB.buf], writes=[B_sh["xpe"]])
        A.free(XPE); A.free(PEST); A.free(XPB)
        barrier()
        WQ = A.alloc("wqa", 2048); WK = A.alloc("wka", 2048); WV = A.alloc("wva", 2048)
        load_w_bf16(wq0, 4, WQ); load_w_bf16(wk0, 4, WK); load_w_bf16(wv0, 4, WV)
        QT = A.alloc("qt", 2 * S)
        KT = A.alloc("kt", 2 * S)
        VSTG = A.alloc("vstg", 512)
        PIECE = A.alloc("piece", 264 * 2)
        XST = A.alloc("xst", 2048)
        nkc = 512 // 128
        slots = ((psA, bA0, 0), (psA, bA1, 512), (psB, bB0, 0), (psB, bB1, 512))
        si = 0
        for ci, c0 in enumerate(range(0, S, 512)):
            st_ap = XST.b(0, 4096)
            dma("sync", st_ap,
                bass.AP(sh_xpe, c0, [[DT * S, 128], [S, 8], [1, 512]]),
                reads=[B_sh["xpe"]], writes=[XST.buf])
            for wi, (wreg, dreg, bc) in enumerate(((WQ, QT, 0), (WK, KT, 4),
                                                   (WV, None, 8))):
                for m in range(4):
                    ps, pb, po = slots[si % 4]; si += 1
                    g = [(ps[:, po:po + 512], wreg.b((m * 8 + k) * 128, 128),
                          XST.b(k * 512, 512), k == 0, k == DT - 1)
                         for k in range(DT)]
                    mmg(g, [wreg.buf, XST.buf], pb)
                    if dreg is not None:
                        act(dreg.b(m * S + c0, 512), ps[:, po:po + 512],
                            AF.Identity, [pb, R_vecs.buf], [dreg.buf],
                            bias=R_vecs.f(V_BQKV + bc + m, 1))
                    else:
                        act(VSTG.b(0, 512), ps[:, po:po + 512], AF.Identity,
                            [pb, R_vecs.buf], [VSTG.buf],
                            bias=R_vecs.f(V_BQKV + bc + m, 1))
                        pc = PIECE.b((ci % 2) * 528, 520)
                        mset(pc, 1.0, [], [PIECE.buf])
                        for hh in range(2):
                            for kkk in range(nkc):
                                pp, ppb = (psE, bE) if kkk % 2 == 0 else (psF, bF)
                                tr(pp[:, 0:32].bitcast(BF16),
                                   VSTG.b(kkk * 128, 128, hh * 64, hh * 64 + 64),
                                   IDBF[hh * 64:hh * 64 + 64, hh * 64:hh * 64 + 64],
                                   [VSTG.buf, R_const.buf], ppb)
                                cp(PIECE.b((ci % 2) * 528 + hh * 260 + kkk * 65,
                                           64), pp[:, 0:32].bitcast(BF16),
                                   [ppb], [PIECE.buf])
                        dma("sync",
                            bass.AP(av_aug, (2 * m) * (SK * 65) + ci * nkc * 65,
                                    [[8 * SK * 65, 128], [SK * 65, 2],
                                     [1, nkc * 65]]),
                            PIECE.b((ci % 2) * 528, 520), reads=[PIECE.buf],
                            writes=[B_avaug])
        dump("qt", QT.b(0, 4 * S), [QT.buf])
        dump("kt", KT.b(0, 4 * S), [KT.buf])
        A.free(WQ); A.free(WK); A.free(WV); A.free(XST); A.free(VSTG)
        A.free(PIECE)
        SQB = A.alloc("sqb", 1024)
        mset(VS(5, 2), 0.0, [], [R_scal.buf])
        for reg, vi in ((QT, 5), (KT, 6)):
            for c0 in range(0, S, 512):
                for pt in range(4):
                    act(SQB.b(pt * 512, 512), reg.b(pt * S + c0, 512), AF.Square,
                        [reg.buf], [SQB.buf])
                    mm(psE[:, 0:512], ONESBF, SQB.b(pt * 512, 512), pt == 0,
                       pt == 3, [R_const.buf, SQB.buf], bE)
                red(VS(13, 1), psE[:, 0:512], ALU.max, [bE], [R_scal.buf])
                tt(VS(vi, 1), VS(vi, 1), VS(13, 1), ALU.max, [R_scal.buf],
                   [R_scal.buf])
        tt(VS(4, 1), VS(5, 1), VS(6, 1), ALU.mult, [R_scal.buf], [R_scal.buf])
        act(VS(4, 1), VS(4, 1), AF.Sqrt, [R_scal.buf], [R_scal.buf])
        ts(VS(4, 1), VS(4, 1), -0.125, None, ALU.mult, ALU.mult, [R_scal.buf],
           [R_scal.buf])
        A.free(SQB)
        EB = A.alloc("eb", 3 * QC // 2)
        DEN = A.alloc("den", 2 * QC + 8)
        VA = A.alloc("va", 2 * SK * 65 + 8)
        AVN = A.alloc("avn", 2 * QC)
        va_head = SK * 65
        for qc in range(S // QC):
            for hg in range(2):
                dma("sync", VA.b(0, 4 * va_head),
                    av_aug.ap()[:, hg * 4 * va_head:(hg + 1) * 4 * va_head],
                    reads=[B_avaug], writes=[VA.buf])
                for h4 in range(4):
                    h = hg * 4 + h4
                    pt, r0 = h // 2, (h % 2) * 64

                    def score_mm(kk):
                        ps, pbs = ((psA, (bA0, bA1)) if kk % 2 == 0 else
                                   (psB, (bB0, bB1)))
                        for half in range(QC // 512):
                            mm(ps[:, half * 512:half * 512 + 512],
                               KT.b(pt * S + kk * 128, 128, r0, r0 + 64),
                               QT.b(pt * S + qc * QC + half * 512, 512,
                                    r0, r0 + 64),
                               True, True, [KT.buf, QT.buf], pbs[half])

                    score_mm(0)
                    for kk in range(SK):
                        ps, pbs = ((psA, (bA0, bA1)) if kk % 2 == 0 else
                                   (psB, (bB0, bB1)))
                        e_ap = EB.b((kk % 3) * QC, QC)
                        act(e_ap, ps[:, 0:QC], AF.Exp,
                            [pbs[0], pbs[1], R_scal.buf], [EB.buf],
                            bias=VS(4, 1), scale=0.125)
                        if kk + 1 < SK:
                            score_mm(kk + 1)
                        for half in range(QC // 512):
                            pq, pbq = (psC, bC) if half == 0 else (psD, bD)
                            mm(pq[0:65, 0:512],
                               VA.b(h4 * va_head + kk * 65, 65),
                               EB.b((kk % 3) * QC + half * 512, 512),
                               kk == 0, kk == SK - 1, [VA.buf, EB.buf], pbq)
                    for half in range(QC // 512):
                        pq, pbq = (psC, bC) if half == 0 else (psD, bD)
                        cp(DEN.f(half * 512, 512, 0, 1), pq[64:65, 0:512],
                           [pbq], [DEN.buf])
                    rcp(DEN.f(0, QC, 0, 1), DEN.f(0, QC, 0, 1), [DEN.buf],
                        [DEN.buf])
                    for half in range(QC // 512):
                        pr, pbr = (psE, bE) if half == 0 else (psF, bF)
                        mm(pr[0:64, 0:512], ONESF[0:1, 0:64],
                           DEN.f(half * 512, 512, 0, 1), True, True,
                           [R_const.buf, DEN.buf], pbr)
                        cp(DEN.f(QC + half * 512, 512, 0, 64),
                           pr[0:64, 0:512], [pbr], [DEN.buf])
                    for half in range(QC // 512):
                        pq, pbq = (psC, bC) if half == 0 else (psD, bD)
                        tt(AVN.b(pt * QC + half * 512, 512, r0, r0 + 64),
                           pq[0:64, 0:512], DEN.f(QC + half * 512, 512, 0, 64),
                           ALU.mult, [pbq, DEN.buf], [AVN.buf])
            dump("avn%d" % qc, AVN.b(0, 4 * QC), [AVN.buf])
            dmaJ(lambda qc=qc: bass.AP(sh_av, REGS["j4Sq%d" % qc],
                                       [[DT * S, 128], [S, 4], [1, QC]]),
                 lambda: AVN.b(0, 4 * QC), reads=[AVN.buf], writes=[B_sh["av"]])
        A.free(EB); A.free(DEN); A.free(QT); A.free(KT); A.free(VA); A.free(AVN)
        barrier()
        WO = A.alloc("wo", 4096)
        load_w_bf16(wo0, 8, WO)
        AVST = A.alloc("avst", 2048 * 2)
        HATT = A.alloc("hatt", DT * 512)
        XPC = A.alloc("xpc", DT * 512)
        PEST = A.alloc("pest2", 512 * 2)
        SQB = A.alloc("sqb2", 512)
        slots = ((psA, bA0, 0), (psA, bA1, 512), (psB, bB0, 0), (psB, bB1, 512))
        si = 0
        for ci, c0 in enumerate(range(0, SH, 512)):
            st_ap = AVST.b((ci % 2) * 4096, 4096)
            dmaJ(lambda st_ap=st_ap: st_ap,
                 lambda c0=c0: bass.AP(sh_av, REGS["jSHp%d" % c0],
                                       [[DT * S, 128], [S, 8], [1, 512]]),
                 reads=[B_sh["av"]], writes=[AVST.buf])
            for t in range(DT):
                d = XPC.f(t * 512, 512)
                tt(d, R_xres.f(t * SH + c0, 512), STAT["r"].f(c0, 512),
                   ALU.subtract, [R_xres.buf, STAT["r"].buf], [XPC.buf])
                tt(d, d, STAT["r"].f(SH + c0, 512), ALU.mult,
                   [XPC.buf, STAT["r"].buf], [XPC.buf])
                pa = PEST.f((t % 2) * 512, 512)
                dma("sync", pa, pe_in.ap()[:, t * SH + c0:t * SH + c0 + 512],
                    writes=[PEST.buf])
                tt(d, d, pa, ALU.add, [XPC.buf, PEST.buf], [XPC.buf])
            for m in range(DT):
                ps, pb, po = slots[si % 4]; si += 1
                g = [(ps[:, po:po + 512], WO.b((m * 8 + k) * 128, 128),
                      AVST.b((ci % 2) * 4096 + k * 512, 512), k == 0,
                      k == DT - 1) for k in range(DT)]
                mmg(g, [WO.buf, AVST.buf], pb)
                act(HATT.f(m * 512, 512), ps[:, po:po + 512], AF.Identity,
                    [pb, R_vecs.buf], [HATT.buf], bias=R_vecs.f(V_BO + m, 1))
                tt(HATT.f(m * 512, 512), HATT.f(m * 512, 512),
                   XPC.f(m * 512, 512), ALU.add, [HATT.buf, XPC.buf],
                   [HATT.buf])
            if ci == 0:
                dump("avst0", AVST.b(0, 4096), [AVST.buf])
                dump("hatt0", HATT.f(0, DT * 512), [HATT.buf])
            for t in range(DT):
                xb = SQB.b(0, 512)
                cp(xb, HATT.f(t * 512, 512), [HATT.buf], [SQB.buf])
                mm(psC[:, 0:512], ONESBF, xb, t == 0, t == DT - 1,
                   [R_const.buf, SQB.buf], bC)
                sq = SQB.b(512, 512)
                act(sq, HATT.f(t * 512, 512), AF.Square, [HATT.buf], [SQB.buf])
                mm(psD[:, 0:512], ONESBF, sq, t == 0, t == DT - 1,
                   [R_const.buf, SQB.buf], bD)
            m2 = STAT["r"].f(c0, 512); i2 = STAT["r"].f(SH + c0, 512)
            ts(m2, psC[:, 0:512], 1.0 / D, None, ALU.mult, ALU.mult, [bC],
               [STAT["r"].buf])
            tt(i2, m2, m2, ALU.mult, [STAT["r"].buf], [STAT["r"].buf])
            stt(i2, psD[:, 0:512], 1.0 / D, i2, ALU.mult, ALU.subtract,
                [bD, STAT["r"].buf], [STAT["r"].buf])
            act(i2, i2, AF.Sqrt, [STAT["r"].buf], [STAT["r"].buf], bias=1e-5)
            rcp(i2, i2, [STAT["r"].buf], [STAT["r"].buf])
            for t in range(DT):
                tt(HATT.f(t * 512, 512), HATT.f(t * 512, 512), m2, ALU.subtract,
                   [HATT.buf, STAT["r"].buf], [HATT.buf])
                stt(HATT.f(t * 512, 512), HATT.f(t * 512, 512),
                    R_vecs.f(V_G0 + t, 1), i2, ALU.mult, ALU.mult,
                    [HATT.buf, R_vecs.buf, STAT["r"].buf], [HATT.buf])
                ts(HATT.f(t * 512, 512), HATT.f(t * 512, 512),
                   R_vecs.f(V_B0 + t, 1), None, ALU.add, ALU.add,
                   [HATT.buf, R_vecs.buf], [HATT.buf])
                tt(R_xres.f(t * SH + c0, 512), R_xres.f(t * SH + c0, 512),
                   HATT.f(t * 512, 512), ALU.add, [R_xres.buf, HATT.buf],
                   [R_xres.buf])
        A.free(WO); A.free(AVST); A.free(HATT); A.free(XPC); A.free(PEST)
        A.free(SQB)
        free_stat()

    def l1_ffn(inv):
        ln_stats()
        XLC = A.alloc("xlc1", DT * 512)
        XB = A.alloc("xb1", DT * SH // 2)
        for c0 in range(0, SH, 512):
            cw = min(512, SH - c0)
            for t in range(DT):
                d = XLC.f(t * 512, cw)
                tt(d, R_xres.f(t * SH + c0, cw), STAT["r"].f(c0, cw), ALU.subtract,
                   [R_xres.buf, STAT["r"].buf], [XLC.buf])
                tt(d, d, STAT["r"].f(SH + c0, cw), ALU.mult, [XLC.buf, STAT["r"].buf],
                   [XLC.buf])
                ts(XB.b(t * SH + c0, cw), d, R_vecs.f(V_G1 + t, 1),
                   R_vecs.f(V_B1L + t, 1), ALU.mult, ALU.add,
                   [XLC.buf, R_vecs.buf], [XB.buf])
        A.free(XLC)
        free_stat()
        W1S = A.alloc("w1s", 512 * 4, high=True)
        W2S = A.alloc("w2s", 2048 * 4, high=True)
        H2 = A.alloc("h2", 32 * 256)
        M2 = A.alloc("m2", 1024)
        TMPE = A.alloc("tmpe", 512)
        SQ2 = A.alloc("sq2", 256)
        slots = ((psA, bA0, 0), (psA, bA1, 512), (psB, bB0, 0), (psB, bB1, 512))
        for ci, c0 in enumerate(range(0, SH, 512)):
            cw = min(512, SH - c0)
            si = 0
            for m in range(32):
                st_ap = W1S.b((m % 4) * 1024, 1024)
                dma("sync", st_ap,
                    bass.AP(sh_w1q, m * 128,
                            [[4096, 128], [128 * 4096, 8], [1, 128]]),
                    reads=[B_sh["w1q"]], writes=[W1S.buf])
                ps, pb, po = slots[si % 4]; si += 1
                g = [(ps[:, po:po + cw], W1S.b((m % 4) * 1024 + k * 128, 128),
                      XB.b(k * SH + c0, cw), k == 0, k == DT - 1)
                     for k in range(DT)]
                mmg(g, [W1S.buf, XB.buf], pb)
                act(H2.b(m * 512, cw), ps[:, po:po + cw], AF.Gelu,
                    [pb, R_scal.buf, R_vecs.buf], [H2.buf], bias=R_vecs.f(V_BIAS1 + m, 1),
                    scale=VS(0, 1))
            for t in range(32):
                mm(psC[:, 0:cw], ONESBF, H2.b(t * 512, cw), t == 0, t == 31,
                   [R_const.buf, H2.buf], bC)
                sq = SQ2.b(0, cw)
                act(sq, H2.b(t * 512, cw), AF.Square, [H2.buf], [SQ2.buf])
                mm(psD[:, 0:cw], ONESBF, sq, t == 0, t == 31,
                   [R_const.buf, SQ2.buf], bD)
            m2 = M2.f(0, cw); i2 = M2.f(512, cw)
            ts(m2, psC[:, 0:cw], 1.0 / 4096, None, ALU.mult, ALU.mult, [bC],
               [M2.buf])
            tt(i2, m2, m2, ALU.mult, [M2.buf], [M2.buf])
            stt(i2, psD[:, 0:cw], 1.0 / 4096, i2, ALU.mult, ALU.subtract,
                [bD, M2.buf], [M2.buf])
            act(i2, i2, AF.Sqrt, [M2.buf], [M2.buf], bias=1e-5)
            rcp(i2, i2, [M2.buf], [M2.buf])
            si = 0
            for m in range(DT):
                st_ap = W2S.b((m % 4) * 4096, 4096)
                dma("sync", st_ap,
                    bass.AP(sh_w2q, m * 128,
                            [[1024, 128], [128 * 1024, 32], [1, 128]]),
                    reads=[B_sh["w2q"]], writes=[W2S.buf])
                ps, pb, po = slots[si % 4]; si += 1
                g = [(ps[:, po:po + cw], W2S.b((m % 4) * 4096 + k * 128, 128),
                      H2.b(k * 512, cw), k == 0, k == 31)
                     for k in range(32)]
                mmg(g, [W2S.buf, H2.buf], pb)
                stt(TMPE.f(0, cw), m2, R_vecs.f(V_G + m, 1), ps[:, po:po + cw],
                    ALU.mult, ALU.subtract, [M2.buf, R_vecs.buf, pb], [TMPE.buf])
                stt(TMPE.f(0, cw), TMPE.f(0, cw), VS(14, 1), i2, ALU.mult,
                    ALU.mult, [TMPE.buf, R_scal.buf, M2.buf], [TMPE.buf])
                stt(R_xres.f(m * SH + c0, cw), TMPE.f(0, cw), 0.5,
                    R_xres.f(m * SH + c0, cw), ALU.mult, ALU.add,
                    [TMPE.buf, R_xres.buf], [R_xres.buf])
        for t in range(DT):
            ts(R_xres.f(t * SH, SH), R_xres.f(t * SH, SH),
               R_vecs.f(V_HCB + t, 1), None, ALU.add, ALU.add,
               [R_xres.buf, R_vecs.buf], [R_xres.buf])
        A.free(XB); A.free(W1S); A.free(W2S); A.free(H2); A.free(M2)
        A.free(TMPE); A.free(SQ2)

    def l1_attn():
        ln_stats()
        XLC = A.alloc("xlc2", DT * 512)
        XB = A.alloc("xb2", DT * SH // 2)
        for c0 in range(0, SH, 512):
            cw = min(512, SH - c0)
            for t in range(DT):
                d = XLC.f(t * 512, cw)
                tt(d, R_xres.f(t * SH + c0, cw), STAT["r"].f(c0, cw), ALU.subtract,
                   [R_xres.buf, STAT["r"].buf], [XLC.buf])
                tt(d, d, STAT["r"].f(SH + c0, cw), ALU.mult, [XLC.buf, STAT["r"].buf],
                   [XLC.buf])
                cp(XB.b(t * SH + c0, cw), d, [XLC.buf], [XB.buf])
        dmaJ(lambda: bass.AP(sh_xln1, REGS["jSH"],
                             [[DT * S, 128], [S, DT], [1, SH]]),
             lambda: XB.b(0, DT * SH), reads=[XB.buf], writes=[B_sh["xln1"]])
        A.free(XLC)
        free_stat()
        WGA = A.alloc("wga", 4096)
        load_w_bf16(w1at, 8, WGA)
        GOB = A.alloc("gob", DT * SH // 2)
        slots = ((psA, bA0, 0), (psA, bA1, 512), (psB, bB0, 0), (psB, bB1, 512))
        si = 0
        for ci, c0 in enumerate(range(0, SH, 512)):
            cw = min(512, SH - c0)
            for m in range(DT):
                ps, pb, po = slots[si % 4]; si += 1
                g = [(ps[:, po:po + cw], WGA.b((m * 8 + k) * 128, 128),
                      XB.b(k * SH + c0, cw), k == 0, k == DT - 1)
                     for k in range(DT)]
                mmg(g, [WGA.buf, XB.buf], pb)
                cp(GOB.b(m * SH + c0, cw), ps[:, po:po + cw], [pb], [GOB.buf])
        dmaJ(lambda: bass.AP(sh_gsh, REGS["jSH"],
                             [[DT * S, 128], [S, DT], [1, SH]]),
             lambda: GOB.b(0, DT * SH), reads=[GOB.buf], writes=[B_sh["gsh"]])
        A.free(WGA); A.free(GOB); A.free(XB)
        barrier()
        for hl in range(2):
            _l1_head(hl)
        barrier()
        WOT = A.alloc("wot", 4096)
        load_w_bf16(wot1, 8, WOT)
        OST = A.alloc("ost", 2048 * 2)
        si = 0
        for ci, c0 in enumerate(range(0, SH, 512)):
            st_ap = OST.b((ci % 2) * 4096, 4096)
            dmaJ(lambda st_ap=st_ap: st_ap,
                 lambda c0=c0: bass.AP(sh_av, REGS["jSHp%d" % c0],
                                       [[DT * S, 128], [S, 8], [1, 512]]),
                 reads=[B_sh["av"]], writes=[OST.buf])
            for m in range(DT):
                ps, pb, po = slots[si % 4]; si += 1
                g = [(ps[:, po:po + 512], WOT.b((m * 8 + k) * 128, 128),
                      OST.b((ci % 2) * 4096 + k * 512, 512), k == 0, k == DT - 1)
                     for k in range(DT)]
                mmg(g, [WOT.buf, OST.buf], pb)
                tt(R_xres.f(m * SH + c0, 512), R_xres.f(m * SH + c0, 512),
                   ps[:, po:po + 512], ALU.add, [R_xres.buf, pb], [R_xres.buf])
        A.free(WOT); A.free(OST)

    def _l1_head(hl):
        HG = None
        WST = A.alloc("wsth", 2048)
        WQH = A.alloc("wqh", 1024, high=True); WKH = A.alloc("wkh", 1024, high=True)
        WVH = A.alloc("wvh", 1024, high=True); WBH = A.alloc("wbh", 8, high=True)
        for wsrc, wreg in ((wq1, WQH), (wk1, WKH), (wv1, WVH)):
            st_ap = WST.f(0, 2048)
            dma("sync", st_ap, bass.AP(wsrc, hl * 256,
                                       [[512, 128], [128 * 512, 8], [1, 256]]),
                writes=[WST.buf])
            cp(wreg.b(0, 2048), st_ap, [WST.buf], [wreg.buf])
        st_ap = WST.f(0, 8)
        dma("sync", st_ap, bass.AP(wb1, hl, [[2, 128], [256, 8], [1, 1]]),
            writes=[WST.buf], allow_nc=True)
        cp(WBH.b(0, 8), st_ap, [WST.buf], [WBH.buf])
        A.free(WST)
        PAD = 4
        QP = A.alloc("qp", S + 2 * PAD)
        KP = A.alloc("kp", S + 2 * PAD)
        VP = A.alloc("vp", S + 2 * PAD)
        BTOK = A.alloc("btok", NBLK + 8, high=True)
        BROW = A.alloc("brow", 520)
        XST = A.alloc("xsth", 2048 * 2, high=True)
        for reg in (QP, KP, VP):
            mset(reg.b(0, 2 * PAD), 0.0, [], [reg.buf])
            mset(reg.b(S + PAD + PAD, 2 * PAD), 0.0, [], [reg.buf])
        for ci, c0 in enumerate(range(0, S, 512)):
            st_ap = XST.b((ci % 2) * 4096, 4096)
            dma("sync", st_ap,
                bass.AP(sh_xln1, c0, [[DT * S, 128], [S, 8], [1, 512]]),
                reads=[B_sh["xln1"]], writes=[XST.buf])
            for wi, (wreg, dreg) in enumerate(((WQH, QP), (WKH, KP), (WVH, VP))):
                for m in range(2):
                    ps, pb = (psC, bC) if (wi * 2 + m) % 2 == 0 else (psD, bD)
                    g = [(ps[:, 0:512], wreg.b((k * 2 + m) * 128, 128),
                          XST.b((ci % 2) * 4096 + k * 512, 512), k == 0,
                          k == DT - 1) for k in range(DT)]
                    mmg(g, [wreg.buf, XST.buf], pb)
                    cp(dreg.b(m * (S + 2 * PAD) + PAD + c0, 512), ps[:, 0:512],
                       [pb], [dreg.buf])
            g = [(psE[0:1, 0:512], WBH.b(k, 1),
                  XST.b((ci % 2) * 4096 + k * 512, 512), k == 0, k == DT - 1)
                 for k in range(DT)]
            mmg(g, [WBH.buf, XST.buf], bE)
            act(BROW.f(0, 512, 0, 1), psE[0:1, 0:512], AF.Sigmoid, [bE],
                [BROW.buf])
            for bi in range(4):
                tr(psF[0:128, bi:bi + 1], BROW.f(bi * 128, 128, 0, 1),
                   IDF[0:1, 0:1], [BROW.buf, R_const.buf], bF)
            cp(BTOK.f(ci * 4, 4), psF[0:128, 0:4], [bF], [BTOK.buf])
        A.free(XST); A.free(BROW); A.free(WBH)
        A.free(WQH); A.free(WKH); A.free(WVH)
        DIAG = A.alloc("diag", 8 * 64)
        mset(DIAG.f(0, 8 * 64), 0.0, [], [DIAG.buf])
        cvs = []
        for (pre, pnm, wcol) in ((QP, "qcv", V_CWQ), (KP, "kcv", V_CWK),
                                 (VP, "vcv", V_CWV)):
            post = A.alloc(pnm, S, high=True)
            cvs.append(post)
            for ct in range(2):
                ctg = hl * 2 + ct
                for tap in range(4):
                    tss(DIAG.b((ct * 4 + tap) * 128, 128), IDF,
                        R_vecs.f(wcol + ctg * 4 + tap, 1), ALU.mult,
                        [R_const.buf, R_vecs.buf], [DIAG.buf])
            for ci, c0 in enumerate(range(0, S, 512)):
                for ct in range(2):
                    ps, pb = (psC, bC) if ct == 0 else (psD, bD)
                    g = [(ps[:, 0:512], DIAG.b((ct * 4 + tap) * 128, 128),
                          pre.b(ct * (S + 2 * PAD) + PAD + c0 - (3 - tap), 512),
                          tap == 0, tap == 3) for tap in range(4)]
                    mmg(g, [DIAG.buf, pre.buf], pb)
                    act(post.b(ct * S + c0, 512), ps[:, 0:512], AF.Silu,
                        [pb], [post.buf])
            A.free(pre)
        QCV, KCV, VCV = cvs
        A.free(DIAG)
        RN = A.alloc("rn", 512)
        SQB = A.alloc("sqbh", 512)
        for reg in (QCV, KCV):
            for c0 in range(0, S, 512):
                for ct in range(2):
                    act(SQB.b(ct * 512, 512), reg.b(ct * S + c0, 512), AF.Square,
                        [reg.buf], [SQB.buf])
                    mm(psE[:, 0:512], ONESBF, SQB.b(ct * 512, 512), ct == 0,
                       ct == 1, [R_const.buf, SQB.buf], bE)
                rnc = RN.f(0, 512)
                act(rnc, psE[:, 0:512], AF.Sqrt, [bE], [RN.buf], bias=1e-6)
                rcp(rnc, rnc, [RN.buf], [RN.buf])
                for ct in range(2):
                    tt(reg.b(ct * S + c0, 512), reg.b(ct * S + c0, 512), rnc,
                       ALU.mult, [reg.buf, RN.buf], [reg.buf])
        A.free(RN); A.free(SQB)
        SRC4 = A.alloc("src4", NBLK * 512)
        mset(SRC4.f(0, NBLK * 512), -777.0, [], [SRC4.buf])
        for blk in range(NBLK):
            for ct in range(2):
                ps, pb = (psE, bE) if ct == 0 else (psF, bF)
                tr(ps[:, 0:64].bitcast(BF16), VCV.b(ct * S + blk * 128, 128),
                   IDBF, [VCV.buf, R_const.buf], pb)
                cp(SRC4.b(blk * 1024 + 256 + ct * 128, 128),
                   ps[:, 0:64].bitcast(BF16), [pb], [SRC4.buf])
        dump("qcv%d" % hl, QCV.b(0, 2 * S), [QCV.buf])
        dump("kcv%d" % hl, KCV.b(0, 2 * S), [KCV.buf])
        dump("vcv%d" % hl, VCV.b(0, 2 * S), [VCV.buf])
        dump("btok%d" % hl, BTOK.f(0, NBLK), [BTOK.buf])
        A.free(VCV)
        _delta(QCV, KCV, BTOK, SRC4)
        A.free(QCV); A.free(KCV)
        _fir(hl, SRC4)
        for sl, snm in ((0, "od"), (256, "vd"), (512, "fs"), (768, "fl")):
            dump("%s%d" % (snm, hl), SRC4.bap([(1024, NBLK), (1, 256)], sl),
                 [SRC4.buf])
        STATSF = A.alloc("statsf", S // 2 + 8)
        _stats(SRC4, STATSF)
        dump("stf%d" % hl, STATSF.b(0, S, 0, 16), [STATSF.buf])
        _gate(hl, SRC4, STATSF)
        A.free(STATSF)
        dump("mix%d" % hl, SRC4.b(0, NBLK * 1024), [SRC4.buf])
        OT = A.alloc("ot", S)
        for blk in range(NBLK):
            for ct in range(2):
                ps, pb = (psE, bE) if ct == 0 else (psF, bF)
                tr(ps[:, 0:64].bitcast(BF16), SRC4.b(blk * 1024 + ct * 128, 128),
                   IDBF, [SRC4.buf, R_const.buf], pb)
                cp(OT.b(ct * S + blk * 128, 128), ps[:, 0:64].bitcast(BF16),
                   [pb], [OT.buf])
        dmaJ(lambda hl=hl: bass.AP(sh_av,
                                   REGS["j4S" if hl == 0 else "j4So2"],
                                   [[DT * S, 128], [S, 2], [1, S]]),
             lambda: OT.b(0, 2 * S), reads=[OT.buf], writes=[B_sh["av"]])
        A.free(OT); A.free(SRC4); A.free(BTOK)

    def _delta(QN, KN, BTOK, SRC4):
        NCH = S // 128
        POW = A.alloc("pow", 7 * 64)
        APW = A.alloc("apow", 7 * 64)
        TA = A.alloc("ta", 256)
        TB = A.alloc("tb", 128)
        KTOK = A.alloc("ktok", 128)
        WT = A.alloc("wtd", 128)
        SSB = A.alloc("ssb", 256)
        SSF = A.alloc("ssf", 512)
        UB = A.alloc("ub", 128)
        ATT = A.alloc("attd", 64)
        BR = A.alloc("brd", 132)
        for i in range(NCH):
            first, last = (i == 0), (i == NCH - 1)
            tr(psF[0:1, 0:128], BTOK.f(i, 1), IDF[0:128, 0:128], [BTOK.buf,
                                                                  R_const.buf], bF)
            cp(BR.f(0, 128, 0, 1), psF[0:1, 0:128], [bF], [BR.buf])
            mm(psF[:, 128:256], ONESF[0:1, :], BR.f(0, 128, 0, 1), True, True,
               [R_const.buf, BR.buf], bF)
            g = [(psC[:, 0:128], KN.b(ct * S + i * 128, 128),
                  KN.b(ct * S + i * 128, 128), ct == 0, ct == 1)
                 for ct in range(2)]
            mmg(g, [KN.buf], bC)
            a0 = APW.b(0, 128)
            stt(a0, psC[:, 0:128], BTOK.f(i, 1), TRILS, ALU.mult, ALU.mult,
                [bC, BTOK.buf, R_const.buf], [APW.buf])
            ts(a0, a0, -1.0, None, ALU.mult, ALU.mult, [APW.buf], [APW.buf])
            BM = TB.f(0, 128)
            tt(BM, TRIUS, psF[:, 128:256], ALU.mult, [R_const.buf, bF], [TB.buf])
            n0 = POW.b(0, 128)
            stt(n0, psC[:, 0:128], -1.0, BM, ALU.mult, ALU.mult, [bC, TB.buf],
                [POW.buf])
            for jj in range(1, 7):
                mm(psD[:, 0:128], APW.b((jj - 1) * 128, 128),
                   POW.b((jj - 1) * 128, 128), True, True, [APW.buf, POW.buf], bD)
                cp(POW.b(jj * 128, 128), psD[:, 0:128], [bD], [POW.buf])
                mm(psD[:, 128:256], POW.b((jj - 1) * 128, 128),
                   APW.b((jj - 1) * 128, 128), True, True, [APW.buf, POW.buf], bD)
                cp(APW.b(jj * 128, 128), psD[:, 128:256], [bD], [APW.buf])
            for ct in range(2):
                ps, pb = (psE, bE) if ct == 0 else (psF, bF)
                tr(ps[:, 0:64].bitcast(BF16), KN.b(ct * S + i * 128, 128),
                   IDBF, [KN.buf, R_const.buf], pb)
                cp(KTOK.b(ct * 128, 128), ps[:, 0:64].bitcast(BF16), [pb],
                   [KTOK.buf])
            for ct in range(2):
                ts(TA.b(256 + ct * 128, 128), KTOK.b(ct * 128, 128),
                   BTOK.f(i, 1), None, ALU.mult, ALU.mult,
                   [KTOK.buf, BTOK.buf], [TA.buf])
                ts(TA.b(ct * 128, 128), SRC4.b(i * 1024 + 256 + ct * 128, 128),
                   BTOK.f(i, 1), None, ALU.mult, ALU.mult,
                   [SRC4.buf, BTOK.buf], [TA.buf])
            for jj in range(6, -1, -1):
                mm(psB[:, 0:512], POW.b(jj * 128, 128), TA.b(0, 512), True, True,
                   [POW.buf, TA.buf], bB0)
                tt(TA.b(0, 512), TA.b(0, 512), psB[:, 0:512], ALU.add,
                   [TA.buf, bB0], [TA.buf])
            for ct in range(2):
                ps, pb = (psE, bE) if ct == 0 else (psF, bF)
                tr(ps[:, 0:64].bitcast(BF16), TA.b(256 + ct * 128, 128),
                   IDBF, [TA.buf, R_const.buf], pb)
                cp(WT.b(ct * 128, 128), ps[:, 0:64].bitcast(BF16), [pb],
                   [WT.buf])
            if first:
                cp(UB.b(0, 256), TA.b(0, 256), [TA.buf], [UB.buf])
            else:
                g = [(psB[:, 512:768], WT.b(ct * 128, 128), SSB.b(ct * 256, 256),
                      ct == 0, ct == 1) for ct in range(2)]
                mmg(g, [WT.buf, SSB.buf], bB1)
                tt(UB.b(0, 256), TA.b(0, 256), psB[:, 512:768], ALU.subtract,
                   [TA.buf, bB1], [UB.buf])
            g = [(psC[:, 128:256], KN.b(ct * S + i * 128, 128),
                  QN.b(ct * S + i * 128, 128), ct == 0, ct == 1)
                 for ct in range(2)]
            mmg(g, [KN.buf, QN.buf], bC)
            tt(ATT.b(0, 128), TRIUI, psC[:, 128:256], ALU.mult,
               [R_const.buf, bC], [ATT.buf])
            if not first:
                g = [(psD[:, 0:256], QN.b(ct * S + i * 128, 128),
                      SSB.b(ct * 256, 256), ct == 0, False) for ct in range(2)]
                g.append((psD[:, 0:256], ATT.b(0, 128), UB.b(0, 256), False, True))
                mmg(g, [QN.buf, SSB.buf, ATT.buf, UB.buf], bD)
            else:
                mmg([(psD[:, 0:256], ATT.b(0, 128), UB.b(0, 256), True, True)],
                    [ATT.buf, UB.buf], bD)
            cp(SRC4.b(i * 1024, 256), psD[:, 0:256], [bD], [SRC4.buf])
            if not last:
                mm(psA[:, 0:256], KTOK.b(0, 128), UB.b(0, 256), True, True,
                   [KTOK.buf, UB.buf], bA0)
                mm(psA[:, 512:768], KTOK.b(128, 128), UB.b(0, 256), True, True,
                   [KTOK.buf, UB.buf], bA1)
                if first:
                    cp(SSF.f(0, 256), psA[:, 0:256], [bA0], [SSF.buf])
                    cp(SSF.f(256, 256), psA[:, 512:768], [bA1], [SSF.buf])
                else:
                    tt(SSF.f(0, 256), SSF.f(0, 256), psA[:, 0:256], ALU.add,
                       [SSF.buf, bA0], [SSF.buf])
                    tt(SSF.f(256, 256), SSF.f(256, 256), psA[:, 512:768],
                       ALU.add, [SSF.buf, bA1], [SSF.buf])
                cp(SSB.b(0, 256), SSF.f(0, 256), [SSF.buf], [SSB.buf])
                cp(SSB.b(256, 256), SSF.f(256, 256), [SSF.buf], [SSB.buf])
        for r in (POW, APW, TA, TB, KTOK, WT, SSB, SSF, UB, ATT, BR):
            A.free(r)

    def _fir(hl, SRC4):
        TOEP = A.alloc("toep", 64 * 64)
        RHW = A.alloc("rhw", 32 * NW + 8)
        RHW2 = A.alloc("rhw2", 32 * NW + 8)
        for g4 in range(4):
            ch0 = hl * 256 + g4 * 64
            dma("sync", TOEP.bap([(128, 64), (1, 64)], 0),
                bass.AP(tp_l, ch0 * 191, [[1, 128], [191, 64], [1, 64]]),
                reads=[B_sh["tpl"]], writes=[TOEP.buf])
            dma("sync", TOEP.bap([(128, 64), (1, 64)], 64),
                bass.AP(tp_s, ch0 * 191, [[1, 128], [191, 64], [1, 64]]),
                reads=[B_sh["tps"]], writes=[TOEP.buf])
            voff = 256 + g4 * 64
            P.op("vector", lambda e, voff=voff: e.tensor_copy(
                out=RHW.bap([(NW, 64), (2, NW // 2)], 1),
                in_=SRC4.bap([(1, 64), (1024, NW // 2)], voff)),
                reads=[SRC4.buf], writes=[RHW.buf])
            P.op("vector", lambda e, voff=voff: e.tensor_copy(
                out=RHW.bap([(NW, 64), (2, NW // 2)], 0, 64, 128),
                in_=SRC4.bap([(1, 64), (1024, NW // 2)], voff, 0, 64)),
                reads=[SRC4.buf], writes=[RHW.buf])
            P.op("vector", lambda e, voff=voff: e.tensor_copy(
                out=RHW.bap([(NW, 64), (2, NW // 2 - 1)], 2, 0, 64),
                in_=SRC4.bap([(1, 64), (1024, NW // 2 - 1)], voff, 64, 128)),
                reads=[SRC4.buf], writes=[RHW.buf])
            mset(RHW.bap([(NW, 64)], 0, 0, 64), 0.0, [], [RHW.buf])
            for b8 in range(64 * NW // 512):
                pr, pbr = (psE, bE) if b8 % 2 == 0 else (psF, bF)
                mm(pr[:, 0:512], REVBF, RHW.b(b8 * 512, 512), True, True,
                   [R_const.buf, RHW.buf], pbr)
                cp(RHW2.b(b8 * 512, 512), pr[:, 0:512], [pbr], [RHW2.buf])
            if g4 == 0:
                dump("rhw%d" % hl, RHW.b(0, 64 * NW), [RHW.buf])
                dump("toep%d" % hl, TOEP.b(0, 64 * 128), [TOEP.buf])
            for cg in range(8):
                ps, pb = (psC, bC) if cg % 2 == 0 else (psD, bD)
                grp = []
                for c8 in range(8):
                    c = cg * 8 + c8
                    grp.append((ps[:, c8 * NW:c8 * NW + NW],
                                TOEP.b(c * 128, 128),
                                RHW2.b(c * NW, NW), True, True))
                mmg(grp, [TOEP.buf, RHW2.buf], pb)
                for (rows0, slot) in ((0, 768), (64, 512)):
                    for par, po in ((0, 0), (64, 1)):
                        eng = "vector" if par == 0 else "vector"
                        P.op(eng, lambda e, ps=ps, rows0=rows0, slot=slot,
                             par=par, po=po, cg=cg, g4=g4: e.tensor_copy(
                                 out=SRC4.bap([(1, 8), (1024, NW // 2)],
                                              slot + g4 * 64 + cg * 8, par,
                                              par + 64),
                                 in_=sub_ap(ps[rows0:rows0 + 64, 0:512], po,
                                            [(NW, 8), (2, NW // 2)])),
                             reads=[pb], writes=[SRC4.buf])
        A.free(TOEP); A.free(RHW); A.free(RHW2)

    def _stats(SRC4, STATSF):
        RAW = A.alloc("raw", 3 * 4 * NBLK + 16)
        SQS = A.alloc("sqs", 2048)
        for kind in range(2):
            red(RAW.f(kind * 4 * NBLK, 4 * NBLK),
                SRC4.bap([(256, 4 * NBLK), (1, 256)], 0), ALU.add,
                [SRC4.buf], [RAW.buf], absval=(kind == 1))
        for g8 in range(NBLK // 4):
            act(SQS.b(0, 4096), SRC4.b(g8 * 4096, 4096), AF.Square, [SRC4.buf],
                [SQS.buf])
            red(RAW.f(8 * NBLK + g8 * 16, 16), SQS.bap([(256, 16), (1, 256)], 0),
                ALU.add, [SQS.buf], [RAW.buf])
        A.free(SQS)
        DS = A.alloc("ds", NBLK * 16)
        perm = {0: 8, 1: 12, 2: 0, 3: 4}
        for sidx in range(4):
            rb = perm[sidx]
            sm = RAW.fap([(4, NBLK)], sidx)
            ab = RAW.fap([(4, NBLK)], 4 * NBLK + sidx)
            qq = RAW.fap([(4, NBLK)], 8 * NBLK + sidx)
            mean = DS.fap([(16, NBLK)], rb + 0)
            var = DS.fap([(16, NBLK)], rb + 1)
            am = DS.fap([(16, NBLK)], rb + 2)
            l2 = DS.fap([(16, NBLK)], rb + 3)
            ts(mean, sm, 1.0 / 256, None, ALU.mult, ALU.mult, [RAW.buf], [DS.buf])
            ts(am, ab, 1.0 / 256, None, ALU.mult, ALU.mult, [RAW.buf], [DS.buf])
            P.op("scalar", lambda e, o=l2, i=qq: e.activation(
                out=o, in_=i, func=AF.Sqrt), reads=[RAW.buf], writes=[DS.buf])
            tt(var, mean, mean, ALU.mult, [DS.buf], [DS.buf])
            stt(var, qq, 1.0 / 256, var, ALU.mult, ALU.subtract,
                [RAW.buf, DS.buf], [DS.buf])
        for blk in range(NBLK):
            ps, pb = (psE, bE) if blk % 2 == 0 else (psF, bF)
            tr(ps[0:16, 0:128], DS.f(blk * 16, 16), IDF, [DS.buf, R_const.buf],
               pb)
            cp(STATSF.b(blk * 128, 128, 0, 16), ps[0:16, 0:128], [pb],
               [STATSF.buf])
        A.free(RAW); A.free(DS)

    def _gate(hl, SRC4, STATSF):
        W1B = A.alloc("w1b", 520)
        W2G = A.alloc("w2g", 20)
        WST = A.alloc("wstg2", 1024)
        dma("sync", WST.f(0, 1024, 0, 16),
            bass.AP(w1bt, 0, [[1024, 16], [1, 1024]]), writes=[WST.buf])
        cp(W1B.b(0, 1024, 0, 16), WST.f(0, 1024, 0, 16), [WST.buf], [W1B.buf])
        dma("sync", WST.f(0, 32), bass.AP(w2gt, 0, [[4, 128], [512, 8], [1, 4]]),
            writes=[WST.buf])
        cp(W2G.b(0, 32), WST.f(0, 32), [WST.buf], [W2G.buf])
        A.free(WST)
        act(VS(16, 1, 0, 4), R_vecs.f(V_GMISC + 3 + hl, 1, 0, 4), AF.Exp,
            [R_scal.buf, R_vecs.buf], [R_scal.buf, R_vecs.buf], scale=-1.0)
        ts(VS(17, 1, 0, 4), R_vecs.f(V_GMISC + 1 + hl, 1, 0, 4), DECAY, None,
           ALU.mult, ALU.mult, [R_scal.buf, R_vecs.buf], [R_scal.buf, R_vecs.buf])
        tt(VS(17, 1, 0, 4), VS(17, 1, 0, 4), R_vecs.f(V_GMISC, 1, 0, 4),
           ALU.add, [R_scal.buf, R_vecs.buf], [R_scal.buf, R_vecs.buf])
        tt(VS(18, 1, 0, 4), VS(17, 1, 0, 4), VS(16, 1, 0, 4), ALU.mult,
           [R_scal.buf], [R_scal.buf])
        GST = A.alloc("gst", 2048 * 2)
        H1 = A.alloc("h1", 2048)
        LG = A.alloc("lg", 1040)
        WGT = A.alloc("wgt", NBLK * 4 + 8)
        slots = ((psA, bA0, 0), (psA, bA1, 512), (psB, bB0, 0), (psB, bB1, 512))
        si = 0
        for ci, c0 in enumerate(range(0, S, 512)):
            st_ap = GST.b((ci % 2) * 4096, 4096)
            dma("sync", st_ap,
                bass.AP(sh_gsh, c0, [[DT * S, 128], [S, 8], [1, 512]]),
                reads=[B_sh["gsh"]], writes=[GST.buf])
            for m in range(DT):
                ps, pb, po = slots[si % 4]; si += 1
                g = [(ps[:, po:po + 512], W1B.b(m * 128, 128, 0, 16),
                      STATSF.b(c0, 512, 0, 16), True, False),
                     (ps[:, po:po + 512], IDBF,
                      GST.b((ci % 2) * 4096 + m * 512, 512), False, True)]
                mmg(g, [W1B.buf, STATSF.buf, GST.buf, R_const.buf], pb)
                act(H1.b(m * 512, 512), ps[:, po:po + 512], AF.Gelu_apprx_tanh,
                    [pb, R_vecs.buf], [H1.buf], bias=R_vecs.f(V_GB1 + m, 1))
            g = [(psE[0:4, 0:512], W2G.b(k * 4, 4), H1.b(k * 512, 512),
                  k == 0, k == DT - 1) for k in range(DT)]
            mmg(g, [W2G.buf, H1.buf], bE)
            act(LG.f(0, 512, 0, 4), psE[0:4, 0:512], AF.Identity,
                [bE, R_scal.buf], [LG.buf], bias=VS(18, 1, 0, 4),
                scale=VS(16, 1, 0, 4))
            dump("lgz%d_%d" % (hl, ci), LG.f(0, 512, 0, 4), [LG.buf])
            dump("h1_%d_%d" % (hl, ci), H1.b(0, 4096), [H1.buf])
            red(LG.f(512, 1, 0, 4), LG.f(0, 512, 0, 4), ALU.max, [LG.buf],
                [LG.buf])
            tr(psF[0:1, 0:4], LG.f(512, 1, 0, 4), IDF[0:4, 0:4],
               [LG.buf, R_const.buf], bF)
            cp(LG.f(513, 4, 0, 1), psF[0:1, 0:4], [bF], [LG.buf])
            red(LG.f(517, 1, 0, 1), LG.f(513, 4, 0, 1), ALU.max, [LG.buf],
                [LG.buf])
            mm(psF[0:4, 4:5], ONESF[0:1, 0:4], LG.f(517, 1, 0, 1), True, True,
               [R_const.buf, LG.buf], bF)
            ts(LG.f(518, 1, 0, 4), psF[0:4, 4:5], -1.0, None, ALU.mult, ALU.mult,
               [bF], [LG.buf])
            act(LG.f(0, 512, 0, 4), LG.f(0, 512, 0, 4), AF.Exp, [LG.buf],
                [LG.buf], bias=LG.f(518, 1, 0, 4))
            mm(psF[:, 0:512], ONESF[0:4, :], LG.f(0, 512, 0, 4), True, True,
               [R_const.buf, LG.buf], bF)
            rcp(LG.f(520, 512, 0, 4), psF[0:4, 0:512], [bF], [LG.buf])
            tt(LG.f(0, 512, 0, 4), LG.f(0, 512, 0, 4), LG.f(520, 512, 0, 4),
               ALU.mult, [LG.buf], [LG.buf])
            ts(LG.f(0, 512, 0, 4), LG.f(0, 512, 0, 4), 1.0 - 4.0 * EPSF, EPSF,
               ALU.mult, ALU.add, [LG.buf], [LG.buf])
            for bi in range(4):
                tr(psE[0:128, 0:4], LG.f(bi * 128, 128, 0, 4), IDF[0:4, 0:4],
                   [LG.buf, R_const.buf], bE)
                cp(WGT.f((ci * 4 + bi) * 4, 4), psE[0:128, 0:4], [bE], [WGT.buf])
        A.free(GST); A.free(H1); A.free(LG); A.free(W1B); A.free(W2G)
        RS4 = A.alloc("rs4", 264)
        for blk in range(NBLK):
            w0 = WGT.f(blk * 4 + 0, 1); w1 = WGT.f(blk * 4 + 1, 1)
            w2 = WGT.f(blk * 4 + 2, 1); w3 = WGT.f(blk * 4 + 3, 1)
            od = SRC4.b(blk * 1024, 256); vd = SRC4.b(blk * 1024 + 256, 256)
            fs = SRC4.b(blk * 1024 + 512, 256); fl = SRC4.b(blk * 1024 + 768, 256)
            acc = RS4.b(0, 256)
            ts(acc, fs, w0, None, ALU.mult, ALU.mult, [SRC4.buf, WGT.buf],
               [RS4.buf])
            stt(acc, fl, w1, acc, ALU.mult, ALU.add, [SRC4.buf, WGT.buf,
                                                      RS4.buf], [RS4.buf])
            stt(acc, od, w2, acc, ALU.mult, ALU.add, [SRC4.buf, WGT.buf,
                                                      RS4.buf], [RS4.buf])
            stt(od, vd, w3, acc, ALU.mult, ALU.add, [SRC4.buf, WGT.buf,
                                                     RS4.buf], [SRC4.buf])
            act(RS4.b(256, 256), od, AF.Square, [SRC4.buf], [RS4.buf],
                accum=RS4.f(256, 1))
            ts(RS4.f(257, 1), RS4.f(256, 1), 1.0 / 256, None, ALU.mult, ALU.mult,
               [RS4.buf], [RS4.buf])
            act(RS4.f(257, 1), RS4.f(257, 1), AF.Sqrt, [RS4.buf], [RS4.buf],
                bias=1e-5)
            rcp(RS4.f(257, 1), RS4.f(257, 1), [RS4.buf], [RS4.buf])
            stt(od, od, RS4.f(257, 1), R_vecs.f(V_ONW, 256), ALU.mult, ALU.mult,
                [SRC4.buf, RS4.buf, R_vecs.buf], [SRC4.buf])
        A.free(RS4); A.free(WGT)

    if "l0f1" in stages:
        l0_ffn(0)
    quantize()
    if "l0a" in stages:
        l0_attn()
    if "l0f2" in stages:
        l0_ffn(1)
    if "l1f1" in stages:
        l1_ffn(0)
    if "l1a" in stages:
        l1_attn()
    if "l1f2" in stages:
        l1_ffn(1)
    dma("gpsimd", out_t.ap(), R_xres.f(0, DT * SH), reads=[R_xres.buf],
        writes=[])

    import contextlib
    with contextlib.ExitStack() as st:
        sems, qsems = {}, {}
        for e in ENGS:
            sems[("e", e)] = st.enter_context(nc.semaphore("sem_" + e))
            qsems[e] = [st.enter_context(nc.semaphore("q_%s_%d" % (e, s)))
                        for s in range(Prog.NQ)] if P.dma_cnt[e] else []
        block = st.enter_context(nc.Block())
        P.emit_all(sems, qsems, block, gps_prolog)
    return nc


def _fm(a):
    n = a.shape[0]
    return np.ascontiguousarray(
        a.T.reshape(DT, 128, n).transpose(1, 0, 2).reshape(128, DT * n))


def _pp(v, nt):
    return np.ascontiguousarray(v.reshape(nt, 128).T)


def make_inputs(inputs, S=4096, ncores=8):
    f32 = np.float32
    SH = S // 2
    g = {k: np.asarray(v) for k, v in inputs.items()}
    cid = np.eye(128)
    consts = {
        "fconst": np.tile(np.array([[1e-5, 1e-6, 0.125, -1.0]], f32), (128, 1)),
        "c_ones_bf": np.ones((128, 128), ml_dtypes.bfloat16),
        "c_id_bf": cid.astype(ml_dtypes.bfloat16),
        "c_rev_bf": cid[::-1].astype(ml_dtypes.bfloat16),
        "c_id_f": cid.astype(f32),
        "c_ones_f": np.ones((128, 128), f32),
        "c_trils": np.tril(np.ones((128, 128), f32), -1),
        "c_trili": np.tril(np.ones((128, 128), f32)),
        "c_trius": np.triu(np.ones((128, 128), f32), 1),
        "c_triui": np.triu(np.ones((128, 128), f32)),
    }
    fl = np.asarray(g["L1_attn_fir_long_filt"], f32).reshape(1024, 64)
    fs = np.asarray(g["L1_attn_fir_short_filt"], f32).reshape(1024, 5)
    w1T = np.ascontiguousarray(np.asarray(g["L1_ffn_w1"], f32).T)
    w2T = np.ascontiguousarray(np.asarray(g["L1_ffn_w2"], f32).T)
    wq1T = np.ascontiguousarray(np.asarray(g["L1_attn_Wq"], f32).T)
    wk1T = np.ascontiguousarray(np.asarray(g["L1_attn_Wk"], f32).T)
    wv1T = np.ascontiguousarray(np.asarray(g["L1_attn_Wv"], f32).T)
    wb1T = np.ascontiguousarray(np.asarray(g["L1_attn_Wb"], f32).T)
    w1aT = np.ascontiguousarray(np.asarray(g["L1_attn_gate_W1"], f32)[:, :1024].T)
    w1bT = np.ascontiguousarray(np.asarray(g["L1_attn_gate_W1"], f32)[:, 1024:].T)
    w2gT = np.ascontiguousarray(np.asarray(g["L1_attn_gate_W2"], f32).T)
    wo1T = np.ascontiguousarray(np.asarray(g["L1_attn_Wo"], f32).T)
    in_maps = []
    for c in range(ncores):
        b, j = c // 2, c % 2
        sl = slice(j * SH, (j + 1) * SH)
        hc0 = j * 512
        m = dict(consts)
        m["x_in"] = _fm(np.asarray(g["L0_attn_x"], f32)[b, sl])
        m["pe_in"] = _fm(np.asarray(g["L0_attn_pe"], f32)[0, sl])
        m["wq0"] = np.ascontiguousarray(np.asarray(g["L0_attn_wq"], f32)[:, hc0:hc0 + 512])
        m["wk0"] = np.ascontiguousarray(np.asarray(g["L0_attn_wk"], f32)[:, hc0:hc0 + 512])
        m["wv0"] = np.ascontiguousarray(np.asarray(g["L0_attn_wv"], f32)[:, hc0:hc0 + 512])
        m["wo0"] = np.asarray(g["L0_attn_wo"], f32)
        m["bqkv0"] = np.concatenate(
            [_pp(np.asarray(g["L0_attn_b" + t], f32)[hc0:hc0 + 512], 4)
             for t in ("q", "k", "v")], axis=1)
        m["bo0"] = _pp(np.asarray(g["L0_attn_bo"], f32), 8)
        m["gam0"] = _pp(np.asarray(g["L0_attn_gamma"], f32), 8)
        m["bet0"] = _pp(np.asarray(g["L0_attn_beta"], f32), 8)
        m["wrt0"] = _pp(np.asarray(g["L0_ffn_w_router"], f32), 8)
        m["wbl0"] = np.asarray(g["L0_ffn_w_block"], f32)
        m["w1t_h"] = np.ascontiguousarray(w1T[:, j * 2048:(j + 1) * 2048])
        m["w2t_h"] = np.ascontiguousarray(w2T[:, j * 512:(j + 1) * 512])
        m["g1v"] = _pp(np.asarray(g["L1_ffn_ln1_g"], f32), 8)
        m["b1lv"] = _pp(np.asarray(g["L1_ffn_ln1_b"], f32), 8)
        m["g2v"] = _pp(np.asarray(g["L1_ffn_ln2_g"], f32), 32)
        m["b2lv"] = _pp(np.asarray(g["L1_ffn_ln2_b"], f32), 32)
        m["bias1"] = _pp(np.asarray(g["L1_ffn_b1"], f32), 32)
        m["bias2"] = _pp(np.asarray(g["L1_ffn_b2"], f32), 8)
        m["wq1"] = np.ascontiguousarray(wq1T[:, j * 512:(j + 1) * 512])
        m["wk1"] = np.ascontiguousarray(wk1T[:, j * 512:(j + 1) * 512])
        m["wv1"] = np.ascontiguousarray(wv1T[:, j * 512:(j + 1) * 512])
        m["wb1"] = np.ascontiguousarray(wb1T[:, 2 * j:2 * j + 2])
        for nm, cw in (("cwq", "L1_attn_conv_q_w"), ("cwk", "L1_attn_conv_k_w"),
                       ("cwv", "L1_attn_conv_v_w")):
            w = np.asarray(g[cw], f32)[j * 512:(j + 1) * 512]
            m[nm] = np.ascontiguousarray(
                w.reshape(4, 128, 4).transpose(1, 0, 2).reshape(128, 16))
        for nm, filt, K in (("fpad_l", fl, 64), ("fpad_s", fs, 5)):
            p = np.zeros((512, 191), f32)
            p[:, 63:63 + K] = filt[j * 512:(j + 1) * 512, ::-1]
            m[nm] = p
        m["w1at"] = w1aT
        m["w1bt"] = w1bT
        m["gb1"] = _pp(np.asarray(g["L1_attn_gate_b1"], f32), 8)
        m["w2gt"] = w2gT
        gm = np.zeros((4, 5), f32)
        gm[:, 0] = np.asarray(g["L1_attn_gate_b2"], f32)
        gm[3, 1] = np.asarray(g["L1_attn_gate_copy_bias"], f32)[2 * j]
        gm[3, 2] = np.asarray(g["L1_attn_gate_copy_bias"], f32)[2 * j + 1]
        gm[:, 3] = np.asarray(g["L1_attn_gate_log_temp"], f32)[2 * j]
        gm[:, 4] = np.asarray(g["L1_attn_gate_log_temp"], f32)[2 * j + 1]
        m["gmisc"] = gm
        m["onw2"] = np.tile(np.asarray(g["L1_attn_o_norm_w"], f32), 2)[None, :]
        m["wot1"] = wo1T
        in_maps.append(m)
    return in_maps


def kernel(**inputs):
    S = 4096
    SH = S // 2
    nc = build(S=S)
    in_maps = make_inputs(inputs, S=S)
    res = run_bass_kernel_spmd(nc, in_maps, core_ids=list(range(8)))
    out = np.zeros((4, S, D), np.float32)
    for c in range(8):
        b, j = c // 2, c % 2
        o = np.asarray(res.results[c]["out"])
        o = o.reshape(128, DT, SH).transpose(1, 0, 2).reshape(D, SH)
        out[b, j * SH:(j + 1) * SH, :] = o.T
    return out
